# version 11
# baseline (speedup 1.0000x reference)
import contextlib
import math
import numpy as np
import ml_dtypes
import concourse.bass as bass
import concourse.mybir as mybir
from concourse.bass_utils import run_bass_kernel_spmd

F32 = mybir.dt.float32
BF16 = mybir.dt.bfloat16
AF = mybir.ActivationFunctionType
ALU = mybir.AluOpType
BF = ml_dtypes.bfloat16

S = 4096
D = 1024
H = 16
HD = 64
NCHUNK = 8
CH = 512
NFT = 8
NTT = 32
SHIFT = 12.0
ALIBI_M = 20.0
LN_EPS = 1e-5
ATTNB_EPS = 1e-6
DEBUG_TAPS = False
N_SL = 4

SLOPES = [2.0 ** (-(i + 1) / 2.0) for i in range(H)]
CUTOFF = [ALIBI_M / s for s in SLOPES]


def kept_ks_range(h, qi):
    cap = int(math.floor((ALIBI_M + 20.0) / SLOPES[h] / 128.0))
    ks_hi = min(4 * qi + 3, cap, NTT - 1)
    return 0, ks_hi


def exact_split3(v):
    hi = 4.0 * np.floor(v / 4.0)
    rem = v - hi
    mid = 0.125 * np.floor(rem / 0.125)
    lo = rem - mid
    return hi, mid, lo


class Builder:
    ENGS = ("sync", "tensor", "vector", "scalar", "gpsimd")

    def __init__(self, nc):
        self.nc = nc
        self.bodies = {e: [] for e in self.ENGS}
        self.sems = {}
        self.cnt = {}
        self.sdone = {}

    def sem(self, stack, name):
        self.sems[name] = stack.enter_context(self.nc.semaphore(name))
        self.cnt[name] = 0

    def emit(self, eng, fn):
        self.bodies[eng].append(fn)

    def wait(self, eng, semname, val=None):
        v = self.cnt[semname] if val is None else val
        if v <= 0:
            return
        s = self.sems[semname]
        self.emit(eng, lambda e, s=s, v=v: e.wait_ge(s, v))

    def inc(self, semname, n=1):
        self.cnt[semname] += n
        return self.sems[semname]

    def op(self, eng, fn, inc=None, incn=1):
        if inc is not None:
            s = self.inc(inc, incn)
            self.emit(eng, lambda e, fn=fn, s=s, n=incn: fn(e).then_inc(s, n))
        else:
            self.emit(eng, fn)

    def dma(self, eng, out, in_, inc):
        s = self.inc(inc, 16)
        self.emit(eng, lambda e, o=out, i=in_, s=s: e.dma_start(out=o, in_=i).then_inc(s, 16))

    def dma_count(self, n):
        return 16 * n

    def store(self, eng, out, in_, key=None):
        self.dma(eng, out, in_, "stx")
        if key is not None:
            self.sdone[key] = self.cnt["stx"]

    def wait_store(self, eng, key):
        if key in self.sdone:
            self.wait(eng, "stx", self.sdone[key])

    def junction(self):
        for eng in self.ENGS:
            for sname in ("pe", "ve", "ac", "po", "stx"):
                self.wait(eng, sname)


def build_program():
    nc = bass.Bass("TRN2", target_bir_lowering=False)

    d_x = nc.dram_tensor("x", [S, D], F32, kind="ExternalInput").ap()
    d_y = nc.dram_tensor("y", [S, D], F32, kind="ExternalOutput").ap()
    wnames = ["aq", "ak", "av", "ao", "bq", "bk", "bv", "bo", "fr", "fk", "fv"]
    d_w = {n: nc.dram_tensor("w_" + n, [D, D], F32, kind="ExternalInput").ap()
           for n in wnames}
    d_mur = nc.dram_tensor("mu_r", [128, 8], F32, kind="ExternalInput").ap()
    d_muk = nc.dram_tensor("mu_k", [128, 8], F32, kind="ExternalInput").ap()

    d_xT = nc.dram_tensor("xT_scr", [NFT, 128, S], F32, kind="Internal").ap()
    d_Qsp = nc.dram_tensor("Q_sp", [H, HD, S], BF16, kind="Internal").ap()
    d_Ksp = nc.dram_tensor("K_sp", [H, HD, S], BF16, kind="Internal").ap()
    d_Vsp = nc.dram_tensor("V_sp", [H, S, HD], BF16, kind="Internal").ap()
    d_xnsp = nc.dram_tensor("xn_sp", [NFT, 128, S], BF16, kind="Internal").ap()
    d_dbg = {}
    if DEBUG_TAPS:
        d_dbg["dbg_Q"] = nc.dram_tensor("dbg_Q", [H, HD, S], BF16, kind="ExternalOutput").ap()
        d_dbg["dbg_K"] = nc.dram_tensor("dbg_K", [H, HD, S], BF16, kind="ExternalOutput").ap()
        d_dbg["dbg_V"] = nc.dram_tensor("dbg_V", [H, S, HD], BF16, kind="ExternalOutput").ap()
        d_dbg["dbg_ao"] = nc.dram_tensor("dbg_ao", [128, 8 * 4096], BF16, kind="ExternalOutput").ap()
        d_dbg["dbg_xnsp"] = nc.dram_tensor("dbg_xnsp", [NFT, 128, S], BF16, kind="ExternalOutput").ap()
        d_dbg["dbg_kv"] = nc.dram_tensor("dbg_kv", [128, 8 * 65], F32, kind="ExternalOutput").ap()
        d_dbg["dbg_ktok"] = nc.dram_tensor("dbg_ktok", [128, 4096], BF16, kind="ExternalOutput").ap()
        d_dbg["dbg_vaug"] = nc.dram_tensor("dbg_vaug", [128, 4352], BF16, kind="ExternalOutput").ap()
        d_dbg["dbg_praw"] = nc.dram_tensor("dbg_praw", [128, 512], F32, kind="ExternalOutput").ap()
        d_dbg["dbg_w1"] = nc.dram_tensor("dbg_w1", [128, 8192], BF16, kind="ExternalOutput").ap()
        d_dbg["dbg_xnc7"] = nc.dram_tensor("dbg_xnc7", [128, 4096], BF16, kind="ExternalOutput").ap()

    ident_f = nc.inline_tensor(np.eye(128, dtype=np.float32), "ident_f").ap()
    ones_col_b = nc.inline_tensor(np.ones((128, 1), dtype=np.float32).astype(BF), "ones_col_b").ap()
    ones_1x128_f = nc.inline_tensor(np.ones((1, 128), dtype=np.float32), "ones_1x128_f").ap()
    ones_1x64_f = nc.inline_tensor(np.ones((1, 64), dtype=np.float32), "ones_1x64_f").ap()
    ones_1x64_b = nc.inline_tensor(np.ones((1, 64), dtype=np.float32).astype(BF), "ones_1x64_b").ap()
    ones3_b = nc.inline_tensor(np.ones((3, S), dtype=np.float32).astype(BF), "ones3_b").ap()
    tri_np = np.where(np.arange(128)[:, None] > np.arange(128)[None, :], -1e9, 0.0).astype(np.float32)
    tri_m = nc.inline_tensor(tri_np, "tri_m").ap()

    kal = np.zeros((H, 3, S), dtype=np.float64)
    for h in range(H):
        vk = SLOPES[h] * (np.arange(S) % 128).astype(np.float64)
        hi, mid, lo = exact_split3(vk)
        kal[h, 0], kal[h, 1], kal[h, 2] = -hi, -mid, -lo
    d_kal = nc.inline_tensor(kal.astype(np.float32).astype(BF), "kal").ap()

    fvals_raw = [0.0, 1.0, -1.0, 1.0 / D, LN_EPS, 0.125, -SHIFT, ATTNB_EPS]
    for h in range(H):
        for qi in range(NCHUNK):
            ks_lo, ks_hi = kept_ks_range(h, qi)
            for ks in range(ks_lo, ks_hi + 1):
                fvals_raw.append(-SLOPES[h] * 128 * ks - SHIFT)
    fidx = {}
    fcols = []
    for v in fvals_raw:
        v32 = np.float32(v).item()
        if v32 not in fidx:
            fidx[v32] = len(fcols)
            fcols.append(v32)
        fidx[v] = fidx[v32]
    d_fpool = nc.inline_tensor(
        np.tile(np.array([fcols], dtype=np.float32), (128, 1)), "fpool").ap()

    uid = [0]

    with contextlib.ExitStack() as st:
        b = Builder(nc)
        for s in ("ldx", "ldw", "ldh", "ldm", "stx", "pe", "ve", "ac", "po"):
            b.sem(st, s)

        def mk_sb(stack, name, shape, dt):
            uid[0] += 1
            return stack.enter_context(nc.sbuf_tensor(f"{name}_{uid[0]}", shape, dt))

        s_fpool = mk_sb(st, "fpool", [128, len(fcols)], F32)
        s_identf = mk_sb(st, "identf", [128, 128], F32)
        s_onescol = mk_sb(st, "onescol", [128, 1], BF16)
        s_ones128 = mk_sb(st, "ones128", [1, 128], F32)
        s_ones64 = mk_sb(st, "ones64", [1, 64], F32)
        s_ones64b = mk_sb(st, "ones64b", [1, 64], BF16)
        s_tri = mk_sb(st, "tri", [128, 128], F32)
        s_mur = mk_sb(st, "mur", [128, 8], F32)
        s_muk = mk_sb(st, "muk", [128, 8], F32)
        W0 = mk_sb(st, "W0", [128, 8192], BF16)
        W1 = mk_sb(st, "W1", [128, 8192], BF16)
        W2 = mk_sb(st, "W2", [128, 8192], BF16)
        W3 = mk_sb(st, "W3", [128, 8192], BF16)
        W4 = mk_sb(st, "W4", [128, 8192], BF16)
        kvsb = mk_sb(st, "kvsb", [64, 16 * 65], F32)
        kvbf = mk_sb(st, "kvbf", [128, 8 * 65], BF16)

        for v, i in fidx.items():
            nc.const_aps.register(F32, v, s_fpool[:, i:i + 1])

        def fc(v):
            return s_fpool[:, fidx[v]:fidx[v] + 1]

        b.dma("sync", s_fpool[:], d_fpool, "ldm")
        b.dma("sync", s_identf[:], ident_f, "ldm")
        b.dma("sync", s_onescol[:], ones_col_b, "ldm")
        b.dma("sync", s_ones128[:], ones_1x128_f, "ldm")
        b.dma("sync", s_ones64[:], ones_1x64_f, "ldm")
        b.dma("sync", s_ones64b[:], ones_1x64_b, "ldm")
        b.dma("sync", s_tri[:], tri_m, "ldm")
        b.dma("sync", s_mur[:], d_mur, "ldm")
        b.dma("sync", s_muk[:], d_muk, "ldm")
        for eng in ("tensor", "vector", "scalar", "gpsimd"):
            b.wait(eng, "ldm")

        def psum(stack2, name, shape, dt=F32):
            uid[0] += 1
            return stack2.enter_context(nc.psum_tensor(f"{name}_{uid[0]}", shape, dt))

        def load_weight(slot, wname, pe_guard=None):
            b.wait("gpsimd", "pe", b.cnt["pe"] if pe_guard is None else pe_guard)
            b.dma("gpsimd", slot.ap().rearrange("p (c d) -> p c d", c=8),
                  d_w[wname].rearrange("(c p) d -> p c d", p=128), "po", )

        def wslot_ap(slot, c, eo):
            return slot[:, 1024 * c + 128 * eo: 1024 * c + 128 * eo + 128]

        with contextlib.ExitStack() as ps:
            xc = [mk_sb(ps, "p0xc", [128, 1024], F32) for _ in range(2)]
            outc = [mk_sb(ps, "p0oc", [128, 1024], F32) for _ in range(2)]
            p_tr = [psum(ps, "ptr", [128, 128], F32) for _ in range(2)]
            p0_pe = {}
            for tt in range(NTT):
                t0 = tt * 128
                buf = xc[tt % 2]
                b.wait("sync", "pe", p0_pe.get(tt % 2, 0))
                b.dma("sync", buf[:], d_x[t0:t0 + 128, :], "ldx")
                b.wait("tensor", "ldx")
                b.wait_store("vector", ("p0oc", tt % 2))
                for f in range(NFT):
                    pb = p_tr[f % 2]
                    b.wait("tensor", "ve", b.cnt["ve"] - 1)
                    b.op("tensor", lambda t, pb=pb, buf=buf, f=f: t.transpose(
                        pb[:], buf[:, 128 * f:128 * f + 128], s_identf[:]), inc="pe")
                    b.wait("vector", "pe")
                    b.op("vector", lambda v, pb=pb, f=f, tt=tt: v.tensor_copy(
                        outc[tt % 2][:, 128 * f:128 * f + 128], pb[:]), inc="ve")
                p0_pe[tt % 2] = b.cnt["pe"]
                b.wait("scalar", "ve")
                b.store("scalar",
                        d_xT[:, :, t0:t0 + 128].rearrange("f p t -> p f t"),
                        outc[tt % 2].ap().rearrange("p (f t) -> p f t", f=8),
                        key=("p0oc", tt % 2))
                if tt % 4 == 3:
                    b.sdone[("xT", tt // 4)] = b.cnt["stx"]
        b.junction()

        def emit_ln(ar, par, xn_dest=None, xn_store_key=None):
            xcb, xbb = ar["xc"][par], ar["xb"][par]
            ln_row = ar["lnrow"]
            if xn_dest is None:
                xnb = ar["xn"][par]
                xn_dest = [xnb[:, 512 * f:512 * f + 512] for f in range(NFT)]
            with contextlib.ExitStack() as ps:
                p_s = psum(ps, "lns", [1, 512], F32)[:]
                p_q = psum(ps, "lnq", [1, 512], F32)[:]
                p_m = psum(ps, "lnm", [128, 512], F32)
                p_r = psum(ps, "lnr", [128, 512], F32)
                xg = ar.setdefault("xb_g", {})
                b.wait("gpsimd", "pe", xg.get(par, (0, 0))[0])
                b.wait("gpsimd", "ve", xg.get(par, (0, 0))[1])
                b.wait("gpsimd", "pe", ar.setdefault("sq_g", [0])[0])
                b.wait("gpsimd", "ldx")
                b.op("gpsimd", lambda g: g.tensor_copy(xbb[:], xcb[:]), inc="po")
                b.op("gpsimd", lambda g: g.tensor_mul(ar["sq"][:], xbb[:], xbb[:]), inc="po")
                b.wait("tensor", "po")
                for f in range(NFT):
                    b.op("tensor", lambda t, f=f: t.matmul(
                        p_s, s_onescol[:], xbb[:, 512 * f:512 * f + 512],
                        start=(f == 0), stop=(f == 7)), inc="pe" if f == 7 else None)
                for f in range(NFT):
                    b.op("tensor", lambda t, f=f: t.matmul(
                        p_q, s_onescol[:], ar["sq"][:, 512 * f:512 * f + 512],
                        start=(f == 0), stop=(f == 7)), inc="pe" if f == 7 else None)
                ar["sq_g"][0] = b.cnt["pe"]
                b.wait("vector", "pe")
                b.op("vector", lambda v: v.tensor_scalar(
                    ln_row[:, 512:1024], p_s, 1.0 / D, None, ALU.mult))
                b.op("vector", lambda v: v.tensor_mul(
                    ln_row[:, 1024:1536], ln_row[:, 512:1024], ln_row[:, 512:1024]))
                b.op("vector", lambda v: v.scalar_tensor_tensor(
                    ln_row[:, 0:512], ln_row[:, 1024:1536], -float(D), p_q,
                    ALU.mult, ALU.add), inc="ve")
                b.wait("scalar", "ve")
                b.op("scalar", lambda s: s.activation(
                    ln_row[:, 0:512], ln_row[:, 0:512], AF.Ln,
                    bias=LN_EPS, scale=1.0 / D))
                b.op("scalar", lambda s: s.activation(
                    ln_row[:, 0:512], ln_row[:, 0:512], AF.Exp, scale=-0.5), inc="ac")
                b.wait("tensor", "ac")
                b.op("tensor", lambda t: t.matmul(p_m[:], s_ones128[:], ln_row[:, 512:1024],
                                                  start=True, stop=True))
                b.op("tensor", lambda t: t.matmul(p_r[:], s_ones128[:], ln_row[:, 0:512],
                                                  start=True, stop=True), inc="pe")
                b.wait("vector", "pe", ar.setdefault("xn_pe_g", {}).get(par, 0))
                b.wait("vector", "pe")
                b.wait("vector", "po")
                if xn_store_key is not None:
                    b.wait_store("vector", xn_store_key)
                for f in range(NFT):
                    dst = xn_dest[f]
                    b.op("vector", lambda v, dst=dst, f=f: v.tensor_sub(
                        dst, xbb[:, 512 * f:512 * f + 512], p_m[:]))
                    b.op("vector", lambda v, dst=dst: v.tensor_mul(
                        dst, dst, p_r[:]), inc="ve" if f == 7 else None)
                ar["xb_g"][par] = (b.cnt["pe"], b.cnt["ve"])
            b.wait("tensor", "ve")

        def emit_proj_fm(slot, rhs_buf, evac_fn, pbanks, guard):
            nb = len(pbanks)
            for eo in range(8):
                pp = pbanks[eo % nb]
                for g in guard.get(eo % nb, []):
                    b.wait("tensor", g[0], g[1])
                for c in range(8):
                    b.op("tensor", lambda t, pp=pp, c=c, eo=eo: t.matmul(
                        pp[:], wslot_ap(slot, c, eo),
                        rhs_buf[:, 512 * c:512 * c + 512],
                        start=(c == 0), stop=(c == 7)), inc="pe" if c == 7 else None)
                evac_fn(eo, pp)
                guard[eo % nb] = [("ve", b.cnt["ve"]), ("ac", b.cnt["ac"])]

        def load_chunk(ar, par, t0, tc, also_xn_sp=False):
            b.wait_store("sync", ("xT", tc))
            fg = ar.setdefault("free_g", {})
            b.wait("sync", "ve", fg.get(par, 0))
            if "name" in ar:
                b.wait_store("sync", (ar["name"], "xc", par))
            if also_xn_sp:
                b.wait_store("sync", ("bxn", tc))
                b.dma("sync", ar["xn"][par].ap().rearrange("p (f j) -> p f j", f=8),
                      d_xnsp[:, :, t0:t0 + CH].rearrange("f p j -> p f j"), "ldx")
            b.dma("sync", ar["xc"][par].ap().rearrange("p (f j) -> p f j", f=8),
                  d_xT[:, :, t0:t0 + CH].rearrange("f p j -> p f j"), "ldx")

        def sl_attn_a(prefetch_after_b=None):
            load_weight(W0, "aq", pe_guard=0)
            load_weight(W1, "ak", pe_guard=0)
            load_weight(W2, "av", pe_guard=0)
            load_weight(W3, "ao", pe_guard=0)
            with contextlib.ExitStack() as ps:
                scrB = mk_sb(ps, "ascrB", [128, 4096], BF16)
                scrC = mk_sb(ps, "ascrC", [128, 4096], BF16)
                scrD = mk_sb(ps, "ascrD", [128, 4096], BF16)
                ar = {
                    "xc": [mk_sb(ps, "axc", [128, 4096], F32) for _ in range(2)],
                    "xb": [mk_sb(ps, "axb", [128, 4096], BF16) for _ in range(2)],
                    "xn": [mk_sb(ps, "axn", [128, 4096], BF16) for _ in range(2)],
                    "sq": mk_sb(ps, "asq", [128, 4096], BF16),
                    "lnrow": mk_sb(ps, "alnr", [1, 1536], F32),
                }
                pb = [psum(ps, "pa", [128, 512], F32) for _ in range(2)]
                pv = [psum(ps, "pv", [128, 512], F32) for _ in range(2)]
                guard = {}
                vguard = {}
                load_chunk(ar, 0, 0, 0)
                emit_ln(ar, 0)
                for tc in range(NCHUNK):
                    par = tc % 2
                    t0 = tc * CH
                    b.wait("tensor", "po", b.cnt["po"])
                    b.wait_store("scalar", "aQK")

                    def ev_q(eo, pp):
                        b.wait("scalar", "pe")
                        b.op("scalar", lambda s, pp=pp, eo=eo: s.activation(
                            scrB[:, 512 * eo:512 * eo + 512], pp[:], AF.Copy,
                            scale=0.125), inc="ac")

                    def ev_k(eo, pp):
                        b.wait("scalar", "pe")
                        b.op("scalar", lambda s, pp=pp, eo=eo: s.activation(
                            scrC[:, 512 * eo:512 * eo + 512], pp[:], AF.Copy), inc="ac")

                    emit_proj_fm(W0, ar["xn"][par], ev_q, pb, guard)
                    if tc < NCHUNK - 1:
                        load_chunk(ar, 1 - par, t0 + CH, tc + 1)
                        emit_ln(ar, 1 - par)
                    emit_proj_fm(W1, ar["xn"][par], ev_k, pb, guard)
                    b.wait("scalar", "ac")
                    b.store("scalar",
                            d_Qsp.rearrange("(e two) hd t -> e (two hd) t", two=2)[:, :, t0:t0 + CH].rearrange("e p t -> p e t"),
                            scrB.ap().rearrange("p (e t) -> p e t", e=8))
                    b.store("scalar",
                            d_Ksp.rearrange("(e two) hd t -> e (two hd) t", two=2)[:, :, t0:t0 + CH].rearrange("e p t -> p e t"),
                            scrC.ap().rearrange("p (e t) -> p e t", e=8), key="aQK")
                    b.wait_store("vector", "aV")
                    for eo in range(2):
                        for ts in range(4):
                            bi = (eo * 4 + ts) % 2
                            pp = pv[bi]
                            for g in vguard.get(bi, []):
                                b.wait("tensor", g[0], g[1])
                            for c in range(8):
                                b.op("tensor", lambda t, pp=pp, c=c, ts=ts, eo=eo, par=par: t.matmul(
                                    pp[:], ar["xn"][par][:, 512 * c + 128 * ts:512 * c + 128 * ts + 128],
                                    W2[:, 1024 * c + 512 * eo:1024 * c + 512 * eo + 512],
                                    start=(c == 0), stop=(c == 7)), inc="pe" if c == 7 else None)
                            b.wait("vector", "pe")
                            b.op("vector", lambda v, pp=pp, ts=ts, eo=eo: v.tensor_copy(
                                scrD[:, 1024 * ts + 512 * eo:1024 * ts + 512 * eo + 512], pp[:]),
                                inc="ve")
                            vguard[bi] = [("ve", b.cnt["ve"])]
                    b.wait("scalar", "ve")
                    for ts in range(4):
                        b.store("scalar",
                                d_Vsp[:, t0 + 128 * ts:t0 + 128 * ts + 128, :].rearrange("h p hd -> p h hd"),
                                scrD[:, 1024 * ts:1024 * ts + 1024].rearrange("p (h hd) -> p h hd", h=16),
                                key="aV" if ts == 3 else None)
                    ar["free_g"][par] = b.cnt["ve"]
                    ar["xn_pe_g"][par] = b.cnt["pe"]
            po_own = b.cnt["po"]
            if prefetch_after_b:
                prefetch_after_b()
            if DEBUG_TAPS:
                b.wait("sync", "stx", b.cnt["stx"])
                b.dma("sync", d_dbg["dbg_Q"], d_Qsp, "stx")
                b.dma("sync", d_dbg["dbg_K"], d_Ksp, "stx")
                b.dma("sync", d_dbg["dbg_V"], d_Vsp, "stx")
            b.junction()
            with contextlib.ExitStack() as outer:
                aoT = mk_sb(outer, "aoT", [128, 8 * 4096], BF16)
                with contextlib.ExitStack() as ps:
                    Kbuf = [mk_sb(ps, "Kb", [128, S], BF16) for _ in range(2)]
                    Qbuf = [mk_sb(ps, "Qb", [128, S], BF16) for _ in range(2)]
                    Vbuf = [mk_sb(ps, "Vb", [128, 32 * 66], BF16) for _ in range(2)]
                    Pbuf = [mk_sb(ps, "Pb", [128, 512], BF16) for _ in range(3)]
                    rec_sb = mk_sb(ps, "recsb", [1, 512], F32)
                    rec64 = mk_sb(ps, "rec64", [64, 512], F32)
                    s_hilo = mk_sb(ps, "hilo", [1, 1024], BF16)
                    p_s2 = [psum(ps, "ps", [128, 512], F32) for _ in range(3)]
                    p_av = [psum(ps, "pav", [65, 512], F32) for _ in range(2)]
                    p_rb = psum(ps, "prb", [64, 512], F32)
                    for bb in range(2):
                        b.op("vector", lambda v, bb=bb: v.memset(Vbuf[bb][:], 1.0))
                        b.op("vector", lambda v, bb=bb: v.memset(Kbuf[bb][64:128, :], 0.0))
                        b.op("vector", lambda v, bb=bb: v.memset(Qbuf[bb][64:128, :], 0.0), inc="ve")
                    b.wait("sync", "ve", b.cnt["ve"])
                    for bb in range(2):
                        b.dma("sync", Qbuf[bb][64:67, :], ones3_b, "ldm")
                    b.wait("tensor", "ldm", b.cnt["ldm"])
                    last_head_pe = {}
                    sbank_g = {0: 0, 1: 0, 2: 0}
                    pbuf_g = {0: 0, 1: 0, 2: 0}
                    pav_g = {0: 0, 1: 0}
                    prb_g = [0]
                    pending_div = [None]

                    def flush_div():
                        if pending_div[0] is not None:
                            pending_div[0]()
                            pending_div[0] = None
                    for h in range(H):
                        hb = h % 2
                        if h >= 2:
                            b.wait("sync", "pe", last_head_pe[h - 2])
                        b.dma("sync", Kbuf[hb][0:64, :], d_Ksp[h], "ldh")
                        b.dma("sync", Kbuf[hb][64:67, :], d_kal[h], "ldh")
                        b.dma("sync", Qbuf[hb][0:64, :], d_Qsp[h], "ldh")
                        b.dma("sync", Vbuf[hb].ap().rearrange("p (ts c) -> p ts c", c=66)[:, :, 0:64],
                              d_Vsp[h].rearrange("(ts p) hd -> p ts hd", p=128), "ldh")
                        b.wait("tensor", "ldh")
                        for qi in range(NCHUNK):
                            q0 = qi * CH
                            ks_lo, ks_hi = kept_ks_range(h, qi)
                            ks_list = list(range(ks_lo, ks_hi + 1))
                            nks = len(ks_list)
                            pav = p_av[qi % 2]
                            b.wait("tensor", "ve", pav_g[qi % 2])
                            exp_at = {}
                            av_emitted = [0]

                            def emit_av(i, ks, first, last, pav=pav, hb=hb):
                                co = max(0, 128 * ks - q0)
                                b.wait("tensor", "ac", exp_at[i])
                                b.op("tensor", lambda t, pav=pav, i=i, ks=ks, co=co, hb=hb,
                                     first=first, last=last: t.matmul(
                                    pav[0:65, co:512], Vbuf[hb][:, 66 * ks:66 * ks + 65],
                                    Pbuf[i % 3][:, co:512],
                                    start=first, stop=last), inc="pe")
                                pbuf_g[i % 3] = b.cnt["pe"]

                            for i, ks in enumerate(ks_list):
                                k0 = ks * 128
                                co = max(0, k0 - q0)
                                sp = p_s2[i % 3]
                                diag = (k0 >= q0)
                                b.wait("tensor", "ac", sbank_g[i % 3])
                                b.op("tensor", lambda t, sp=sp, ks=ks, co=co, q0=q0, hb=hb: t.matmul(
                                    sp[:, co:512], Kbuf[hb][0:128, 128 * ks:128 * ks + 128],
                                    Qbuf[hb][0:128, q0 + co:q0 + 512],
                                    start=True, stop=True), inc="pe")
                                s_pe = b.cnt["pe"]
                                if diag:
                                    b.wait("vector", "pe", s_pe)
                                    b.op("vector", lambda v, sp=sp, co=co: v.tensor_add(
                                        sp[:, co:co + 128], sp[:, co:co + 128], s_tri[:]), inc="ve")
                                    b.wait("scalar", "ve")
                                else:
                                    b.wait("scalar", "pe", s_pe)
                                b.wait("scalar", "pe", pbuf_g[i % 3])
                                bias = np.float32(-SLOPES[h] * 128 * ks - SHIFT).item()
                                b.op("scalar", lambda s, sp=sp, co=co, i=i, bias=bias: s.activation(
                                    Pbuf[i % 3][:, co:512], sp[:, co:512], AF.Exp, bias=fc(bias)),
                                    inc="ac")
                                exp_at[i] = b.cnt["ac"]
                                sbank_g[i % 3] = b.cnt["ac"]
                                if i == min(2, nks - 1):
                                    flush_div()
                                if i >= 1:
                                    emit_av(i - 1, ks_list[i - 1], i - 1 == 0, False)
                            emit_av(nks - 1, ks_list[nks - 1], nks == 1, True)
                            av_pe = b.cnt["pe"]

                            def _div(pav=pav, av_pe=av_pe, prb_tgt=prb_g[0],
                                     h=h, q0=q0, qparity=qi % 2):
                                b.wait("scalar", "pe", av_pe)
                                b.op("scalar", lambda s, pav=pav: s.activation(
                                    rec_sb[:], pav[64:65, :], AF.Ln))
                                b.op("scalar", lambda s: s.activation(
                                    rec_sb[:], rec_sb[:], AF.Exp, scale=-1.0), inc="ac")
                                div_ac = b.cnt["ac"]
                                b.wait("vector", "ac", div_ac)
                                b.op("vector", lambda v: v.tensor_copy(
                                    s_hilo[:, 0:512], rec_sb[:]))
                                b.op("vector", lambda v: v.scalar_tensor_tensor(
                                    s_hilo[:, 512:1024], s_hilo[:, 0:512], -1.0, rec_sb[:],
                                    ALU.mult, ALU.add), inc="ve")
                                hilo_ve = b.cnt["ve"]
                                b.wait("tensor", "ve", max(hilo_ve, prb_tgt))
                                b.op("tensor", lambda t: t.matmul(
                                    p_rb[:], s_ones64b[:], s_hilo[:, 0:512],
                                    start=True, stop=False))
                                b.op("tensor", lambda t: t.matmul(
                                    p_rb[:], s_ones64b[:], s_hilo[:, 512:1024],
                                    start=False, stop=True), inc="pe")
                                b.wait("vector", "pe")
                                b.op("vector", lambda v: v.tensor_copy(rec64[:], p_rb[:]))
                                fo, ro = h // 2, 64 * (h % 2)
                                b.op("vector", lambda v, pav=pav, fo=fo, ro=ro, q0=q0: v.tensor_mul(
                                    aoT[ro:ro + 64, 4096 * fo + q0:4096 * fo + q0 + 512],
                                    pav[0:64, :], rec64[:]), inc="ve")
                                prb_g[0] = b.cnt["ve"]
                                pav_g[qparity] = b.cnt["ve"]

                            pending_div[0] = _div
                        last_head_pe[h] = b.cnt["pe"]
                    flush_div()
                b.junction()
                if DEBUG_TAPS:
                    b.dma("sync", d_dbg["dbg_ao"], aoT[:], "stx")
                with contextlib.ExitStack() as ps:
                    xc2 = [mk_sb(ps, "wxc", [128, 4096], F32) for _ in range(2)]
                    pw = [psum(ps, "pw", [128, 512], F32) for _ in range(2)]
                    wg = {}
                    for tc in range(NCHUNK):
                        par = tc % 2
                        t0 = tc * CH
                        b.wait_store("sync", ("wxc", par))
                        b.wait_store("sync", ("xT", tc))
                        b.dma("sync", xc2[par].ap().rearrange("p (f j) -> p f j", f=8),
                              d_xT[:, :, t0:t0 + CH].rearrange("f p j -> p f j"), "ldx")
                        b.wait("tensor", "ldx")
                        b.wait("tensor", "po", po_own)
                        for eo in range(8):
                            pp = pw[eo % 2]
                            for g in wg.get(eo % 2, []):
                                b.wait("tensor", g[0], g[1])
                            for c in range(8):
                                b.op("tensor", lambda t, pp=pp, c=c, eo=eo, t0=t0: t.matmul(
                                    pp[:], wslot_ap(W3, c, eo),
                                    aoT[:, 4096 * c + t0:4096 * c + t0 + 512],
                                    start=(c == 0), stop=(c == 7)), inc="pe" if c == 7 else None)
                            b.wait("vector", "pe")
                            b.op("vector", lambda v, pp=pp, eo=eo, par=par: v.tensor_add(
                                xc2[par][:, 512 * eo:512 * eo + 512], pp[:],
                                xc2[par][:, 512 * eo:512 * eo + 512]), inc="ve")
                            wg[eo % 2] = [("ve", b.cnt["ve"])]
                        b.wait("scalar", "ve")
                        b.store("scalar", d_xT[:, :, t0:t0 + CH].rearrange("f p j -> p f j"),
                                xc2[par].ap().rearrange("p (f j) -> p f j", f=8),
                                key=("wxc", par))
                        b.sdone[("xT", tc)] = b.cnt["stx"]
            b.junction()

        def sl_ffn(tag, po_guard, chunk_cbs=None):
            Wr_, Wkf_, Wvf_ = W0, W1, W2
            with contextlib.ExitStack() as ps:
                xnsh = mk_sb(ps, "fxnsh", [128, 8 * 520], BF16)
                scrB = mk_sb(ps, "fscrB", [128, 4096], BF16)
                scrC = mk_sb(ps, "fscrC", [128, 4096], BF16)
                scrD = mk_sb(ps, "fscrD", [128, 4096], BF16)
                ar = {
                    "xc": [mk_sb(ps, "fxc", [128, 4096], F32) for _ in range(2)],
                    "xb": [mk_sb(ps, "fxb", [128, 4096], BF16) for _ in range(2)],
                    "sq": mk_sb(ps, "fsq", [128, 4096], BF16),
                    "lnrow": mk_sb(ps, "flnr", [1, 1536], F32),
                    "name": "f" + tag,
                }
                tmp512 = mk_sb(ps, "ftmp", [128, 1024], F32)
                pb = [psum(ps, "pf", [128, 512], F32) for _ in range(3)]
                for f in range(NFT):
                    b.op("vector", lambda v, f=f: v.memset(xnsh[:, 520 * f:520 * f + 1], 0.0),
                         inc="ve" if f == 7 else None)
                guard = {}
                xnd = [xnsh[:, 520 * f + 1:520 * f + 513] for f in range(NFT)]
                load_chunk(ar, 0, 0, 0)
                emit_ln(ar, 0, xn_dest=xnd)
                for tc in range(NCHUNK):
                    par = tc % 2
                    t0 = tc * CH
                    dbuf = ar["xb"][par]
                    b.wait("gpsimd", "ve")
                    b.wait("gpsimd", "pe", ar["sq_g"][0])
                    for f in range(NFT):
                        b.op("gpsimd", lambda g, dbuf=dbuf, f=f: g.tensor_sub(
                            dbuf[:, 512 * f:512 * f + 512],
                            xnsh[:, 520 * f + 1:520 * f + 513], xnsh[:, 520 * f:520 * f + 512]),
                            inc="po" if f == 7 else None)
                    b.wait("vector", "po")
                    b.wait("vector", "pe", b.cnt["pe"])
                    for f in range(NFT):
                        b.op("vector", lambda v, dbuf=dbuf, f=f: v.scalar_tensor_tensor(
                            scrB[:, 512 * f:512 * f + 512], dbuf[:, 512 * f:512 * f + 512],
                            s_mur[:, f:f + 1], xnsh[:, 520 * f:520 * f + 512], ALU.mult, ALU.add))
                        b.op("vector", lambda v, dbuf=dbuf, f=f: v.scalar_tensor_tensor(
                            scrC[:, 512 * f:512 * f + 512], dbuf[:, 512 * f:512 * f + 512],
                            s_muk[:, f:f + 1], xnsh[:, 520 * f:520 * f + 512], ALU.mult, ALU.add),
                            inc="ve" if f == 7 else None)
                    ar["xb_g"][par] = (b.cnt["pe"], b.cnt["ve"])
                    b.wait("tensor", "po", po_guard)
                    b.wait("tensor", "ve")

                    def ev_k(eo, pp):
                        b.wait("vector", "pe")
                        b.op("vector", lambda v, pp=pp: v.tensor_scalar(
                            pp[:], pp[:], 0.0, None, ALU.max), inc="ve")
                        b.wait("scalar", "ve")
                        b.op("scalar", lambda s, pp=pp, eo=eo: s.activation(
                            scrD[:, 512 * eo:512 * eo + 512], pp[:],
                            AF.Square), inc="ac")

                    emit_proj_fm(Wkf_, scrC, ev_k, pb, guard)
                    if tc < NCHUNK - 1:
                        b.wait("vector", "po")
                        for f in range(NFT):
                            b.op("vector", lambda v, f=f: v.tensor_copy(
                                xnsh[:, 520 * f:520 * f + 1], xnsh[:, 520 * f + 512:520 * f + 513]),
                                inc="ve" if f == 7 else None)
                        load_chunk(ar, 1 - par, t0 + CH, tc + 1)
                        emit_ln(ar, 1 - par, xn_dest=xnd)

                    def ev_r(eo, pp):
                        b.wait("scalar", "pe")
                        b.op("scalar", lambda s, pp=pp, eo=eo: s.activation(
                            scrC[:, 512 * eo:512 * eo + 512], pp[:], AF.Sigmoid), inc="ac")

                    emit_proj_fm(Wr_, scrB, ev_r, pb, guard)
                    b.wait("tensor", "ac")

                    def ev_o(eo, pp):
                        b.wait("vector", "pe")
                        b.wait("vector", "ac")
                        b.op("vector", lambda v, pp=pp, eo=eo: v.tensor_mul(
                            tmp512[:, 512 * (eo % 2):512 * (eo % 2) + 512], pp[:],
                            scrC[:, 512 * eo:512 * eo + 512]), inc="ve")
                        b.wait("gpsimd", "ve")
                        b.op("gpsimd", lambda g, eo=eo, par=par: g.tensor_add(
                            ar["xc"][par][:, 512 * eo:512 * eo + 512],
                            tmp512[:, 512 * (eo % 2):512 * (eo % 2) + 512],
                            ar["xc"][par][:, 512 * eo:512 * eo + 512]), inc="po")

                    emit_proj_fm(Wvf_, scrD, ev_o, pb, guard)
                    b.wait("scalar", "ve")
                    b.wait("scalar", "po", b.cnt["po"])
                    b.store("scalar", d_xT[:, :, t0:t0 + CH].rearrange("f p j -> p f j"),
                            ar["xc"][par].ap().rearrange("p (f j) -> p f j", f=8),
                            key=(ar["name"], "xc", par))
                    b.sdone[("xT", tc)] = b.cnt["stx"]
                    ar["free_g"][par] = b.cnt["ve"]
                    if chunk_cbs and tc in chunk_cbs:
                        chunk_cbs[tc]()
            b.junction()

        def sl_attn_b():
            s1k, s1v = W3, W4
            with contextlib.ExitStack() as ps:
                scrB = mk_sb(ps, "bscrB", [128, 4096], BF16)
                scrC = mk_sb(ps, "bscrC", [128, 4 * 1088], BF16)
                scrE = mk_sb(ps, "bscrE", [128, 1024], F32)
                ar = {
                    "xc": [mk_sb(ps, "bxc", [128, 4096], F32) for _ in range(2)],
                    "xb": [mk_sb(ps, "bxb", [128, 4096], BF16) for _ in range(2)],
                    "xn": [mk_sb(ps, "bxn", [128, 4096], BF16) for _ in range(2)],
                    "sq": mk_sb(ps, "bsq", [128, 4096], BF16),
                    "lnrow": mk_sb(ps, "blnr", [1, 1536], F32),
                }
                pb = [psum(ps, "pk", [128, 512], F32) for _ in range(2)]
                pkv = [psum(ps, "pkv", [64, 65], F32) for _ in range(2)]
                b.op("vector", lambda v: v.memset(kvsb[:], 0.0), inc="ve")
                b.op("vector", lambda v: v.memset(scrC[:], 1.0), inc="ve")
                guard = {}
                kvg = {}
                load_chunk(ar, 0, 0, 0)
                emit_ln(ar, 0, xn_store_key=("bxn_b", 0))
                for tc in range(NCHUNK):
                    par = tc % 2
                    t0 = tc * CH
                    b.wait("scalar", "ve")
                    b.wait_store("scalar", ("bxn_b", par))
                    b.store("scalar", d_xnsp[:, :, t0:t0 + CH].rearrange("f p j -> p f j"),
                            ar["xn"][par].ap().rearrange("p (f j) -> p f j", f=8),
                            key=("bxn_b", par))
                    b.sdone[("bxn", tc)] = b.cnt["stx"]
                    b.wait("tensor", "po", b.cnt["po"])
                    b.wait("tensor", "ve")
                    for which in ("k", "v"):
                        if which == "v" and tc < NCHUNK - 1:
                            load_chunk(ar, 1 - par, t0 + CH, tc + 1)
                            emit_ln(ar, 1 - par, xn_store_key=("bxn_b", 1 - par))
                        slot = s1k if which == "k" else s1v
                        for eo in range(2):
                            for ts in range(4):
                                bi = (eo * 4 + ts) % 2
                                pp = pb[bi]
                                for g in guard.get(bi, []):
                                    b.wait("tensor", g[0], g[1])
                                for c in range(8):
                                    b.op("tensor", lambda t, pp=pp, c=c, ts=ts, eo=eo, slot=slot, par=par: t.matmul(
                                        pp[:], ar["xn"][par][:, 512 * c + 128 * ts:512 * c + 128 * ts + 128],
                                        slot[:, 1024 * c + 512 * eo:1024 * c + 512 * eo + 512],
                                        start=(c == 0), stop=(c == 7)), inc="pe" if c == 7 else None)
                                if which == "k":
                                    b.wait("vector", "pe")
                                    b.op("vector", lambda v, pp=pp: v.tensor_scalar(
                                        scrE[:, 0:512], pp[:], 0.0, None, ALU.min), inc="ve")
                                    b.wait("scalar", "ve")
                                    b.op("scalar", lambda s: s.activation(
                                        scrE[:, 512:1024], scrE[:, 0:512], AF.Exp), inc="ac")
                                    b.wait("vector", "ac")
                                    b.op("vector", lambda v, pp=pp, ts=ts, eo=eo: v.scalar_tensor_tensor(
                                        scrB[:, 1024 * ts + 512 * eo:1024 * ts + 512 * eo + 512],
                                        pp[:], 0.0, scrE[:, 512:1024], ALU.max, ALU.add), inc="ve")
                                else:
                                    b.wait("vector", "pe")
                                    if DEBUG_TAPS and tc == 7 and eo == 0 and ts == 0:
                                        b.op("vector", lambda v, pp=pp: v.tensor_copy(
                                            scrE[:, 0:512], pp[:]), inc="ve")
                                        b.wait("scalar", "ve")
                                        b.dma("scalar", d_dbg["dbg_praw"], scrE[:, 0:512], "stx")
                                        b.dma("scalar", d_dbg["dbg_w1"], W1[:], "stx")
                                        b.dma("scalar", d_dbg["dbg_xnc7"], ar["xn"][par][:], "stx")
                                    for i8 in range(8):
                                        base = 1088 * ts + 544 * eo + 68 * i8
                                        b.op("vector", lambda v, pp=pp, base=base, i8=i8: v.tensor_copy(
                                            scrC[:, base:base + 64], pp[:, 64 * i8:64 * i8 + 64]),
                                            inc="ve" if i8 == 7 else None)
                                guard[bi] = [("ve", b.cnt["ve"]), ("ac", b.cnt["ac"])]
                    b.wait("tensor", "ve")
                    for hh in range(H):
                        pk = pkv[hh % 2]
                        for g in kvg.get(hh % 2, []):
                            b.wait("tensor", g[0], g[1])
                        for ts in range(4):
                            b.op("tensor", lambda t, pk=pk, hh=hh, ts=ts: t.matmul(
                                pk[0:64, 0:65],
                                scrB[:, 1024 * ts + 64 * hh:1024 * ts + 64 * hh + 64],
                                scrC[:, 1088 * ts + 68 * hh:1088 * ts + 68 * hh + 65],
                                start=(ts == 0), stop=(ts == 3)), inc="pe" if ts == 3 else None)
                        b.wait("vector", "pe")
                        b.op("vector", lambda v, pk=pk, hh=hh: v.tensor_add(
                            kvsb[:, 65 * hh:65 * hh + 65], kvsb[:, 65 * hh:65 * hh + 65],
                            pk[0:64, :]), inc="ve")
                        kvg[hh % 2] = [("ve", b.cnt["ve"])]
                        if hh == H - 1:
                            ar["free_g"][par] = b.cnt["ve"]
                            ar["xn_pe_g"][par] = b.cnt["pe"]
                if DEBUG_TAPS:
                    b.junction()
                    b.dma("sync", d_dbg["dbg_xnsp"], d_xnsp, "stx")
                    b.dma("sync", d_dbg["dbg_kv"], kvsb[:], "stx")
                    b.dma("sync", d_dbg["dbg_ktok"], scrB[:], "stx")
                    b.dma("sync", d_dbg["dbg_vaug"], scrC[:], "stx")
                    b.wait("sync", "stx", b.cnt["stx"])
            b.junction()
            load_weight(W3, "bq")
            load_weight(W4, "bo")
            with contextlib.ExitStack() as ps:
                ar2 = {
                    "xc": [mk_sb(ps, "cxc", [128, 4096], F32) for _ in range(2)],
                    "xn": [mk_sb(ps, "cxn", [128, 4096], BF16) for _ in range(2)],
                }
                scrB2 = mk_sb(ps, "cscrB", [128, 4096], BF16)
                scrD2 = mk_sb(ps, "cscrD", [128, 4096], BF16)
                scrE2 = mk_sb(ps, "cscrE", [128, 1024], F32)
                t_ln2 = [mk_sb(ps, "tln2", [1, 512], F32) for _ in range(2)]
                rec64p = [mk_sb(ps, "rec64b", [64, 512], F32) for _ in range(2)]
                s_hilo2 = [mk_sb(ps, "hilo2", [1, 1024], BF16) for _ in range(2)]
                outc2 = [mk_sb(ps, "coc", [128, 4096], F32) for _ in range(2)]
                pb2 = [psum(ps, "pq", [128, 512], F32) for _ in range(2)]
                pnp = [(psum(ps, "pnA2", [128, 512], F32),
                        psum(ps, "pnB2", [128, 512], F32)) for _ in range(2)]
                pw2x = [psum(ps, "pw2", [128, 512], F32) for _ in range(2)]
                pn_g = {0: (0, 0), 1: (0, 0)}
                rec_g = {0: 0, 1: 0}
                pending_wo = [None]
                wo_g = [0]
                div_ve_g = [0]

                def flush_wo():
                    if pending_wo[0] is not None:
                        pending_wo[0]()
                        pending_wo[0] = None

                b.wait("vector", "ve")
                for pr in range(8):
                    b.op("vector", lambda v, pr=pr: v.tensor_copy(
                        kvbf[0:64, 65 * pr:65 * pr + 65],
                        kvsb[0:64, 130 * pr:130 * pr + 65]),
                        inc="ve" if pr == 7 else None)
                b.wait("gpsimd", "ve")
                b.dma("gpsimd",
                      kvbf[64:128, :].rearrange("p (pr c) -> p pr c", c=65),
                      kvsb.ap().rearrange("p (pr two c) -> p two pr c", two=2, c=65)[:, 1, :, :],
                      "ldw")
                b.wait("tensor", "ldw", b.cnt["ldw"])
                b.wait("vector", "ldw", b.cnt["ldw"])
                guard2 = {}
                wg2 = {}
                for tc in range(NCHUNK):
                    par = tc % 2
                    t0 = tc * CH
                    load_chunk(ar2, par, t0, tc, also_xn_sp=True)
                    b.wait("tensor", "ldx")
                    b.wait("tensor", "po", b.cnt["po"])

                    evq_ve = {}

                    def ev_q(eo, pp):
                        b.wait("vector", "pe")
                        b.op("vector", lambda v, pp=pp: v.tensor_scalar(
                            scrE2[:, 0:512], pp[:], 0.0, None, ALU.min), inc="ve")
                        b.wait("scalar", "ve")
                        b.op("scalar", lambda s: s.activation(
                            scrE2[:, 512:1024], scrE2[:, 0:512], AF.Exp), inc="ac")
                        b.wait("vector", "ac")
                        b.op("vector", lambda v, pp=pp, eo=eo: v.scalar_tensor_tensor(
                            scrB2[:, 512 * eo:512 * eo + 512], pp[:], 0.0,
                            scrE2[:, 512:1024], ALU.max, ALU.add), inc="ve")
                        evq_ve[eo] = b.cnt["ve"]

                    emit_proj_fm(W3, ar2["xn"][par], ev_q, pb2, guard2)
                    flush_wo()
                    pending_bc = [None]
                    for pr in range(8):
                        pnA2, pnB2 = pnp[pr % 2]
                        g_ac, g_ve = pn_g[pr % 2]
                        b.wait("tensor", "ve", max(g_ve, evq_ve[pr]))
                        b.wait("tensor", "ac", g_ac)
                        b.op("tensor", lambda t, pr=pr, pnA2=pnA2: t.matmul(
                            pnA2[0:65, :], kvbf[0:64, 65 * pr:65 * pr + 65],
                            scrB2[0:64, 512 * pr:512 * pr + 512],
                            start=True, stop=True), inc="pe")
                        b.op("tensor", lambda t, pr=pr, pnB2=pnB2: t.matmul(
                            pnB2[0:65, :], kvbf[64:128, 65 * pr:65 * pr + 65],
                            scrB2[64:128, 512 * pr:512 * pr + 512],
                            start=True, stop=True, tile_position=(64, 0)), inc="pe")
                        num_pe = b.cnt["pe"]
                        if pending_bc[0] is not None:
                            pending_bc[0]()
                        b.wait("scalar", "pe", num_pe)
                        for two, pn in ((0, pnA2), (1, pnB2)):
                            b.op("scalar", lambda s, pn=pn, tl=t_ln2[two]: s.activation(
                                tl[:], pn[64:65, :], AF.Ln, bias=ATTNB_EPS), inc="ac")
                        ln_ac = b.cnt["ac"]
                        for two in (0, 1):
                            b.wait("vector", "ac", ln_ac - 1 + two)
                            b.op("vector", lambda v, tl=t_ln2[two], sh=s_hilo2[two]:
                                 v.tensor_copy(sh[:, 0:512], tl[:]))
                            b.op("vector", lambda v, tl=t_ln2[two], sh=s_hilo2[two]:
                                 v.scalar_tensor_tensor(
                                     sh[:, 512:1024], sh[:, 0:512], -1.0, tl[:],
                                     ALU.mult, ALU.add), inc="ve")
                        hilo_ve = b.cnt["ve"]

                        def _bc(pr=pr, pnA2=pnA2, pnB2=pnB2, hilo_ve=hilo_ve):
                            for two, pn in ((0, pnA2), (1, pnB2)):
                                b.wait("tensor", "ve", hilo_ve - 1 + two)
                                b.op("tensor", lambda t, pn=pn, sh=s_hilo2[two]: t.matmul(
                                    pn[64:128, :], s_ones64b[:], sh[:, 0:512],
                                    start=True, stop=False))
                                b.op("tensor", lambda t, pn=pn, sh=s_hilo2[two]: t.matmul(
                                    pn[64:128, :], s_ones64b[:], sh[:, 512:1024],
                                    start=False, stop=True), inc="pe")
                            bc_pe = b.cnt["pe"]
                            for two, pn in ((0, pnA2), (1, pnB2)):
                                hh = 2 * pr + two
                                b.wait("scalar", "pe", bc_pe - 1 + two)
                                b.wait("scalar", "ve", rec_g[two])
                                b.op("scalar", lambda s, pn=pn, rc=rec64p[two]:
                                     s.activation(rc[:], pn[64:128, :], AF.Exp,
                                                  scale=-1.0), inc="ac")
                                b.wait("vector", "ac")
                                b.wait("vector", "pe", wo_g[0])
                                fo, ro = hh // 2, 64 * (hh % 2)
                                b.op("vector", lambda v, pn=pn, fo=fo, ro=ro,
                                     rc=rec64p[two]: v.tensor_mul(
                                         scrD2[ro:ro + 64, 512 * fo:512 * fo + 512],
                                         pn[0:64, :], rc[:]), inc="ve")
                                rec_g[two] = b.cnt["ve"]
                            pn_g[pr % 2] = (b.cnt["ac"], b.cnt["ve"])

                        pending_bc[0] = _bc
                    pending_bc[0]()
                    div_end_ve = b.cnt["ve"]

                    def _wo(par=par, t0=t0, tc=tc, div_end_ve=div_end_ve):
                        b.wait("tensor", "ve", div_end_ve)
                        b.wait_store("vector", ("coc", par))
                        for eo in range(8):
                            pp = pw2x[eo % 2]
                            for g in wg2.get(eo % 2, []):
                                b.wait("tensor", g[0], g[1])
                            for c in range(8):
                                b.op("tensor", lambda t, pp=pp, c=c, eo=eo: t.matmul(
                                    pp[:], wslot_ap(W4, c, eo), scrD2[:, 512 * c:512 * c + 512],
                                    start=(c == 0), stop=(c == 7)), inc="pe" if c == 7 else None)
                            b.wait("vector", "pe")
                            b.op("vector", lambda v, pp=pp, eo=eo, par=par: v.tensor_add(
                                outc2[par][:, 512 * eo:512 * eo + 512], pp[:],
                                ar2["xc"][par][:, 512 * eo:512 * eo + 512]), inc="ve")
                            wg2[eo % 2] = [("ve", b.cnt["ve"])]
                        wo_g[0] = b.cnt["pe"]
                        b.wait("scalar", "ve")
                        b.store("scalar", d_xT[:, :, t0:t0 + CH].rearrange("f p j -> p f j"),
                                outc2[par].ap().rearrange("p (f j) -> p f j", f=8),
                                key=("coc", par))
                        b.sdone[("xT", tc)] = b.cnt["stx"]
                        ar2["free_g"][par] = b.cnt["ve"]
                        div_ve_g[0] = div_end_ve

                    pending_wo[0] = _wo
                flush_wo()
            b.junction()

        def final_out():
            with contextlib.ExitStack() as ps:
                xc = [mk_sb(ps, "zxc", [128, 1024], F32) for _ in range(2)]
                ytok = [mk_sb(ps, "ytok", [128, 1024], F32) for _ in range(2)]
                p_tr = [psum(ps, "pft", [128, 128], F32) for _ in range(2)]
                fin_pe = {}
                for tt in range(NTT):
                    t0 = tt * 128
                    buf = xc[tt % 2]
                    b.wait_store("sync", ("xT", tt // 4))
                    b.wait("sync", "pe", fin_pe.get(tt % 2, 0))
                    b.dma("sync", buf.ap().rearrange("p (f j) -> p f j", f=8),
                          d_xT[:, :, t0:t0 + 128].rearrange("f p j -> p f j"), "ldx")
                    b.wait("tensor", "ldx")
                    b.wait_store("vector", ("ytok", tt % 2))
                    for f in range(NFT):
                        pb2 = p_tr[f % 2]
                        b.wait("tensor", "ve", b.cnt["ve"] - 1)
                        b.op("tensor", lambda t, pb2=pb2, buf=buf, f=f: t.transpose(
                            pb2[:], buf[:, 128 * f:128 * f + 128], s_identf[:]), inc="pe")
                        b.wait("vector", "pe")
                        b.op("vector", lambda v, pb2=pb2, f=f, tt=tt: v.tensor_copy(
                            ytok[tt % 2][:, 128 * f:128 * f + 128], pb2[:]), inc="ve")
                    fin_pe[tt % 2] = b.cnt["pe"]
                    b.wait("scalar", "ve")
                    b.store("scalar", d_y[t0:t0 + 128, :], ytok[tt % 2][:],
                            key=("ytok", tt % 2))
            b.wait("sync", "stx")

        po_ffn = [0]

        def _prefetch_ffn():
            load_weight(W0, "fr")
            load_weight(W1, "fk")
            load_weight(W2, "fv")
            po_ffn[0] = b.cnt["po"]

        if N_SL >= 1:
            sl_attn_a(prefetch_after_b=_prefetch_ffn if N_SL >= 2 else None)
        if N_SL >= 2:
            cbs = None
            if N_SL >= 3:
                cbs = {1: lambda: load_weight(W3, "bk"),
                       3: lambda: load_weight(W4, "bv", pe_guard=0)}
            sl_ffn("f1", po_ffn[0], chunk_cbs=cbs)
        if N_SL >= 3:
            sl_attn_b()
        if N_SL >= 4:
            sl_ffn("f2", po_ffn[0])
        final_out()

        with nc.Block() as blk:
            @blk.sync
            def _(e):
                for fn in b.bodies["sync"]:
                    fn(e)

            @blk.tensor
            def _(e):
                for fn in b.bodies["tensor"]:
                    fn(e)

            @blk.vector
            def _(e):
                for fn in b.bodies["vector"]:
                    fn(e)

            @blk.scalar
            def _(e):
                for fn in b.bodies["scalar"]:
                    fn(e)

            @blk.gpsimd
            def _(e):
                for fn in b.bodies["gpsimd"]:
                    fn(e)

    return nc


def kernel(L0_attn_a_x, L0_attn_a_Wq, L0_attn_a_Wk, L0_attn_a_Wv, L0_attn_a_Wo,
           L0_attn_b_Wq, L0_attn_b_Wk, L0_attn_b_Wv, L0_attn_b_Wo,
           L0_ffn_W_r, L0_ffn_W_k, L0_ffn_W_v, L0_ffn_mu_r, L0_ffn_mu_k):
    x = np.asarray(L0_attn_a_x, dtype=np.float32)
    ws = {
        "aq": L0_attn_a_Wq, "ak": L0_attn_a_Wk, "av": L0_attn_a_Wv, "ao": L0_attn_a_Wo,
        "bq": L0_attn_b_Wq, "bk": L0_attn_b_Wk, "bv": L0_attn_b_Wv, "bo": L0_attn_b_Wo,
        "fr": L0_ffn_W_r, "fk": L0_ffn_W_k, "fv": L0_ffn_W_v,
    }
    wT = {("w_" + k): np.ascontiguousarray(np.asarray(v, dtype=np.float32).T)
          for k, v in ws.items()}
    mu_r = np.ascontiguousarray(np.asarray(L0_ffn_mu_r, np.float32).reshape(8, 128).T)
    mu_k = np.ascontiguousarray(np.asarray(L0_ffn_mu_k, np.float32).reshape(8, 128).T)

    nc = build_program()
    in_maps = []
    for b_ in range(8):
        m = {"x": np.ascontiguousarray(x[b_])}
        m.update(wT)
        m["mu_r"] = mu_r
        m["mu_k"] = mu_k
        in_maps.append(m)
    res = run_bass_kernel_spmd(nc, in_maps, core_ids=list(range(8)))
    kernel._last_results = res.results
    out = np.stack([res.results[b_]["y"] for b_ in range(8)], axis=0)
    return out


# version 14
# speedup vs baseline: 1.0080x; 1.0080x over previous
import contextlib
import math
import numpy as np
import ml_dtypes
import concourse.bass as bass
import concourse.mybir as mybir
from concourse.bass_utils import run_bass_kernel_spmd

F32 = mybir.dt.float32
BF16 = mybir.dt.bfloat16
AF = mybir.ActivationFunctionType
ALU = mybir.AluOpType
BF = ml_dtypes.bfloat16

S = 4096
D = 1024
H = 16
HD = 64
NCHUNK = 8
CH = 512
NFT = 8
NTT = 32
SHIFT = 12.0
ALIBI_M = 20.0
LN_EPS = 1e-5
ATTNB_EPS = 1e-6
DEBUG_TAPS = False
N_SL = 4

SLOPES = [2.0 ** (-(i + 1) / 2.0) for i in range(H)]
CUTOFF = [ALIBI_M / s for s in SLOPES]


def kept_ks_range(h, qi):
    cap = int(math.floor((ALIBI_M + 20.0) / SLOPES[h] / 128.0))
    ks_hi = min(4 * qi + 3, cap, NTT - 1)
    return 0, ks_hi


def exact_split3(v):
    hi = 4.0 * np.floor(v / 4.0)
    rem = v - hi
    mid = 0.125 * np.floor(rem / 0.125)
    lo = rem - mid
    return hi, mid, lo


class Builder:
    ENGS = ("sync", "tensor", "vector", "scalar", "gpsimd")

    def __init__(self, nc):
        self.nc = nc
        self.bodies = {e: [] for e in self.ENGS}
        self.sems = {}
        self.cnt = {}
        self.sdone = {}

    def sem(self, stack, name):
        self.sems[name] = stack.enter_context(self.nc.semaphore(name))
        self.cnt[name] = 0

    def emit(self, eng, fn):
        self.bodies[eng].append(fn)

    def wait(self, eng, semname, val=None):
        v = self.cnt[semname] if val is None else val
        if v <= 0:
            return
        s = self.sems[semname]
        self.emit(eng, lambda e, s=s, v=v: e.wait_ge(s, v))

    def inc(self, semname, n=1):
        self.cnt[semname] += n
        return self.sems[semname]

    def op(self, eng, fn, inc=None, incn=1):
        if inc is not None:
            s = self.inc(inc, incn)
            self.emit(eng, lambda e, fn=fn, s=s, n=incn: fn(e).then_inc(s, n))
        else:
            self.emit(eng, fn)

    def dma(self, eng, out, in_, inc):
        s = self.inc(inc, 16)
        self.emit(eng, lambda e, o=out, i=in_, s=s: e.dma_start(out=o, in_=i).then_inc(s, 16))

    def dma_count(self, n):
        return 16 * n

    def store(self, eng, out, in_, key=None):
        self.dma(eng, out, in_, "stx")
        if key is not None:
            self.sdone[key] = self.cnt["stx"]

    def wait_store(self, eng, key):
        if key in self.sdone:
            self.wait(eng, "stx", self.sdone[key])

    def junction(self):
        for eng in self.ENGS:
            for sname in ("pe", "ve", "ac", "po", "stx"):
                self.wait(eng, sname)


def build_program():
    nc = bass.Bass("TRN2", target_bir_lowering=False)

    d_x = nc.dram_tensor("x", [S, D], F32, kind="ExternalInput").ap()
    d_y = nc.dram_tensor("y", [S, D], F32, kind="ExternalOutput").ap()
    wnames = ["aq", "ak", "av", "ao", "bq", "bk", "bv", "bo", "fr", "fk", "fv"]
    d_w = {n: nc.dram_tensor("w_" + n, [D, D], F32, kind="ExternalInput").ap()
           for n in wnames}
    d_mur = nc.dram_tensor("mu_r", [128, 8], F32, kind="ExternalInput").ap()
    d_muk = nc.dram_tensor("mu_k", [128, 8], F32, kind="ExternalInput").ap()

    d_xT = nc.dram_tensor("xT_scr", [NFT, 128, S], F32, kind="Internal").ap()
    d_Qsp = nc.dram_tensor("Q_sp", [H, HD, S], BF16, kind="Internal").ap()
    d_Ksp = nc.dram_tensor("K_sp", [H, HD, S], BF16, kind="Internal").ap()
    d_Vsp = nc.dram_tensor("V_sp", [H, S, HD], BF16, kind="Internal").ap()
    d_xnsp = nc.dram_tensor("xn_sp", [NFT, 128, S], BF16, kind="Internal").ap()
    d_dbg = {}
    if DEBUG_TAPS:
        d_dbg["dbg_Q"] = nc.dram_tensor("dbg_Q", [H, HD, S], BF16, kind="ExternalOutput").ap()
        d_dbg["dbg_K"] = nc.dram_tensor("dbg_K", [H, HD, S], BF16, kind="ExternalOutput").ap()
        d_dbg["dbg_V"] = nc.dram_tensor("dbg_V", [H, S, HD], BF16, kind="ExternalOutput").ap()
        d_dbg["dbg_ao"] = nc.dram_tensor("dbg_ao", [128, 8 * 4096], BF16, kind="ExternalOutput").ap()
        d_dbg["dbg_xnsp"] = nc.dram_tensor("dbg_xnsp", [NFT, 128, S], BF16, kind="ExternalOutput").ap()
        d_dbg["dbg_kv"] = nc.dram_tensor("dbg_kv", [128, 8 * 65], F32, kind="ExternalOutput").ap()
        d_dbg["dbg_ktok"] = nc.dram_tensor("dbg_ktok", [128, 4096], BF16, kind="ExternalOutput").ap()
        d_dbg["dbg_vaug"] = nc.dram_tensor("dbg_vaug", [128, 4352], BF16, kind="ExternalOutput").ap()
        d_dbg["dbg_praw"] = nc.dram_tensor("dbg_praw", [128, 512], F32, kind="ExternalOutput").ap()
        d_dbg["dbg_w1"] = nc.dram_tensor("dbg_w1", [128, 8192], BF16, kind="ExternalOutput").ap()
        d_dbg["dbg_xnc7"] = nc.dram_tensor("dbg_xnc7", [128, 4096], BF16, kind="ExternalOutput").ap()

    ident_f = nc.inline_tensor(np.eye(128, dtype=np.float32), "ident_f").ap()
    ones_col_b = nc.inline_tensor(np.ones((128, 1), dtype=np.float32).astype(BF), "ones_col_b").ap()
    ones_1x128_f = nc.inline_tensor(np.ones((1, 128), dtype=np.float32), "ones_1x128_f").ap()
    ones_1x64_f = nc.inline_tensor(np.ones((1, 64), dtype=np.float32), "ones_1x64_f").ap()
    ones_1x64_b = nc.inline_tensor(np.ones((1, 64), dtype=np.float32).astype(BF), "ones_1x64_b").ap()
    ones3_b = nc.inline_tensor(np.ones((3, S), dtype=np.float32).astype(BF), "ones3_b").ap()
    tri_np = np.where(np.arange(128)[:, None] > np.arange(128)[None, :], -1e9, 0.0).astype(np.float32)
    tri_m = nc.inline_tensor(tri_np, "tri_m").ap()

    kal = np.zeros((H, 3, S), dtype=np.float64)
    for h in range(H):
        vk = SLOPES[h] * (np.arange(S) % 128).astype(np.float64)
        hi, mid, lo = exact_split3(vk)
        kal[h, 0], kal[h, 1], kal[h, 2] = -hi, -mid, -lo
    d_kal = nc.inline_tensor(kal.astype(np.float32).astype(BF), "kal").ap()

    fvals_raw = [0.0, 1.0, -1.0, 1.0 / D, LN_EPS, 0.125, -SHIFT, ATTNB_EPS]
    for h in range(H):
        for qi in range(NCHUNK):
            ks_lo, ks_hi = kept_ks_range(h, qi)
            for ks in range(ks_lo, ks_hi + 1):
                fvals_raw.append(-SLOPES[h] * 128 * ks - SHIFT)
    fidx = {}
    fcols = []
    for v in fvals_raw:
        v32 = np.float32(v).item()
        if v32 not in fidx:
            fidx[v32] = len(fcols)
            fcols.append(v32)
        fidx[v] = fidx[v32]
    d_fpool = nc.inline_tensor(
        np.tile(np.array([fcols], dtype=np.float32), (128, 1)), "fpool").ap()

    uid = [0]

    with contextlib.ExitStack() as st:
        b = Builder(nc)
        for s in ("ldx", "ldw", "ldh", "ldm", "stx", "pe", "ve", "ac", "po"):
            b.sem(st, s)

        def mk_sb(stack, name, shape, dt):
            uid[0] += 1
            return stack.enter_context(nc.sbuf_tensor(f"{name}_{uid[0]}", shape, dt))

        s_fpool = mk_sb(st, "fpool", [128, len(fcols)], F32)
        s_identf = mk_sb(st, "identf", [128, 128], F32)
        s_onescol = mk_sb(st, "onescol", [128, 1], BF16)
        s_ones128 = mk_sb(st, "ones128", [1, 128], F32)
        s_ones64 = mk_sb(st, "ones64", [1, 64], F32)
        s_ones64b = mk_sb(st, "ones64b", [1, 64], BF16)
        s_tri = mk_sb(st, "tri", [128, 128], F32)
        s_mur = mk_sb(st, "mur", [128, 8], F32)
        s_muk = mk_sb(st, "muk", [128, 8], F32)
        W0 = mk_sb(st, "W0", [128, 8192], BF16)
        W1 = mk_sb(st, "W1", [128, 8192], BF16)
        W2 = mk_sb(st, "W2", [128, 8192], BF16)
        W3 = mk_sb(st, "W3", [128, 8192], BF16)
        W4 = mk_sb(st, "W4", [128, 8192], BF16)
        kvsb = mk_sb(st, "kvsb", [64, 16 * 65], F32)
        kvbf = mk_sb(st, "kvbf", [128, 8 * 65], BF16)

        for v, i in fidx.items():
            nc.const_aps.register(F32, v, s_fpool[:, i:i + 1])

        def fc(v):
            return s_fpool[:, fidx[v]:fidx[v] + 1]

        b.dma("sync", s_fpool[:], d_fpool, "ldm")
        b.dma("sync", s_identf[:], ident_f, "ldm")
        b.dma("sync", s_onescol[:], ones_col_b, "ldm")
        b.dma("sync", s_ones128[:], ones_1x128_f, "ldm")
        b.dma("sync", s_ones64[:], ones_1x64_f, "ldm")
        b.dma("sync", s_ones64b[:], ones_1x64_b, "ldm")
        b.dma("sync", s_tri[:], tri_m, "ldm")
        b.dma("sync", s_mur[:], d_mur, "ldm")
        b.dma("sync", s_muk[:], d_muk, "ldm")
        for eng in ("tensor", "vector", "scalar", "gpsimd"):
            b.wait(eng, "ldm")

        def psum(stack2, name, shape, dt=F32):
            uid[0] += 1
            return stack2.enter_context(nc.psum_tensor(f"{name}_{uid[0]}", shape, dt))

        def load_weight(slot, wname, pe_guard=None):
            b.wait("gpsimd", "pe", b.cnt["pe"] if pe_guard is None else pe_guard)
            b.dma("gpsimd", slot.ap().rearrange("p (c d) -> p c d", c=8),
                  d_w[wname].rearrange("(c p) d -> p c d", p=128), "po", )

        def wslot_ap(slot, c, eo):
            return slot[:, 1024 * c + 128 * eo: 1024 * c + 128 * eo + 128]

        with contextlib.ExitStack() as ps:
            xc = [mk_sb(ps, "p0xc", [128, 1024], F32) for _ in range(2)]
            outc = [mk_sb(ps, "p0oc", [128, 1024], F32) for _ in range(2)]
            p_tr = [psum(ps, "ptr", [128, 128], F32) for _ in range(2)]
            p0_pe = {}
            for tt in range(NTT):
                t0 = tt * 128
                buf = xc[tt % 2]
                b.wait("sync", "pe", p0_pe.get(tt % 2, 0))
                b.dma("sync", buf[:], d_x[t0:t0 + 128, :], "ldx")
                b.wait("tensor", "ldx")
                b.wait_store("vector", ("p0oc", tt % 2))
                for f in range(NFT):
                    pb = p_tr[f % 2]
                    b.wait("tensor", "ve", b.cnt["ve"] - 1)
                    b.op("tensor", lambda t, pb=pb, buf=buf, f=f: t.transpose(
                        pb[:], buf[:, 128 * f:128 * f + 128], s_identf[:]), inc="pe")
                    b.wait("vector", "pe")
                    b.op("vector", lambda v, pb=pb, f=f, tt=tt: v.tensor_copy(
                        outc[tt % 2][:, 128 * f:128 * f + 128], pb[:]), inc="ve")
                p0_pe[tt % 2] = b.cnt["pe"]
                b.wait("scalar", "ve")
                b.store("scalar",
                        d_xT[:, :, t0:t0 + 128].rearrange("f p t -> p f t"),
                        outc[tt % 2].ap().rearrange("p (f t) -> p f t", f=8),
                        key=("p0oc", tt % 2))
                if tt % 4 == 3:
                    b.sdone[("xT", tt // 4)] = b.cnt["stx"]
        b.junction()

        def emit_ln(ar, par, xn_dest=None, xn_store_key=None):
            xcb, xbb = ar["xc"][par], ar["xb"][par]
            ln_row = ar["lnrow"]
            if xn_dest is None:
                xnb = ar["xn"][par]
                xn_dest = [xnb[:, 512 * f:512 * f + 512] for f in range(NFT)]
            with contextlib.ExitStack() as ps:
                p_s = psum(ps, "lns", [1, 512], F32)[:]
                p_q = psum(ps, "lnq", [1, 512], F32)[:]
                p_m = psum(ps, "lnm", [128, 512], F32)
                p_r = psum(ps, "lnr", [128, 512], F32)
                xg = ar.setdefault("xb_g", {})
                b.wait("gpsimd", "pe", xg.get(par, (0, 0))[0])
                b.wait("gpsimd", "ve", xg.get(par, (0, 0))[1])
                b.wait("gpsimd", "pe", ar.setdefault("sq_g", [0])[0])
                b.wait("gpsimd", "ldx")
                b.op("gpsimd", lambda g: g.tensor_copy(xbb[:], xcb[:]), inc="po")
                b.op("gpsimd", lambda g: g.tensor_mul(ar["sq"][:], xbb[:], xbb[:]), inc="po")
                b.wait("tensor", "po")
                for f in range(NFT):
                    b.op("tensor", lambda t, f=f: t.matmul(
                        p_s, s_onescol[:], xbb[:, 512 * f:512 * f + 512],
                        start=(f == 0), stop=(f == 7)), inc="pe" if f == 7 else None)
                for f in range(NFT):
                    b.op("tensor", lambda t, f=f: t.matmul(
                        p_q, s_onescol[:], ar["sq"][:, 512 * f:512 * f + 512],
                        start=(f == 0), stop=(f == 7)), inc="pe" if f == 7 else None)
                ar["sq_g"][0] = b.cnt["pe"]
                b.wait("vector", "pe")
                b.op("vector", lambda v: v.tensor_scalar(
                    ln_row[:, 512:1024], p_s, 1.0 / D, None, ALU.mult))
                b.op("vector", lambda v: v.tensor_mul(
                    ln_row[:, 1024:1536], ln_row[:, 512:1024], ln_row[:, 512:1024]))
                b.op("vector", lambda v: v.scalar_tensor_tensor(
                    ln_row[:, 0:512], ln_row[:, 1024:1536], -float(D), p_q,
                    ALU.mult, ALU.add), inc="ve")
                b.wait("scalar", "ve")
                b.op("scalar", lambda s: s.activation(
                    ln_row[:, 0:512], ln_row[:, 0:512], AF.Ln,
                    bias=LN_EPS, scale=1.0 / D))
                b.op("scalar", lambda s: s.activation(
                    ln_row[:, 0:512], ln_row[:, 0:512], AF.Exp, scale=-0.5), inc="ac")
                b.wait("tensor", "ac")
                b.op("tensor", lambda t: t.matmul(p_m[:], s_ones128[:], ln_row[:, 512:1024],
                                                  start=True, stop=True))
                b.op("tensor", lambda t: t.matmul(p_r[:], s_ones128[:], ln_row[:, 0:512],
                                                  start=True, stop=True), inc="pe")
                b.wait("vector", "pe", ar.setdefault("xn_pe_g", {}).get(par, 0))
                b.wait("vector", "pe")
                b.wait("vector", "po")
                if xn_store_key is not None:
                    b.wait_store("vector", xn_store_key)
                for f in range(NFT):
                    dst = xn_dest[f]
                    b.op("vector", lambda v, dst=dst, f=f: v.tensor_sub(
                        dst, xbb[:, 512 * f:512 * f + 512], p_m[:]))
                    b.op("vector", lambda v, dst=dst: v.tensor_mul(
                        dst, dst, p_r[:]), inc="ve" if f == 7 else None)
                ar["xb_g"][par] = (b.cnt["pe"], b.cnt["ve"])
            ar.setdefault("xn_ready", {})[par] = b.cnt["ve"]

        def emit_proj_fm(slot, rhs_buf, evac_fn, pbanks, guard):
            nb = len(pbanks)
            for eo in range(8):
                pp = pbanks[eo % nb]
                for g in guard.get(eo % nb, []):
                    b.wait("tensor", g[0], g[1])
                for c in range(8):
                    b.op("tensor", lambda t, pp=pp, c=c, eo=eo: t.matmul(
                        pp[:], wslot_ap(slot, c, eo),
                        rhs_buf[:, 512 * c:512 * c + 512],
                        start=(c == 0), stop=(c == 7)), inc="pe" if c == 7 else None)
                evac_fn(eo, pp)
                guard[eo % nb] = [("ve", b.cnt["ve"]), ("ac", b.cnt["ac"])]

        def load_chunk(ar, par, t0, tc, also_xn_sp=False):
            b.wait_store("sync", ("xT", tc))
            fg = ar.setdefault("free_g", {})
            b.wait("sync", "ve", fg.get(par, 0))
            if "name" in ar:
                b.wait_store("sync", (ar["name"], "xc", par))
            if also_xn_sp:
                b.wait_store("sync", ("bxn", tc))
                b.dma("sync", ar["xn"][par].ap().rearrange("p (f j) -> p f j", f=8),
                      d_xnsp[:, :, t0:t0 + CH].rearrange("f p j -> p f j"), "ldx")
            b.dma("sync", ar["xc"][par].ap().rearrange("p (f j) -> p f j", f=8),
                  d_xT[:, :, t0:t0 + CH].rearrange("f p j -> p f j"), "ldx")

        def sl_attn_a(prefetch_after_b=None):
            load_weight(W0, "aq", pe_guard=0)
            load_weight(W1, "ak", pe_guard=0)
            load_weight(W2, "av", pe_guard=0)
            load_weight(W3, "ao", pe_guard=0)
            with contextlib.ExitStack() as ps:
                scrB = mk_sb(ps, "ascrB", [128, 4096], BF16)
                scrC = mk_sb(ps, "ascrC", [128, 4096], BF16)
                scrD = mk_sb(ps, "ascrD", [128, 4096], BF16)
                ar = {
                    "xc": [mk_sb(ps, "axc", [128, 4096], F32) for _ in range(2)],
                    "xb": [mk_sb(ps, "axb", [128, 4096], BF16) for _ in range(2)],
                    "xn": [mk_sb(ps, "axn", [128, 4096], BF16) for _ in range(2)],
                    "sq": mk_sb(ps, "asq", [128, 4096], BF16),
                    "lnrow": mk_sb(ps, "alnr", [1, 1536], F32),
                }
                pb = [psum(ps, "pa", [128, 512], F32) for _ in range(2)]
                pv = [psum(ps, "pv", [128, 512], F32) for _ in range(2)]
                guard = {}
                vguard = {}
                load_chunk(ar, 0, 0, 0)
                emit_ln(ar, 0)
                for tc in range(NCHUNK):
                    par = tc % 2
                    t0 = tc * CH
                    b.wait("tensor", "po", b.cnt["po"])
                    b.wait("tensor", "ve", ar["xn_ready"][par])
                    b.wait_store("scalar", "aQK")

                    def ev_q(eo, pp):
                        b.wait("scalar", "pe")
                        b.op("scalar", lambda s, pp=pp, eo=eo: s.activation(
                            scrB[:, 512 * eo:512 * eo + 512], pp[:], AF.Copy,
                            scale=0.125), inc="ac")

                    def ev_k(eo, pp):
                        b.wait("scalar", "pe")
                        b.op("scalar", lambda s, pp=pp, eo=eo: s.activation(
                            scrC[:, 512 * eo:512 * eo + 512], pp[:], AF.Copy), inc="ac")

                    emit_proj_fm(W0, ar["xn"][par], ev_q, pb, guard)
                    if tc < NCHUNK - 1:
                        load_chunk(ar, 1 - par, t0 + CH, tc + 1)
                        emit_ln(ar, 1 - par)
                    emit_proj_fm(W1, ar["xn"][par], ev_k, pb, guard)
                    b.wait("scalar", "ac")
                    b.store("scalar",
                            d_Qsp.rearrange("(e two) hd t -> e (two hd) t", two=2)[:, :, t0:t0 + CH].rearrange("e p t -> p e t"),
                            scrB.ap().rearrange("p (e t) -> p e t", e=8))
                    b.store("scalar",
                            d_Ksp.rearrange("(e two) hd t -> e (two hd) t", two=2)[:, :, t0:t0 + CH].rearrange("e p t -> p e t"),
                            scrC.ap().rearrange("p (e t) -> p e t", e=8), key="aQK")
                    b.wait_store("vector", "aV")
                    for eo in range(2):
                        for ts in range(4):
                            bi = (eo * 4 + ts) % 2
                            pp = pv[bi]
                            for g in vguard.get(bi, []):
                                b.wait("tensor", g[0], g[1])
                            for c in range(8):
                                b.op("tensor", lambda t, pp=pp, c=c, ts=ts, eo=eo, par=par: t.matmul(
                                    pp[:], ar["xn"][par][:, 512 * c + 128 * ts:512 * c + 128 * ts + 128],
                                    W2[:, 1024 * c + 512 * eo:1024 * c + 512 * eo + 512],
                                    start=(c == 0), stop=(c == 7)), inc="pe" if c == 7 else None)
                            b.wait("vector", "pe")
                            b.op("vector", lambda v, pp=pp, ts=ts, eo=eo: v.tensor_copy(
                                scrD[:, 1024 * ts + 512 * eo:1024 * ts + 512 * eo + 512], pp[:]),
                                inc="ve")
                            vguard[bi] = [("ve", b.cnt["ve"])]
                    b.wait("scalar", "ve")
                    for ts in range(4):
                        b.store("scalar",
                                d_Vsp[:, t0 + 128 * ts:t0 + 128 * ts + 128, :].rearrange("h p hd -> p h hd"),
                                scrD[:, 1024 * ts:1024 * ts + 1024].rearrange("p (h hd) -> p h hd", h=16),
                                key="aV" if ts == 3 else None)
                    ar["free_g"][par] = b.cnt["ve"]
                    ar["xn_pe_g"][par] = b.cnt["pe"]
            po_own = b.cnt["po"]
            if prefetch_after_b:
                prefetch_after_b()
            if DEBUG_TAPS:
                b.wait("sync", "stx", b.cnt["stx"])
                b.dma("sync", d_dbg["dbg_Q"], d_Qsp, "stx")
                b.dma("sync", d_dbg["dbg_K"], d_Ksp, "stx")
                b.dma("sync", d_dbg["dbg_V"], d_Vsp, "stx")
            b.junction()
            with contextlib.ExitStack() as outer:
                aoT = mk_sb(outer, "aoT", [128, 8 * 4096], BF16)
                with contextlib.ExitStack() as ps:
                    Kbuf = [mk_sb(ps, "Kb", [128, S], BF16) for _ in range(2)]
                    Qbuf = [mk_sb(ps, "Qb", [128, S], BF16) for _ in range(2)]
                    Vbuf = [mk_sb(ps, "Vb", [128, 32 * 66], BF16) for _ in range(2)]
                    Pbuf = [mk_sb(ps, "Pb", [128, 512], BF16) for _ in range(3)]
                    rec_sb = mk_sb(ps, "recsb", [1, 512], F32)
                    rec64 = mk_sb(ps, "rec64", [64, 512], F32)
                    s_hilo = mk_sb(ps, "hilo", [1, 1024], BF16)
                    p_s2 = [psum(ps, "ps", [128, 512], F32) for _ in range(3)]
                    p_av = [psum(ps, "pav", [65, 512], F32) for _ in range(2)]
                    p_rb = psum(ps, "prb", [64, 512], F32)
                    for bb in range(2):
                        b.op("vector", lambda v, bb=bb: v.memset(Vbuf[bb][:], 1.0))
                        b.op("vector", lambda v, bb=bb: v.memset(Kbuf[bb][64:128, :], 0.0))
                        b.op("vector", lambda v, bb=bb: v.memset(Qbuf[bb][64:128, :], 0.0), inc="ve")
                    b.wait("sync", "ve", b.cnt["ve"])
                    for bb in range(2):
                        b.dma("sync", Qbuf[bb][64:67, :], ones3_b, "ldm")
                    b.wait("tensor", "ldm", b.cnt["ldm"])
                    last_head_pe = {}
                    sbank_g = {0: 0, 1: 0, 2: 0}
                    pbuf_g = {0: 0, 1: 0, 2: 0}
                    pav_g = {0: 0, 1: 0}
                    prb_g = [0]
                    pending_div = [None]

                    def flush_div():
                        if pending_div[0] is not None:
                            pending_div[0]()
                            pending_div[0] = None
                    for h in range(H):
                        hb = h % 2
                        if h >= 2:
                            b.wait("sync", "pe", last_head_pe[h - 2])
                        b.dma("sync", Kbuf[hb][0:64, :], d_Ksp[h], "ldh")
                        b.dma("sync", Kbuf[hb][64:67, :], d_kal[h], "ldh")
                        b.dma("sync", Qbuf[hb][0:64, :], d_Qsp[h], "ldh")
                        b.dma("sync", Vbuf[hb].ap().rearrange("p (ts c) -> p ts c", c=66)[:, :, 0:64],
                              d_Vsp[h].rearrange("(ts p) hd -> p ts hd", p=128), "ldh")
                        b.wait("tensor", "ldh")
                        for qi in range(NCHUNK):
                            q0 = qi * CH
                            ks_lo, ks_hi = kept_ks_range(h, qi)
                            ks_list = list(range(ks_lo, ks_hi + 1))
                            nks = len(ks_list)
                            pav = p_av[qi % 2]
                            b.wait("tensor", "ve", pav_g[qi % 2])
                            exp_at = {}
                            av_emitted = [0]

                            def emit_av(i, ks, first, last, pav=pav, hb=hb):
                                co = max(0, 128 * ks - q0)
                                b.wait("tensor", "ac", exp_at[i])
                                b.op("tensor", lambda t, pav=pav, i=i, ks=ks, co=co, hb=hb,
                                     first=first, last=last: t.matmul(
                                    pav[0:65, co:512], Vbuf[hb][:, 66 * ks:66 * ks + 65],
                                    Pbuf[i % 3][:, co:512],
                                    start=first, stop=last), inc="pe")
                                pbuf_g[i % 3] = b.cnt["pe"]

                            for i, ks in enumerate(ks_list):
                                k0 = ks * 128
                                co = max(0, k0 - q0)
                                sp = p_s2[i % 3]
                                diag = (k0 >= q0)
                                b.wait("tensor", "ac", sbank_g[i % 3])
                                b.op("tensor", lambda t, sp=sp, ks=ks, co=co, q0=q0, hb=hb: t.matmul(
                                    sp[:, co:512], Kbuf[hb][0:128, 128 * ks:128 * ks + 128],
                                    Qbuf[hb][0:128, q0 + co:q0 + 512],
                                    start=True, stop=True), inc="pe")
                                s_pe = b.cnt["pe"]
                                if diag:
                                    b.wait("vector", "pe", s_pe)
                                    b.op("vector", lambda v, sp=sp, co=co: v.tensor_add(
                                        sp[:, co:co + 128], sp[:, co:co + 128], s_tri[:]), inc="ve")
                                    b.wait("scalar", "ve")
                                else:
                                    b.wait("scalar", "pe", s_pe)
                                b.wait("scalar", "pe", pbuf_g[i % 3])
                                bias = np.float32(-SLOPES[h] * 128 * ks - SHIFT).item()
                                b.op("scalar", lambda s, sp=sp, co=co, i=i, bias=bias: s.activation(
                                    Pbuf[i % 3][:, co:512], sp[:, co:512], AF.Exp, bias=fc(bias)),
                                    inc="ac")
                                exp_at[i] = b.cnt["ac"]
                                sbank_g[i % 3] = b.cnt["ac"]
                                if i == min(2, nks - 1):
                                    flush_div()
                                if i >= 1:
                                    emit_av(i - 1, ks_list[i - 1], i - 1 == 0, False)
                            emit_av(nks - 1, ks_list[nks - 1], nks == 1, True)
                            av_pe = b.cnt["pe"]

                            def _div(pav=pav, av_pe=av_pe, prb_tgt=prb_g[0],
                                     h=h, q0=q0, qparity=qi % 2):
                                b.wait("scalar", "pe", av_pe)
                                b.op("scalar", lambda s, pav=pav: s.activation(
                                    rec_sb[:], pav[64:65, :], AF.Ln))
                                b.op("scalar", lambda s: s.activation(
                                    rec_sb[:], rec_sb[:], AF.Exp, scale=-1.0), inc="ac")
                                div_ac = b.cnt["ac"]
                                b.wait("vector", "ac", div_ac)
                                b.op("vector", lambda v: v.tensor_copy(
                                    s_hilo[:, 0:512], rec_sb[:]))
                                b.op("vector", lambda v: v.scalar_tensor_tensor(
                                    s_hilo[:, 512:1024], s_hilo[:, 0:512], -1.0, rec_sb[:],
                                    ALU.mult, ALU.add), inc="ve")
                                hilo_ve = b.cnt["ve"]
                                b.wait("tensor", "ve", max(hilo_ve, prb_tgt))
                                b.op("tensor", lambda t: t.matmul(
                                    p_rb[:], s_ones64b[:], s_hilo[:, 0:512],
                                    start=True, stop=False))
                                b.op("tensor", lambda t: t.matmul(
                                    p_rb[:], s_ones64b[:], s_hilo[:, 512:1024],
                                    start=False, stop=True), inc="pe")
                                b.wait("vector", "pe")
                                b.op("vector", lambda v: v.tensor_copy(rec64[:], p_rb[:]))
                                fo, ro = h // 2, 64 * (h % 2)
                                b.op("vector", lambda v, pav=pav, fo=fo, ro=ro, q0=q0: v.tensor_mul(
                                    aoT[ro:ro + 64, 4096 * fo + q0:4096 * fo + q0 + 512],
                                    pav[0:64, :], rec64[:]), inc="ve")
                                prb_g[0] = b.cnt["ve"]
                                pav_g[qparity] = b.cnt["ve"]

                            pending_div[0] = _div
                        last_head_pe[h] = b.cnt["pe"]
                    flush_div()
                b.junction()
                if DEBUG_TAPS:
                    b.dma("sync", d_dbg["dbg_ao"], aoT[:], "stx")
                with contextlib.ExitStack() as ps:
                    xc2 = [mk_sb(ps, "wxc", [128, 4096], F32) for _ in range(2)]
                    pw = [psum(ps, "pw", [128, 512], F32) for _ in range(2)]
                    wg = {}
                    for tc in range(NCHUNK):
                        par = tc % 2
                        t0 = tc * CH
                        b.wait_store("sync", ("wxc", par))
                        b.wait_store("sync", ("xT", tc))
                        b.dma("sync", xc2[par].ap().rearrange("p (f j) -> p f j", f=8),
                              d_xT[:, :, t0:t0 + CH].rearrange("f p j -> p f j"), "ldx")
                        b.wait("tensor", "ldx")
                        b.wait("tensor", "po", po_own)
                        for eo in range(8):
                            pp = pw[eo % 2]
                            for g in wg.get(eo % 2, []):
                                b.wait("tensor", g[0], g[1])
                            for c in range(8):
                                b.op("tensor", lambda t, pp=pp, c=c, eo=eo, t0=t0: t.matmul(
                                    pp[:], wslot_ap(W3, c, eo),
                                    aoT[:, 4096 * c + t0:4096 * c + t0 + 512],
                                    start=(c == 0), stop=(c == 7)), inc="pe" if c == 7 else None)
                            b.wait("vector", "pe")
                            b.op("vector", lambda v, pp=pp, eo=eo, par=par: v.tensor_add(
                                xc2[par][:, 512 * eo:512 * eo + 512], pp[:],
                                xc2[par][:, 512 * eo:512 * eo + 512]), inc="ve")
                            wg[eo % 2] = [("ve", b.cnt["ve"])]
                        b.wait("scalar", "ve")
                        b.store("scalar", d_xT[:, :, t0:t0 + CH].rearrange("f p j -> p f j"),
                                xc2[par].ap().rearrange("p (f j) -> p f j", f=8),
                                key=("wxc", par))
                        b.sdone[("xT", tc)] = b.cnt["stx"]
            b.junction()

        def sl_ffn(tag, po_guard, chunk_cbs=None):
            Wr_, Wkf_, Wvf_ = W0, W1, W2
            with contextlib.ExitStack() as ps:
                xnsh2 = [mk_sb(ps, "fxnsh", [128, 8 * 520], BF16) for _ in range(2)]
                scrB = mk_sb(ps, "fscrB", [128, 4096], BF16)
                scrC2 = [mk_sb(ps, "fscrC", [128, 4096], BF16) for _ in range(2)]
                scrD = mk_sb(ps, "fscrD", [128, 4096], BF16)
                ar = {
                    "xc": [mk_sb(ps, "fxc", [128, 4096], F32) for _ in range(2)],
                    "xb": [mk_sb(ps, "fxb", [128, 4096], BF16) for _ in range(2)],
                    "sq": mk_sb(ps, "fsq", [128, 4096], BF16),
                    "lnrow": mk_sb(ps, "flnr", [1, 1536], F32),
                    "name": "f" + tag,
                }
                tmp512 = mk_sb(ps, "ftmp", [128, 1024], F32)
                pb = [psum(ps, "pf", [128, 512], F32) for _ in range(3)]
                xnd2 = [[xnsh2[p][:, 520 * f + 1:520 * f + 513] for f in range(NFT)]
                        for p in range(2)]
                mix_done = {}
                mixb_done = {}
                shift_po = {}
                evk_pe = {0: 0, 1: 0}
                tmp_po_g = {0: 0, 1: 0}
                guard = {}

                def emit_shift(npar):
                    dbuf, xnsh = ar["xb"][npar], xnsh2[npar]
                    b.wait("gpsimd", "ve")
                    b.wait("gpsimd", "pe", ar["sq_g"][0])
                    for f in range(NFT):
                        b.op("gpsimd", lambda g, dbuf=dbuf, xnsh=xnsh, f=f: g.tensor_sub(
                            dbuf[:, 512 * f:512 * f + 512],
                            xnsh[:, 520 * f + 1:520 * f + 513], xnsh[:, 520 * f:520 * f + 512]),
                            inc="po" if f == 7 else None)
                    shift_po[npar] = b.cnt["po"]

                def emit_mix_c(npar, scrC_pe_guard):
                    dbuf, xnsh = ar["xb"][npar], xnsh2[npar]
                    b.wait("vector", "po", shift_po[npar])
                    b.wait("vector", "pe", scrC_pe_guard)
                    for f in range(NFT):
                        b.op("vector", lambda v, dbuf=dbuf, xnsh=xnsh, f=f, npar=npar: v.scalar_tensor_tensor(
                            scrC2[npar][:, 512 * f:512 * f + 512], dbuf[:, 512 * f:512 * f + 512],
                            s_muk[:, f:f + 1], xnsh[:, 520 * f:520 * f + 512], ALU.mult, ALU.add),
                            inc="ve" if f == 7 else None)
                    mix_done[npar] = b.cnt["ve"]

                def emit_mix_b(npar, scrB_pe_guard):
                    dbuf, xnsh = ar["xb"][npar], xnsh2[npar]
                    b.wait("vector", "po", shift_po[npar])
                    b.wait("vector", "pe", scrB_pe_guard)
                    for f in range(NFT):
                        b.op("vector", lambda v, dbuf=dbuf, xnsh=xnsh, f=f: v.scalar_tensor_tensor(
                            scrB[:, 512 * f:512 * f + 512], dbuf[:, 512 * f:512 * f + 512],
                            s_mur[:, f:f + 1], xnsh[:, 520 * f:520 * f + 512], ALU.mult, ALU.add),
                            inc="ve" if f == 7 else None)
                    ar["xb_g"][npar] = (b.cnt["pe"], b.cnt["ve"])
                    mixb_done[npar] = b.cnt["ve"]

                for f in range(NFT):
                    b.op("vector", lambda v, f=f: v.memset(xnsh2[0][:, 520 * f:520 * f + 1], 0.0),
                         inc="ve" if f == 7 else None)
                load_chunk(ar, 0, 0, 0)
                emit_ln(ar, 0, xn_dest=xnd2[0])
                emit_shift(0)
                emit_mix_c(0, b.cnt["pe"])
                emit_mix_b(0, b.cnt["pe"])
                for tc in range(NCHUNK):
                    par = tc % 2
                    t0 = tc * CH
                    scrC = scrC2[par]
                    b.wait("tensor", "po", po_guard)
                    b.wait("tensor", "ve", mix_done[par])

                    def ev_k(eo, pp):
                        b.wait("vector", "pe")
                        b.op("vector", lambda v, pp=pp: v.tensor_scalar(
                            pp[:], pp[:], 0.0, None, ALU.max), inc="ve")
                        b.wait("scalar", "ve")
                        b.op("scalar", lambda s, pp=pp, eo=eo: s.activation(
                            scrD[:, 512 * eo:512 * eo + 512], pp[:],
                            AF.Square), inc="ac")

                    emit_proj_fm(Wkf_, scrC, ev_k, pb, guard)
                    evk_pe[par] = b.cnt["pe"]
                    if tc < NCHUNK - 1:
                        b.wait("vector", "po")
                        for f in range(NFT):
                            b.op("vector", lambda v, f=f, par=par: v.tensor_copy(
                                xnsh2[1 - par][:, 520 * f:520 * f + 1],
                                xnsh2[par][:, 520 * f + 512:520 * f + 513]),
                                inc="ve" if f == 7 else None)
                        load_chunk(ar, 1 - par, t0 + CH, tc + 1)
                        emit_ln(ar, 1 - par, xn_dest=xnd2[1 - par])
                        emit_shift(1 - par)
                        emit_mix_c(1 - par, evk_pe[1 - par])

                    def ev_r(eo, pp):
                        b.wait("scalar", "pe")
                        b.op("scalar", lambda s, pp=pp, eo=eo, scrC=scrC: s.activation(
                            scrC[:, 512 * eo:512 * eo + 512], pp[:], AF.Sigmoid), inc="ac")

                    b.wait("tensor", "ve", mixb_done[par])
                    emit_proj_fm(Wr_, scrB, ev_r, pb, guard)
                    wr_pe = b.cnt["pe"]
                    b.wait("tensor", "ac")

                    def ev_o(eo, pp):
                        b.wait("vector", "pe")
                        b.wait("vector", "ac")
                        b.wait("vector", "po", tmp_po_g[eo % 2])
                        b.op("vector", lambda v, pp=pp, eo=eo, scrC=scrC: v.tensor_mul(
                            tmp512[:, 512 * (eo % 2):512 * (eo % 2) + 512], pp[:],
                            scrC[:, 512 * eo:512 * eo + 512]), inc="ve")
                        b.wait("gpsimd", "ve")
                        b.op("gpsimd", lambda g, eo=eo, par=par: g.tensor_add(
                            ar["xc"][par][:, 512 * eo:512 * eo + 512],
                            tmp512[:, 512 * (eo % 2):512 * (eo % 2) + 512],
                            ar["xc"][par][:, 512 * eo:512 * eo + 512]), inc="po")
                        tmp_po_g[eo % 2] = b.cnt["po"]

                    emit_proj_fm(Wvf_, scrD, ev_o, pb, guard)
                    if tc < NCHUNK - 1:
                        emit_mix_b(1 - par, wr_pe)
                    b.wait("scalar", "po", b.cnt["po"])
                    b.store("scalar", d_xT[:, :, t0:t0 + CH].rearrange("f p j -> p f j"),
                            ar["xc"][par].ap().rearrange("p (f j) -> p f j", f=8),
                            key=(ar["name"], "xc", par))
                    b.sdone[("xT", tc)] = b.cnt["stx"]
                    ar["free_g"][par] = b.cnt["ve"]
                    if chunk_cbs and tc in chunk_cbs:
                        chunk_cbs[tc]()
            b.junction()

        def sl_attn_b():
            s1k, s1v = W3, W4
            with contextlib.ExitStack() as ps:
                scrB = mk_sb(ps, "bscrB", [128, 4096], BF16)
                scrC = mk_sb(ps, "bscrC", [128, 4 * 1088], BF16)
                scrE = mk_sb(ps, "bscrE", [128, 1024], F32)
                ar = {
                    "xc": [mk_sb(ps, "bxc", [128, 4096], F32) for _ in range(2)],
                    "xb": [mk_sb(ps, "bxb", [128, 4096], BF16) for _ in range(2)],
                    "xn": [mk_sb(ps, "bxn", [128, 4096], BF16) for _ in range(2)],
                    "sq": mk_sb(ps, "bsq", [128, 4096], BF16),
                    "lnrow": mk_sb(ps, "blnr", [1, 1536], F32),
                }
                pb = [psum(ps, "pk", [128, 512], F32) for _ in range(2)]
                pkv = [psum(ps, "pkv", [64, 65], F32) for _ in range(2)]
                b.op("vector", lambda v: v.memset(kvsb[:], 0.0), inc="ve")
                b.op("vector", lambda v: v.memset(scrC[:], 1.0), inc="ve")
                guard = {}
                kvg = {}
                load_chunk(ar, 0, 0, 0)
                emit_ln(ar, 0, xn_store_key=("bxn_b", 0))
                for tc in range(NCHUNK):
                    par = tc % 2
                    t0 = tc * CH
                    b.wait("scalar", "ve")
                    b.wait_store("scalar", ("bxn_b", par))
                    b.store("scalar", d_xnsp[:, :, t0:t0 + CH].rearrange("f p j -> p f j"),
                            ar["xn"][par].ap().rearrange("p (f j) -> p f j", f=8),
                            key=("bxn_b", par))
                    b.sdone[("bxn", tc)] = b.cnt["stx"]
                    b.wait("tensor", "po", b.cnt["po"])
                    b.wait("tensor", "ve", ar["xn_ready"][par])
                    for which in ("k", "v"):
                        if which == "v" and tc < NCHUNK - 1:
                            load_chunk(ar, 1 - par, t0 + CH, tc + 1)
                            emit_ln(ar, 1 - par, xn_store_key=("bxn_b", 1 - par))
                        slot = s1k if which == "k" else s1v
                        for eo in range(2):
                            for ts in range(4):
                                bi = (eo * 4 + ts) % 2
                                pp = pb[bi]
                                for g in guard.get(bi, []):
                                    b.wait("tensor", g[0], g[1])
                                for c in range(8):
                                    b.op("tensor", lambda t, pp=pp, c=c, ts=ts, eo=eo, slot=slot, par=par: t.matmul(
                                        pp[:], ar["xn"][par][:, 512 * c + 128 * ts:512 * c + 128 * ts + 128],
                                        slot[:, 1024 * c + 512 * eo:1024 * c + 512 * eo + 512],
                                        start=(c == 0), stop=(c == 7)), inc="pe" if c == 7 else None)
                                if which == "k":
                                    b.wait("vector", "pe")
                                    b.op("vector", lambda v, pp=pp: v.tensor_scalar(
                                        scrE[:, 0:512], pp[:], 0.0, None, ALU.min), inc="ve")
                                    b.wait("scalar", "ve")
                                    b.op("scalar", lambda s: s.activation(
                                        scrE[:, 512:1024], scrE[:, 0:512], AF.Exp), inc="ac")
                                    b.wait("vector", "ac")
                                    b.op("vector", lambda v, pp=pp, ts=ts, eo=eo: v.scalar_tensor_tensor(
                                        scrB[:, 1024 * ts + 512 * eo:1024 * ts + 512 * eo + 512],
                                        pp[:], 0.0, scrE[:, 512:1024], ALU.max, ALU.add), inc="ve")
                                else:
                                    b.wait("vector", "pe")
                                    if DEBUG_TAPS and tc == 7 and eo == 0 and ts == 0:
                                        b.op("vector", lambda v, pp=pp: v.tensor_copy(
                                            scrE[:, 0:512], pp[:]), inc="ve")
                                        b.wait("scalar", "ve")
                                        b.dma("scalar", d_dbg["dbg_praw"], scrE[:, 0:512], "stx")
                                        b.dma("scalar", d_dbg["dbg_w1"], W1[:], "stx")
                                        b.dma("scalar", d_dbg["dbg_xnc7"], ar["xn"][par][:], "stx")
                                    for i8 in range(8):
                                        base = 1088 * ts + 544 * eo + 68 * i8
                                        b.op("vector", lambda v, pp=pp, base=base, i8=i8: v.tensor_copy(
                                            scrC[:, base:base + 64], pp[:, 64 * i8:64 * i8 + 64]),
                                            inc="ve" if i8 == 7 else None)
                                guard[bi] = [("ve", b.cnt["ve"]), ("ac", b.cnt["ac"])]
                    b.wait("tensor", "ve")
                    for hh in range(H):
                        pk = pkv[hh % 2]
                        for g in kvg.get(hh % 2, []):
                            b.wait("tensor", g[0], g[1])
                        for ts in range(4):
                            b.op("tensor", lambda t, pk=pk, hh=hh, ts=ts: t.matmul(
                                pk[0:64, 0:65],
                                scrB[:, 1024 * ts + 64 * hh:1024 * ts + 64 * hh + 64],
                                scrC[:, 1088 * ts + 68 * hh:1088 * ts + 68 * hh + 65],
                                start=(ts == 0), stop=(ts == 3)), inc="pe" if ts == 3 else None)
                        b.wait("vector", "pe")
                        b.op("vector", lambda v, pk=pk, hh=hh: v.tensor_add(
                            kvsb[:, 65 * hh:65 * hh + 65], kvsb[:, 65 * hh:65 * hh + 65],
                            pk[0:64, :]), inc="ve")
                        kvg[hh % 2] = [("ve", b.cnt["ve"])]
                        if hh == H - 1:
                            ar["free_g"][par] = b.cnt["ve"]
                            ar["xn_pe_g"][par] = b.cnt["pe"]
                if DEBUG_TAPS:
                    b.junction()
                    b.dma("sync", d_dbg["dbg_xnsp"], d_xnsp, "stx")
                    b.dma("sync", d_dbg["dbg_kv"], kvsb[:], "stx")
                    b.dma("sync", d_dbg["dbg_ktok"], scrB[:], "stx")
                    b.dma("sync", d_dbg["dbg_vaug"], scrC[:], "stx")
                    b.wait("sync", "stx", b.cnt["stx"])
            b.junction()
            load_weight(W3, "bq")
            load_weight(W4, "bo")
            with contextlib.ExitStack() as ps:
                ar2 = {
                    "xc": [mk_sb(ps, "cxc", [128, 4096], F32) for _ in range(2)],
                    "xn": [mk_sb(ps, "cxn", [128, 4096], BF16) for _ in range(2)],
                }
                scrB2 = mk_sb(ps, "cscrB", [128, 4096], BF16)
                scrD2 = mk_sb(ps, "cscrD", [128, 4096], BF16)
                scrE2 = mk_sb(ps, "cscrE", [128, 1024], F32)
                t_ln2 = [mk_sb(ps, "tln2", [1, 512], F32) for _ in range(2)]
                rec64p = [mk_sb(ps, "rec64b", [64, 512], F32) for _ in range(2)]
                s_hilo2 = [mk_sb(ps, "hilo2", [1, 1024], BF16) for _ in range(2)]
                outc2 = [mk_sb(ps, "coc", [128, 4096], F32) for _ in range(2)]
                pb2 = [psum(ps, "pq", [128, 512], F32) for _ in range(2)]
                pnp = [(psum(ps, "pnA2", [128, 512], F32),
                        psum(ps, "pnB2", [128, 512], F32)) for _ in range(2)]
                pw2x = [psum(ps, "pw2", [128, 512], F32) for _ in range(2)]
                pn_g = {0: (0, 0), 1: (0, 0)}
                rec_g = {0: 0, 1: 0}
                pending_wo = [None]
                wo_g = [0]
                div_ve_g = [0]

                def flush_wo():
                    if pending_wo[0] is not None:
                        pending_wo[0]()
                        pending_wo[0] = None

                b.wait("vector", "ve")
                for pr in range(8):
                    b.op("vector", lambda v, pr=pr: v.tensor_copy(
                        kvbf[0:64, 65 * pr:65 * pr + 65],
                        kvsb[0:64, 130 * pr:130 * pr + 65]),
                        inc="ve" if pr == 7 else None)
                b.wait("gpsimd", "ve")
                b.dma("gpsimd",
                      kvbf[64:128, :].rearrange("p (pr c) -> p pr c", c=65),
                      kvsb.ap().rearrange("p (pr two c) -> p two pr c", two=2, c=65)[:, 1, :, :],
                      "ldw")
                b.wait("tensor", "ldw", b.cnt["ldw"])
                b.wait("vector", "ldw", b.cnt["ldw"])
                guard2 = {}
                wg2 = {}
                for tc in range(NCHUNK):
                    par = tc % 2
                    t0 = tc * CH
                    load_chunk(ar2, par, t0, tc, also_xn_sp=True)
                    b.wait("tensor", "ldx")
                    b.wait("tensor", "po", b.cnt["po"])

                    evq_ve = {}

                    def ev_q(eo, pp):
                        b.wait("vector", "pe")
                        b.op("vector", lambda v, pp=pp: v.tensor_scalar(
                            scrE2[:, 0:512], pp[:], 0.0, None, ALU.min), inc="ve")
                        b.wait("scalar", "ve")
                        b.op("scalar", lambda s: s.activation(
                            scrE2[:, 512:1024], scrE2[:, 0:512], AF.Exp), inc="ac")
                        b.wait("vector", "ac")
                        b.op("vector", lambda v, pp=pp, eo=eo: v.scalar_tensor_tensor(
                            scrB2[:, 512 * eo:512 * eo + 512], pp[:], 0.0,
                            scrE2[:, 512:1024], ALU.max, ALU.add), inc="ve")
                        evq_ve[eo] = b.cnt["ve"]

                    emit_proj_fm(W3, ar2["xn"][par], ev_q, pb2, guard2)
                    flush_wo()
                    pending_bc = [None]
                    for pr in range(8):
                        pnA2, pnB2 = pnp[pr % 2]
                        g_ac, g_ve = pn_g[pr % 2]
                        b.wait("tensor", "ve", max(g_ve, evq_ve[pr]))
                        b.wait("tensor", "ac", g_ac)
                        b.op("tensor", lambda t, pr=pr, pnA2=pnA2: t.matmul(
                            pnA2[0:65, :], kvbf[0:64, 65 * pr:65 * pr + 65],
                            scrB2[0:64, 512 * pr:512 * pr + 512],
                            start=True, stop=True), inc="pe")
                        b.op("tensor", lambda t, pr=pr, pnB2=pnB2: t.matmul(
                            pnB2[0:65, :], kvbf[64:128, 65 * pr:65 * pr + 65],
                            scrB2[64:128, 512 * pr:512 * pr + 512],
                            start=True, stop=True, tile_position=(64, 0)), inc="pe")
                        num_pe = b.cnt["pe"]
                        if pending_bc[0] is not None:
                            pending_bc[0]()
                        b.wait("scalar", "pe", num_pe)
                        for two, pn in ((0, pnA2), (1, pnB2)):
                            b.op("scalar", lambda s, pn=pn, tl=t_ln2[two]: s.activation(
                                tl[:], pn[64:65, :], AF.Ln, bias=ATTNB_EPS), inc="ac")
                        ln_ac = b.cnt["ac"]
                        for two in (0, 1):
                            b.wait("vector", "ac", ln_ac - 1 + two)
                            b.op("vector", lambda v, tl=t_ln2[two], sh=s_hilo2[two]:
                                 v.tensor_copy(sh[:, 0:512], tl[:]))
                            b.op("vector", lambda v, tl=t_ln2[two], sh=s_hilo2[two]:
                                 v.scalar_tensor_tensor(
                                     sh[:, 512:1024], sh[:, 0:512], -1.0, tl[:],
                                     ALU.mult, ALU.add), inc="ve")
                        hilo_ve = b.cnt["ve"]

                        def _bc(pr=pr, pnA2=pnA2, pnB2=pnB2, hilo_ve=hilo_ve):
                            for two, pn in ((0, pnA2), (1, pnB2)):
                                b.wait("tensor", "ve", hilo_ve - 1 + two)
                                b.op("tensor", lambda t, pn=pn, sh=s_hilo2[two]: t.matmul(
                                    pn[64:128, :], s_ones64b[:], sh[:, 0:512],
                                    start=True, stop=False))
                                b.op("tensor", lambda t, pn=pn, sh=s_hilo2[two]: t.matmul(
                                    pn[64:128, :], s_ones64b[:], sh[:, 512:1024],
                                    start=False, stop=True), inc="pe")
                            bc_pe = b.cnt["pe"]
                            for two, pn in ((0, pnA2), (1, pnB2)):
                                hh = 2 * pr + two
                                b.wait("scalar", "pe", bc_pe - 1 + two)
                                b.wait("scalar", "ve", rec_g[two])
                                b.op("scalar", lambda s, pn=pn, rc=rec64p[two]:
                                     s.activation(rc[:], pn[64:128, :], AF.Exp,
                                                  scale=-1.0), inc="ac")
                                b.wait("vector", "ac")
                                b.wait("vector", "pe", wo_g[0])
                                fo, ro = hh // 2, 64 * (hh % 2)
                                b.op("vector", lambda v, pn=pn, fo=fo, ro=ro,
                                     rc=rec64p[two]: v.tensor_mul(
                                         scrD2[ro:ro + 64, 512 * fo:512 * fo + 512],
                                         pn[0:64, :], rc[:]), inc="ve")
                                rec_g[two] = b.cnt["ve"]
                            pn_g[pr % 2] = (b.cnt["ac"], b.cnt["ve"])

                        pending_bc[0] = _bc
                    pending_bc[0]()
                    div_end_ve = b.cnt["ve"]

                    def _wo(par=par, t0=t0, tc=tc, div_end_ve=div_end_ve):
                        b.wait("tensor", "ve", div_end_ve)
                        b.wait_store("vector", ("coc", par))
                        for eo in range(8):
                            pp = pw2x[eo % 2]
                            for g in wg2.get(eo % 2, []):
                                b.wait("tensor", g[0], g[1])
                            for c in range(8):
                                b.op("tensor", lambda t, pp=pp, c=c, eo=eo: t.matmul(
                                    pp[:], wslot_ap(W4, c, eo), scrD2[:, 512 * c:512 * c + 512],
                                    start=(c == 0), stop=(c == 7)), inc="pe" if c == 7 else None)
                            b.wait("vector", "pe")
                            b.op("vector", lambda v, pp=pp, eo=eo, par=par: v.tensor_add(
                                outc2[par][:, 512 * eo:512 * eo + 512], pp[:],
                                ar2["xc"][par][:, 512 * eo:512 * eo + 512]), inc="ve")
                            wg2[eo % 2] = [("ve", b.cnt["ve"])]
                        wo_g[0] = b.cnt["pe"]
                        b.wait("scalar", "ve")
                        b.store("scalar", d_xT[:, :, t0:t0 + CH].rearrange("f p j -> p f j"),
                                outc2[par].ap().rearrange("p (f j) -> p f j", f=8),
                                key=("coc", par))
                        b.sdone[("xT", tc)] = b.cnt["stx"]
                        ar2["free_g"][par] = b.cnt["ve"]
                        div_ve_g[0] = div_end_ve

                    pending_wo[0] = _wo
                flush_wo()
            b.junction()

        def final_out():
            with contextlib.ExitStack() as ps:
                xc = [mk_sb(ps, "zxc", [128, 1024], F32) for _ in range(2)]
                ytok = [mk_sb(ps, "ytok", [128, 1024], F32) for _ in range(2)]
                p_tr = [psum(ps, "pft", [128, 128], F32) for _ in range(2)]
                fin_pe = {}
                for tt in range(NTT):
                    t0 = tt * 128
                    buf = xc[tt % 2]
                    b.wait_store("sync", ("xT", tt // 4))
                    b.wait("sync", "pe", fin_pe.get(tt % 2, 0))
                    b.dma("sync", buf.ap().rearrange("p (f j) -> p f j", f=8),
                          d_xT[:, :, t0:t0 + 128].rearrange("f p j -> p f j"), "ldx")
                    b.wait("tensor", "ldx")
                    b.wait_store("vector", ("ytok", tt % 2))
                    for f in range(NFT):
                        pb2 = p_tr[f % 2]
                        b.wait("tensor", "ve", b.cnt["ve"] - 1)
                        b.op("tensor", lambda t, pb2=pb2, buf=buf, f=f: t.transpose(
                            pb2[:], buf[:, 128 * f:128 * f + 128], s_identf[:]), inc="pe")
                        b.wait("vector", "pe")
                        b.op("vector", lambda v, pb2=pb2, f=f, tt=tt: v.tensor_copy(
                            ytok[tt % 2][:, 128 * f:128 * f + 128], pb2[:]), inc="ve")
                    fin_pe[tt % 2] = b.cnt["pe"]
                    b.wait("scalar", "ve")
                    b.store("scalar", d_y[t0:t0 + 128, :], ytok[tt % 2][:],
                            key=("ytok", tt % 2))
            b.wait("sync", "stx")

        po_ffn = [0]

        def _prefetch_ffn():
            load_weight(W0, "fr")
            load_weight(W1, "fk")
            load_weight(W2, "fv")
            po_ffn[0] = b.cnt["po"]

        if N_SL >= 1:
            sl_attn_a(prefetch_after_b=_prefetch_ffn if N_SL >= 2 else None)
        if N_SL >= 2:
            cbs = None
            if N_SL >= 3:
                cbs = {1: lambda: load_weight(W3, "bk"),
                       3: lambda: load_weight(W4, "bv", pe_guard=0)}
            sl_ffn("f1", po_ffn[0], chunk_cbs=cbs)
        if N_SL >= 3:
            sl_attn_b()
        if N_SL >= 4:
            sl_ffn("f2", po_ffn[0])
        final_out()

        with nc.Block() as blk:
            @blk.sync
            def _(e):
                for fn in b.bodies["sync"]:
                    fn(e)

            @blk.tensor
            def _(e):
                for fn in b.bodies["tensor"]:
                    fn(e)

            @blk.vector
            def _(e):
                for fn in b.bodies["vector"]:
                    fn(e)

            @blk.scalar
            def _(e):
                for fn in b.bodies["scalar"]:
                    fn(e)

            @blk.gpsimd
            def _(e):
                for fn in b.bodies["gpsimd"]:
                    fn(e)

    return nc


def kernel(L0_attn_a_x, L0_attn_a_Wq, L0_attn_a_Wk, L0_attn_a_Wv, L0_attn_a_Wo,
           L0_attn_b_Wq, L0_attn_b_Wk, L0_attn_b_Wv, L0_attn_b_Wo,
           L0_ffn_W_r, L0_ffn_W_k, L0_ffn_W_v, L0_ffn_mu_r, L0_ffn_mu_k):
    x = np.asarray(L0_attn_a_x, dtype=np.float32)
    ws = {
        "aq": L0_attn_a_Wq, "ak": L0_attn_a_Wk, "av": L0_attn_a_Wv, "ao": L0_attn_a_Wo,
        "bq": L0_attn_b_Wq, "bk": L0_attn_b_Wk, "bv": L0_attn_b_Wv, "bo": L0_attn_b_Wo,
        "fr": L0_ffn_W_r, "fk": L0_ffn_W_k, "fv": L0_ffn_W_v,
    }
    wT = {("w_" + k): np.ascontiguousarray(np.asarray(v, dtype=np.float32).T)
          for k, v in ws.items()}
    mu_r = np.ascontiguousarray(np.asarray(L0_ffn_mu_r, np.float32).reshape(8, 128).T)
    mu_k = np.ascontiguousarray(np.asarray(L0_ffn_mu_k, np.float32).reshape(8, 128).T)

    nc = build_program()
    in_maps = []
    for b_ in range(8):
        m = {"x": np.ascontiguousarray(x[b_])}
        m.update(wT)
        m["mu_r"] = mu_r
        m["mu_k"] = mu_k
        in_maps.append(m)
    res = run_bass_kernel_spmd(nc, in_maps, core_ids=list(range(8)))
    kernel._last_results = res.results
    out = np.stack([res.results[b_]["y"] for b_ in range(8)], axis=0)
    return out


# version 20
# speedup vs baseline: 1.0314x; 1.0232x over previous
import contextlib
import math
import numpy as np
import ml_dtypes
import concourse.bass as bass
import concourse.mybir as mybir
from concourse.bass_utils import run_bass_kernel_spmd

F32 = mybir.dt.float32
BF16 = mybir.dt.bfloat16
AF = mybir.ActivationFunctionType
ALU = mybir.AluOpType
BF = ml_dtypes.bfloat16

S = 4096
D = 1024
H = 16
HD = 64
NCHUNK = 8
CH = 512
NFT = 8
NTT = 32
SHIFT = 12.0
ALIBI_M = 20.0
LN_EPS = 1e-5
ATTNB_EPS = 1e-6
DEBUG_TAPS = False
N_SL = 4

SLOPES = [2.0 ** (-(i + 1) / 2.0) for i in range(H)]
CUTOFF = [ALIBI_M / s for s in SLOPES]


def kept_ks_range(h, qi):
    cap = int(math.floor((ALIBI_M + 20.0) / SLOPES[h] / 128.0))
    ks_hi = min(4 * qi + 3, cap, NTT - 1)
    return 0, ks_hi


def exact_split3(v):
    hi = 4.0 * np.floor(v / 4.0)
    rem = v - hi
    mid = 0.125 * np.floor(rem / 0.125)
    lo = rem - mid
    return hi, mid, lo


class Builder:
    ENGS = ("sync", "tensor", "vector", "scalar", "gpsimd")

    def __init__(self, nc):
        self.nc = nc
        self.bodies = {e: [] for e in self.ENGS}
        self.sems = {}
        self.cnt = {}
        self.sdone = {}

    def sem(self, stack, name):
        self.sems[name] = stack.enter_context(self.nc.semaphore(name))
        self.cnt[name] = 0

    def emit(self, eng, fn):
        self.bodies[eng].append(fn)

    def wait(self, eng, semname, val=None):
        v = self.cnt[semname] if val is None else val
        if v <= 0:
            return
        s = self.sems[semname]
        self.emit(eng, lambda e, s=s, v=v: e.wait_ge(s, v))

    def inc(self, semname, n=1):
        self.cnt[semname] += n
        return self.sems[semname]

    def op(self, eng, fn, inc=None, incn=1):
        if inc is not None:
            s = self.inc(inc, incn)
            self.emit(eng, lambda e, fn=fn, s=s, n=incn: fn(e).then_inc(s, n))
        else:
            self.emit(eng, fn)

    def dma(self, eng, out, in_, inc):
        s = self.inc(inc, 16)
        self.emit(eng, lambda e, o=out, i=in_, s=s: e.dma_start(out=o, in_=i).then_inc(s, 16))

    def dma_count(self, n):
        return 16 * n

    def store(self, eng, out, in_, key=None):
        self.dma(eng, out, in_, "stx")
        if key is not None:
            self.sdone[key] = self.cnt["stx"]

    def wait_store(self, eng, key):
        if key in self.sdone:
            self.wait(eng, "stx", self.sdone[key])

    def junction(self):
        for eng in self.ENGS:
            for sname in ("pe", "ve", "ac", "po", "stx"):
                self.wait(eng, sname)


def build_program():
    nc = bass.Bass("TRN2", target_bir_lowering=False)

    d_x = nc.dram_tensor("x", [S, D], F32, kind="ExternalInput").ap()
    d_y = nc.dram_tensor("y", [S, D], F32, kind="ExternalOutput").ap()
    wnames = ["aq", "ak", "av", "ao", "bq", "bk", "bv", "bo", "fr", "fk", "fv"]
    d_w = {n: nc.dram_tensor("w_" + n, [D, D], F32, kind="ExternalInput").ap()
           for n in wnames}
    d_mur = nc.dram_tensor("mu_r", [128, 8], F32, kind="ExternalInput").ap()
    d_muk = nc.dram_tensor("mu_k", [128, 8], F32, kind="ExternalInput").ap()

    d_xT = nc.dram_tensor("xT_scr", [NFT, 128, S], F32, kind="Internal").ap()
    d_Qsp = nc.dram_tensor("Q_sp", [H, HD, S], BF16, kind="Internal").ap()
    d_Ksp = nc.dram_tensor("K_sp", [H, HD, S], BF16, kind="Internal").ap()
    d_Vsp = nc.dram_tensor("V_sp", [H, S, HD], BF16, kind="Internal").ap()
    d_xnsp = nc.dram_tensor("xn_sp", [NFT, 128, S], BF16, kind="Internal").ap()
    d_dbg = {}
    if DEBUG_TAPS:
        d_dbg["dbg_Q"] = nc.dram_tensor("dbg_Q", [H, HD, S], BF16, kind="ExternalOutput").ap()
        d_dbg["dbg_K"] = nc.dram_tensor("dbg_K", [H, HD, S], BF16, kind="ExternalOutput").ap()
        d_dbg["dbg_V"] = nc.dram_tensor("dbg_V", [H, S, HD], BF16, kind="ExternalOutput").ap()
        d_dbg["dbg_ao"] = nc.dram_tensor("dbg_ao", [128, 8 * 4096], BF16, kind="ExternalOutput").ap()
        d_dbg["dbg_xnsp"] = nc.dram_tensor("dbg_xnsp", [NFT, 128, S], BF16, kind="ExternalOutput").ap()
        d_dbg["dbg_kv"] = nc.dram_tensor("dbg_kv", [128, 8 * 65], F32, kind="ExternalOutput").ap()
        d_dbg["dbg_ktok"] = nc.dram_tensor("dbg_ktok", [128, 4096], BF16, kind="ExternalOutput").ap()
        d_dbg["dbg_vaug"] = nc.dram_tensor("dbg_vaug", [128, 4352], BF16, kind="ExternalOutput").ap()
        d_dbg["dbg_praw"] = nc.dram_tensor("dbg_praw", [128, 512], F32, kind="ExternalOutput").ap()
        d_dbg["dbg_w1"] = nc.dram_tensor("dbg_w1", [128, 8192], BF16, kind="ExternalOutput").ap()
        d_dbg["dbg_xnc7"] = nc.dram_tensor("dbg_xnc7", [128, 4096], BF16, kind="ExternalOutput").ap()

    ident_f = nc.inline_tensor(np.eye(128, dtype=np.float32), "ident_f").ap()
    ones_col_b = nc.inline_tensor(np.ones((128, 1), dtype=np.float32).astype(BF), "ones_col_b").ap()
    ones_1x128_f = nc.inline_tensor(np.ones((1, 128), dtype=np.float32), "ones_1x128_f").ap()
    ones_1x64_f = nc.inline_tensor(np.ones((1, 64), dtype=np.float32), "ones_1x64_f").ap()
    ones_1x64_b = nc.inline_tensor(np.ones((1, 64), dtype=np.float32).astype(BF), "ones_1x64_b").ap()
    ones3_b = nc.inline_tensor(np.ones((3, S), dtype=np.float32).astype(BF), "ones3_b").ap()
    tri_np = np.where(np.arange(128)[:, None] > np.arange(128)[None, :], -1e9, 0.0).astype(np.float32)
    tri_m = nc.inline_tensor(tri_np, "tri_m").ap()

    kal = np.zeros((H, 3, S), dtype=np.float64)
    for h in range(H):
        vk = SLOPES[h] * (np.arange(S) % 128).astype(np.float64)
        hi, mid, lo = exact_split3(vk)
        kal[h, 0], kal[h, 1], kal[h, 2] = -hi, -mid, -lo
    d_kal = nc.inline_tensor(kal.astype(np.float32).astype(BF), "kal").ap()

    fvals_raw = [0.0, 1.0, -1.0, 1.0 / D, LN_EPS, 0.125, -SHIFT, ATTNB_EPS]
    for h in range(H):
        for qi in range(NCHUNK):
            ks_lo, ks_hi = kept_ks_range(h, qi)
            for ks in range(ks_lo, ks_hi + 1):
                fvals_raw.append(-SLOPES[h] * 128 * ks - SHIFT)
    fidx = {}
    fcols = []
    for v in fvals_raw:
        v32 = np.float32(v).item()
        if v32 not in fidx:
            fidx[v32] = len(fcols)
            fcols.append(v32)
        fidx[v] = fidx[v32]
    d_fpool = nc.inline_tensor(
        np.tile(np.array([fcols], dtype=np.float32), (128, 1)), "fpool").ap()

    uid = [0]

    with contextlib.ExitStack() as st:
        b = Builder(nc)
        for s in ("ldx", "ldw", "ldh", "ldm", "stx", "pe", "ve", "ac", "po"):
            b.sem(st, s)

        def mk_sb(stack, name, shape, dt):
            uid[0] += 1
            return stack.enter_context(nc.sbuf_tensor(f"{name}_{uid[0]}", shape, dt))

        s_fpool = mk_sb(st, "fpool", [128, len(fcols)], F32)
        s_identf = mk_sb(st, "identf", [128, 128], F32)
        s_onescol = mk_sb(st, "onescol", [128, 1], BF16)
        s_ones128 = mk_sb(st, "ones128", [1, 128], F32)
        s_ones64 = mk_sb(st, "ones64", [1, 64], F32)
        s_ones64b = mk_sb(st, "ones64b", [1, 64], BF16)
        s_tri = mk_sb(st, "tri", [128, 128], F32)
        s_mur = mk_sb(st, "mur", [128, 8], F32)
        s_muk = mk_sb(st, "muk", [128, 8], F32)
        W0 = mk_sb(st, "W0", [128, 8192], BF16)
        W1 = mk_sb(st, "W1", [128, 8192], BF16)
        W2 = mk_sb(st, "W2", [128, 8192], BF16)
        W3 = mk_sb(st, "W3", [128, 8192], BF16)
        W4 = mk_sb(st, "W4", [128, 8192], BF16)

        for v, i in fidx.items():
            nc.const_aps.register(F32, v, s_fpool[:, i:i + 1])

        def fc(v):
            return s_fpool[:, fidx[v]:fidx[v] + 1]

        b.dma("sync", s_fpool[:], d_fpool, "ldm")
        b.dma("sync", s_identf[:], ident_f, "ldm")
        b.dma("sync", s_onescol[:], ones_col_b, "ldm")
        b.dma("sync", s_ones128[:], ones_1x128_f, "ldm")
        b.dma("sync", s_ones64[:], ones_1x64_f, "ldm")
        b.dma("sync", s_ones64b[:], ones_1x64_b, "ldm")
        b.dma("sync", s_tri[:], tri_m, "ldm")
        b.dma("sync", s_mur[:], d_mur, "ldm")
        b.dma("sync", s_muk[:], d_muk, "ldm")
        for eng in ("tensor", "vector", "scalar", "gpsimd"):
            b.wait(eng, "ldm")

        def psum(stack2, name, shape, dt=F32):
            uid[0] += 1
            return stack2.enter_context(nc.psum_tensor(f"{name}_{uid[0]}", shape, dt))

        def load_weight(slot, wname, pe_guard=None):
            b.wait("gpsimd", "pe", b.cnt["pe"] if pe_guard is None else pe_guard)
            b.dma("gpsimd", slot.ap().rearrange("p (c d) -> p c d", c=8),
                  d_w[wname].rearrange("(c p) d -> p c d", p=128), "po", )

        def wslot_ap(slot, c, eo):
            return slot[:, 1024 * c + 128 * eo: 1024 * c + 128 * eo + 128]

        def emit_ln(ar, par, xn_dest=None, xn_store_key=None):
            xcb, xbb = ar["xc"][par], ar["xb"][par]
            ln_row = ar["lnrow"]
            if xn_dest is None:
                xnb = ar["xn"][par]
                xn_dest = [xnb[:, 512 * f:512 * f + 512] for f in range(NFT)]
            with contextlib.ExitStack() as ps:
                p_m = psum(ps, "lnm", [128, 512], F32)
                p_r = psum(ps, "lnr", [128, 512], F32)
                p_s = p_m[0:1, :]
                p_q = p_r[0:1, :]
                xg = ar.setdefault("xb_g", {})
                b.wait("gpsimd", "pe", xg.get(par, (0, 0))[0])
                b.wait("gpsimd", "ve", xg.get(par, (0, 0))[1])
                b.wait("gpsimd", "pe", ar.setdefault("sq_g", [0])[0])
                xcr = ar.get("xc_ready")
                if xcr is not None:
                    b.wait("gpsimd", "ve", xcr[par])
                else:
                    b.wait("gpsimd", "ldx")
                b.op("gpsimd", lambda g: g.tensor_copy(xbb[:], xcb[:]), inc="po")
                ar.setdefault("cast_po", {})[par] = b.cnt["po"]
                b.op("gpsimd", lambda g: g.tensor_mul(ar["sq"][:], xbb[:], xbb[:]), inc="po")
                b.wait("tensor", "po")
                for f in range(NFT):
                    b.op("tensor", lambda t, f=f: t.matmul(
                        p_s, s_onescol[:], xbb[:, 512 * f:512 * f + 512],
                        start=(f == 0), stop=(f == 7)), inc="pe" if f == 7 else None)
                for f in range(NFT):
                    b.op("tensor", lambda t, f=f: t.matmul(
                        p_q, s_onescol[:], ar["sq"][:, 512 * f:512 * f + 512],
                        start=(f == 0), stop=(f == 7)), inc="pe" if f == 7 else None)
                ar["sq_g"][0] = b.cnt["pe"]
                b.wait("vector", "pe")
                b.op("vector", lambda v: v.tensor_scalar(
                    ln_row[:, 512:1024], p_s, 1.0 / D, None, ALU.mult))
                b.op("vector", lambda v: v.tensor_mul(
                    ln_row[:, 1024:1536], ln_row[:, 512:1024], ln_row[:, 512:1024]))
                b.op("vector", lambda v: v.scalar_tensor_tensor(
                    ln_row[:, 0:512], ln_row[:, 1024:1536], -float(D), p_q,
                    ALU.mult, ALU.add), inc="ve")
                b.wait("scalar", "ve")
                b.op("scalar", lambda s: s.activation(
                    ln_row[:, 0:512], ln_row[:, 0:512], AF.Ln,
                    bias=LN_EPS, scale=1.0 / D))
                b.op("scalar", lambda s: s.activation(
                    ln_row[:, 0:512], ln_row[:, 0:512], AF.Exp, scale=-0.5), inc="ac")
                b.wait("tensor", "ac")
                b.op("tensor", lambda t: t.matmul(p_m[:], s_ones128[:], ln_row[:, 512:1024],
                                                  start=True, stop=True))
                b.op("tensor", lambda t: t.matmul(p_r[:], s_ones128[:], ln_row[:, 0:512],
                                                  start=True, stop=True), inc="pe")
                b.wait("vector", "pe", ar.setdefault("xn_pe_g", {}).get(par, 0))
                b.wait("vector", "pe")
                b.wait("vector", "po")
                if xn_store_key is not None:
                    b.wait_store("vector", xn_store_key)
                for f in range(NFT):
                    dst = xn_dest[f]
                    b.op("vector", lambda v, dst=dst, f=f: v.tensor_sub(
                        dst, xbb[:, 512 * f:512 * f + 512], p_m[:]))
                    b.op("vector", lambda v, dst=dst: v.tensor_mul(
                        dst, dst, p_r[:]), inc="ve" if f == 7 else None)
                ar["xb_g"][par] = (b.cnt["pe"], b.cnt["ve"])
            ar.setdefault("xn_ready", {})[par] = b.cnt["ve"]

        def emit_proj_fm(slot, rhs_buf, evac_fn, pbanks, guard):
            nb = len(pbanks)
            for eo in range(8):
                pp = pbanks[eo % nb]
                for g in guard.get(eo % nb, []):
                    b.wait("tensor", g[0], g[1])
                for c in range(8):
                    b.op("tensor", lambda t, pp=pp, c=c, eo=eo: t.matmul(
                        pp[:], wslot_ap(slot, c, eo),
                        rhs_buf[:, 512 * c:512 * c + 512],
                        start=(c == 0), stop=(c == 7)), inc="pe" if c == 7 else None)
                evac_fn(eo, pp)
                guard[eo % nb] = [("ve", b.cnt["ve"]), ("ac", b.cnt["ac"])]

        def load_chunk(ar, par, t0, tc, also_xn_sp=False):
            b.wait_store("sync", ("xT", tc))
            fg = ar.setdefault("free_g", {})
            b.wait("sync", "ve", fg.get(par, 0))
            xpf = ar.get("xc_pe_free")
            if xpf is not None:
                b.wait("sync", "pe", xpf.get(par, 0))
            if "name" in ar:
                b.wait_store("sync", (ar["name"], "xc", par))
            if also_xn_sp:
                b.wait_store("sync", ("bxn", tc))
                b.dma("sync", ar["xn"][par].ap().rearrange("p (f j) -> p f j", f=8),
                      d_xnsp[:, :, t0:t0 + CH].rearrange("f p j -> p f j"), "ldx")
            b.dma("sync", ar["xc"][par].ap().rearrange("p (f j) -> p f j", f=8),
                  d_xT[:, :, t0:t0 + CH].rearrange("f p j -> p f j"), "ldx")

        def sl_attn_a(prefetch_after_b=None):
            load_weight(W0, "aq", pe_guard=0)
            load_weight(W1, "ak", pe_guard=0)
            load_weight(W2, "av", pe_guard=0)
            load_weight(W3, "ao", pe_guard=0)
            with contextlib.ExitStack() as ps:
                scrB = mk_sb(ps, "ascrB", [128, 4096], BF16)
                scrC = mk_sb(ps, "ascrC", [128, 4096], BF16)
                scrD = mk_sb(ps, "ascrD", [128, 4096], BF16)
                ar = {
                    "xc": [mk_sb(ps, "axc", [128, 4096], F32) for _ in range(2)],
                    "xb": [mk_sb(ps, "axb", [128, 4096], BF16) for _ in range(2)],
                    "xn": [mk_sb(ps, "axn", [128, 4096], BF16) for _ in range(2)],
                    "sq": mk_sb(ps, "asq", [128, 4096], BF16),
                    "lnrow": mk_sb(ps, "alnr", [1, 1536], F32),
                    "xc_ready": {},
                    "free_g": {},
                }
                tkb = [mk_sb(ps, "atkb", [128, 1024], F32) for _ in range(2)]
                pb = [psum(ps, "pa", [128, 512], F32) for _ in range(2)]
                pv = [psum(ps, "pv", [128, 512], F32) for _ in range(2)]
                p_tr = [psum(ps, "ptrA", [128, 512], F32) for _ in range(2)]
                guard = {}
                vguard = {}
                tkb_pe = {}
                ptr_g = {}

                def trans_chunk(par, t0, tc):
                    xcb = ar["xc"][par]
                    b.wait("vector", "po", ar.get("cast_po", {}).get(par, 0))
                    b.wait_store("vector", ("atr", par))
                    for it in range(4):
                        tt = 4 * tc + it
                        tb = tkb[tt % 2]
                        b.wait("sync", "pe", tkb_pe.get(tt % 2, 0))
                        b.dma("sync", tb[:], d_x[t0 + 128 * it:t0 + 128 * it + 128, :], "ldx")
                        b.wait("tensor", "ldx")
                        for f in range(NFT):
                            bk, co = f % 2, 128 * (f // 2)
                            b.wait("tensor", "ve", ptr_g.get(bk, 0))
                            b.op("tensor", lambda t, tb=tb, f=f, bk=bk, co=co: t.transpose(
                                p_tr[bk][:, co:co + 128],
                                tb[:, 128 * f:128 * f + 128], s_identf[:]), inc="pe")
                            b.wait("vector", "pe")
                            b.op("vector", lambda v, xcb=xcb, f=f, it=it, bk=bk, co=co: v.tensor_copy(
                                xcb[:, 512 * f + 128 * it:512 * f + 128 * it + 128],
                                p_tr[bk][:, co:co + 128]), inc="ve")
                            ptr_g[bk] = b.cnt["ve"]
                        tkb_pe[tt % 2] = b.cnt["pe"]
                    ar["xc_ready"][par] = b.cnt["ve"]
                    b.wait("scalar", "ve")
                    b.store("scalar", d_xT[:, :, t0:t0 + CH].rearrange("f p j -> p f j"),
                            xcb.ap().rearrange("p (f j) -> p f j", f=8),
                            key=("atr", par))
                    b.sdone[("xT", tc)] = b.cnt["stx"]

                trans_chunk(0, 0, 0)
                emit_ln(ar, 0)
                for tc in range(NCHUNK):
                    par = tc % 2
                    t0 = tc * CH
                    b.wait("tensor", "po", b.cnt["po"])
                    b.wait("tensor", "ve", ar["xn_ready"][par])
                    b.wait_store("scalar", "aQK")

                    def ev_q(eo, pp):
                        b.wait("scalar", "pe")
                        b.op("scalar", lambda s, pp=pp, eo=eo: s.activation(
                            scrB[:, 512 * eo:512 * eo + 512], pp[:], AF.Copy,
                            scale=0.125), inc="ac")

                    def ev_k(eo, pp):
                        b.wait("scalar", "pe")
                        b.op("scalar", lambda s, pp=pp, eo=eo: s.activation(
                            scrC[:, 512 * eo:512 * eo + 512], pp[:], AF.Copy), inc="ac")

                    emit_proj_fm(W0, ar["xn"][par], ev_q, pb, guard)
                    if tc < NCHUNK - 1:
                        trans_chunk(1 - par, t0 + CH, tc + 1)
                        emit_ln(ar, 1 - par)
                    emit_proj_fm(W1, ar["xn"][par], ev_k, pb, guard)
                    b.wait("scalar", "ac")
                    b.store("scalar",
                            d_Qsp.rearrange("(e two) hd t -> e (two hd) t", two=2)[:, :, t0:t0 + CH].rearrange("e p t -> p e t"),
                            scrB.ap().rearrange("p (e t) -> p e t", e=8))
                    b.store("scalar",
                            d_Ksp.rearrange("(e two) hd t -> e (two hd) t", two=2)[:, :, t0:t0 + CH].rearrange("e p t -> p e t"),
                            scrC.ap().rearrange("p (e t) -> p e t", e=8), key="aQK")
                    b.wait_store("vector", "aV")
                    for eo in range(2):
                        for ts in range(4):
                            bi = (eo * 4 + ts) % 2
                            pp = pv[bi]
                            for g in vguard.get(bi, []):
                                b.wait("tensor", g[0], g[1])
                            for c in range(8):
                                b.op("tensor", lambda t, pp=pp, c=c, ts=ts, eo=eo, par=par: t.matmul(
                                    pp[:], ar["xn"][par][:, 512 * c + 128 * ts:512 * c + 128 * ts + 128],
                                    W2[:, 1024 * c + 512 * eo:1024 * c + 512 * eo + 512],
                                    start=(c == 0), stop=(c == 7)), inc="pe" if c == 7 else None)
                            b.wait("vector", "pe")
                            b.op("vector", lambda v, pp=pp, ts=ts, eo=eo: v.tensor_copy(
                                scrD[:, 1024 * ts + 512 * eo:1024 * ts + 512 * eo + 512], pp[:]),
                                inc="ve")
                            vguard[bi] = [("ve", b.cnt["ve"])]
                    b.wait("scalar", "ve")
                    for ts in range(4):
                        b.store("scalar",
                                d_Vsp[:, t0 + 128 * ts:t0 + 128 * ts + 128, :].rearrange("h p hd -> p h hd"),
                                scrD[:, 1024 * ts:1024 * ts + 1024].rearrange("p (h hd) -> p h hd", h=16),
                                key="aV" if ts == 3 else None)
                    ar["free_g"][par] = b.cnt["ve"]
                    ar["xn_pe_g"][par] = b.cnt["pe"]
            po_own = b.cnt["po"]
            if prefetch_after_b:
                prefetch_after_b()
            if DEBUG_TAPS:
                b.wait("sync", "stx", b.cnt["stx"])
                b.dma("sync", d_dbg["dbg_Q"], d_Qsp, "stx")
                b.dma("sync", d_dbg["dbg_K"], d_Ksp, "stx")
                b.dma("sync", d_dbg["dbg_V"], d_Vsp, "stx")
            b.junction()
            with contextlib.ExitStack() as outer:
                aoT = mk_sb(outer, "aoT", [128, 8 * 4096], BF16)
                with contextlib.ExitStack() as ps:
                    Kbuf = [mk_sb(ps, "Kb", [128, S], BF16) for _ in range(2)]
                    Qbuf = [mk_sb(ps, "Qb", [128, S], BF16) for _ in range(2)]
                    Vbuf = [mk_sb(ps, "Vb", [128, 32 * 66], BF16) for _ in range(2)]
                    Pbuf = [mk_sb(ps, "Pb", [128, 512], BF16) for _ in range(3)]
                    rec_sb = mk_sb(ps, "recsb", [1, 512], F32)
                    rec64 = mk_sb(ps, "rec64", [64, 512], F32)
                    s_hilo = mk_sb(ps, "hilo", [1, 1024], BF16)
                    p_s2 = [psum(ps, "ps", [128, 512], F32) for _ in range(3)]
                    p_av = [psum(ps, "pav", [65, 512], F32) for _ in range(2)]
                    p_rb = psum(ps, "prb", [64, 512], F32)
                    for bb in range(2):
                        b.op("vector", lambda v, bb=bb: v.memset(Vbuf[bb][:], 1.0))
                        b.op("vector", lambda v, bb=bb: v.memset(Kbuf[bb][64:128, :], 0.0))
                        b.op("vector", lambda v, bb=bb: v.memset(Qbuf[bb][64:128, :], 0.0), inc="ve")
                    b.wait("sync", "ve", b.cnt["ve"])
                    for bb in range(2):
                        b.dma("sync", Qbuf[bb][64:67, :], ones3_b, "ldm")
                    b.wait("tensor", "ldm", b.cnt["ldm"])
                    last_head_pe = {}
                    sbank_g = {0: 0, 1: 0, 2: 0}
                    pbuf_g = {0: 0, 1: 0, 2: 0}
                    pav_g = {0: 0, 1: 0}
                    prb_g = [0]
                    pending_div = [None]

                    def flush_div():
                        if pending_div[0] is not None:
                            pending_div[0]()
                            pending_div[0] = None
                    for h in range(H):
                        hb = h % 2
                        if h >= 2:
                            b.wait("sync", "pe", last_head_pe[h - 2])
                        b.dma("sync", Kbuf[hb][0:64, :], d_Ksp[h], "ldh")
                        b.dma("sync", Kbuf[hb][64:67, :], d_kal[h], "ldh")
                        b.dma("sync", Qbuf[hb][0:64, :], d_Qsp[h], "ldh")
                        b.dma("sync", Vbuf[hb].ap().rearrange("p (ts c) -> p ts c", c=66)[:, :, 0:64],
                              d_Vsp[h].rearrange("(ts p) hd -> p ts hd", p=128), "ldh")
                        b.wait("tensor", "ldh")
                        for qi in range(NCHUNK):
                            q0 = qi * CH
                            ks_lo, ks_hi = kept_ks_range(h, qi)
                            ks_list = list(range(ks_lo, ks_hi + 1))
                            nks = len(ks_list)
                            pav = p_av[qi % 2]
                            b.wait("tensor", "ve", pav_g[qi % 2])
                            exp_at = {}
                            av_emitted = [0]

                            def emit_av(i, ks, first, last, pav=pav, hb=hb):
                                co = max(0, 128 * ks - q0)
                                b.wait("tensor", "ac", exp_at[i])
                                b.op("tensor", lambda t, pav=pav, i=i, ks=ks, co=co, hb=hb,
                                     first=first, last=last: t.matmul(
                                    pav[0:65, co:512], Vbuf[hb][:, 66 * ks:66 * ks + 65],
                                    Pbuf[i % 3][:, co:512],
                                    start=first, stop=last), inc="pe")
                                pbuf_g[i % 3] = b.cnt["pe"]

                            for i, ks in enumerate(ks_list):
                                k0 = ks * 128
                                co = max(0, k0 - q0)
                                sp = p_s2[i % 3]
                                diag = (k0 >= q0)
                                b.wait("tensor", "ac", sbank_g[i % 3])
                                b.op("tensor", lambda t, sp=sp, ks=ks, co=co, q0=q0, hb=hb: t.matmul(
                                    sp[:, co:512], Kbuf[hb][0:128, 128 * ks:128 * ks + 128],
                                    Qbuf[hb][0:128, q0 + co:q0 + 512],
                                    start=True, stop=True), inc="pe")
                                s_pe = b.cnt["pe"]
                                if diag:
                                    b.wait("vector", "pe", s_pe)
                                    b.op("vector", lambda v, sp=sp, co=co: v.tensor_add(
                                        sp[:, co:co + 128], sp[:, co:co + 128], s_tri[:]), inc="ve")
                                    b.wait("scalar", "ve")
                                else:
                                    b.wait("scalar", "pe", s_pe)
                                b.wait("scalar", "pe", pbuf_g[i % 3])
                                bias = np.float32(-SLOPES[h] * 128 * ks - SHIFT).item()
                                b.op("scalar", lambda s, sp=sp, co=co, i=i, bias=bias: s.activation(
                                    Pbuf[i % 3][:, co:512], sp[:, co:512], AF.Exp, bias=fc(bias)),
                                    inc="ac")
                                exp_at[i] = b.cnt["ac"]
                                sbank_g[i % 3] = b.cnt["ac"]
                                if i == min(2, nks - 1):
                                    flush_div()
                                if i >= 1:
                                    emit_av(i - 1, ks_list[i - 1], i - 1 == 0, False)
                            emit_av(nks - 1, ks_list[nks - 1], nks == 1, True)
                            av_pe = b.cnt["pe"]

                            def _div(pav=pav, av_pe=av_pe, prb_tgt=prb_g[0],
                                     h=h, q0=q0, qparity=qi % 2):
                                b.wait("scalar", "pe", av_pe)
                                b.op("scalar", lambda s, pav=pav: s.activation(
                                    rec_sb[:], pav[64:65, :], AF.Ln))
                                b.op("scalar", lambda s: s.activation(
                                    rec_sb[:], rec_sb[:], AF.Exp, scale=-1.0), inc="ac")
                                div_ac = b.cnt["ac"]
                                b.wait("vector", "ac", div_ac)
                                b.op("vector", lambda v: v.tensor_copy(
                                    s_hilo[:, 0:512], rec_sb[:]))
                                b.op("vector", lambda v: v.scalar_tensor_tensor(
                                    s_hilo[:, 512:1024], s_hilo[:, 0:512], -1.0, rec_sb[:],
                                    ALU.mult, ALU.add), inc="ve")
                                hilo_ve = b.cnt["ve"]
                                b.wait("tensor", "ve", max(hilo_ve, prb_tgt))
                                b.op("tensor", lambda t: t.matmul(
                                    p_rb[:], s_ones64b[:], s_hilo[:, 0:512],
                                    start=True, stop=False))
                                b.op("tensor", lambda t: t.matmul(
                                    p_rb[:], s_ones64b[:], s_hilo[:, 512:1024],
                                    start=False, stop=True), inc="pe")
                                b.wait("vector", "pe")
                                b.op("vector", lambda v: v.tensor_copy(rec64[:], p_rb[:]))
                                fo, ro = h // 2, 64 * (h % 2)
                                b.op("vector", lambda v, pav=pav, fo=fo, ro=ro, q0=q0: v.tensor_mul(
                                    aoT[ro:ro + 64, 4096 * fo + q0:4096 * fo + q0 + 512],
                                    pav[0:64, :], rec64[:]), inc="ve")
                                prb_g[0] = b.cnt["ve"]
                                pav_g[qparity] = b.cnt["ve"]

                            pending_div[0] = _div
                        last_head_pe[h] = b.cnt["pe"]
                    flush_div()
                b.junction()
                if DEBUG_TAPS:
                    b.dma("sync", d_dbg["dbg_ao"], aoT[:], "stx")
                with contextlib.ExitStack() as ps:
                    xc2 = [mk_sb(ps, "wxc", [128, 4096], F32) for _ in range(2)]
                    pw = [psum(ps, "pw", [128, 512], F32) for _ in range(2)]
                    wg = {}
                    for tc in range(NCHUNK):
                        par = tc % 2
                        t0 = tc * CH
                        b.wait_store("sync", ("wxc", par))
                        b.wait_store("sync", ("xT", tc))
                        b.dma("sync", xc2[par].ap().rearrange("p (f j) -> p f j", f=8),
                              d_xT[:, :, t0:t0 + CH].rearrange("f p j -> p f j"), "ldx")
                        b.wait("tensor", "ldx")
                        b.wait("tensor", "po", po_own)
                        for eo in range(8):
                            pp = pw[eo % 2]
                            for g in wg.get(eo % 2, []):
                                b.wait("tensor", g[0], g[1])
                            for c in range(8):
                                b.op("tensor", lambda t, pp=pp, c=c, eo=eo, t0=t0: t.matmul(
                                    pp[:], wslot_ap(W3, c, eo),
                                    aoT[:, 4096 * c + t0:4096 * c + t0 + 512],
                                    start=(c == 0), stop=(c == 7)), inc="pe" if c == 7 else None)
                            b.wait("vector", "pe")
                            b.op("vector", lambda v, pp=pp, eo=eo, par=par: v.tensor_add(
                                xc2[par][:, 512 * eo:512 * eo + 512], pp[:],
                                xc2[par][:, 512 * eo:512 * eo + 512]), inc="ve")
                            wg[eo % 2] = [("ve", b.cnt["ve"])]
                        b.wait("scalar", "ve")
                        b.store("scalar", d_xT[:, :, t0:t0 + CH].rearrange("f p j -> p f j"),
                                xc2[par].ap().rearrange("p (f j) -> p f j", f=8),
                                key=("wxc", par))
                        b.sdone[("xT", tc)] = b.cnt["stx"]
            b.junction()

        def sl_ffn(tag, po_guard, chunk_cbs=None, to_y=False):
            Wr_, Wkf_, Wvf_ = W0, W1, W2
            with contextlib.ExitStack() as ps:
                xnsh2 = [mk_sb(ps, "fxnsh", [128, 8 * 520], BF16) for _ in range(2)]
                scrB = mk_sb(ps, "fscrB", [128, 4096], BF16)
                scrC2 = [mk_sb(ps, "fscrC", [128, 4096], BF16) for _ in range(2)]
                scrD = mk_sb(ps, "fscrD", [128, 4096], BF16)
                ar = {
                    "xc": [mk_sb(ps, "fxc", [128, 4096], F32) for _ in range(2)],
                    "xb": [mk_sb(ps, "fxb", [128, 4096], BF16) for _ in range(2)],
                    "sq": mk_sb(ps, "fsq", [128, 4096], BF16),
                    "lnrow": mk_sb(ps, "flnr", [1, 1536], F32),
                    "name": "f" + tag,
                }
                tmp512 = mk_sb(ps, "ftmp", [128, 1024], F32)
                pb = [psum(ps, "pf", [128, 512], F32) for _ in range(3)]
                if to_y:
                    ar["xc_pe_free"] = {}
                    ytok = [mk_sb(ps, "fyt", [128, 1024], F32) for _ in range(2)]
                    p_try = [psum(ps, "ptry", [128, 512], F32) for _ in range(2)]
                    ytr_g = {}
                    addpo = {}
                xnd2 = [[xnsh2[p][:, 520 * f + 1:520 * f + 513] for f in range(NFT)]
                        for p in range(2)]
                mix_done = {}
                shift_po = {}
                tmp_po_g = {0: 0, 1: 0}
                guard = {}

                def emit_shift(npar):
                    dbuf, xnsh = ar["xb"][npar], xnsh2[npar]
                    b.wait("gpsimd", "ve")
                    b.wait("gpsimd", "pe", ar["sq_g"][0])
                    for f in range(NFT):
                        b.op("gpsimd", lambda g, dbuf=dbuf, xnsh=xnsh, f=f: g.tensor_sub(
                            dbuf[:, 512 * f:512 * f + 512],
                            xnsh[:, 520 * f + 1:520 * f + 513], xnsh[:, 520 * f:520 * f + 512]),
                            inc="po" if f == 7 else None)
                    shift_po[npar] = b.cnt["po"]

                def emit_mix(npar, scrB_pe_guard):
                    dbuf, xnsh = ar["xb"][npar], xnsh2[npar]
                    b.wait("vector", "po", shift_po[npar])
                    b.wait("vector", "pe", scrB_pe_guard)
                    for f in range(NFT):
                        b.op("vector", lambda v, dbuf=dbuf, xnsh=xnsh, f=f: v.scalar_tensor_tensor(
                            scrB[:, 512 * f:512 * f + 512], dbuf[:, 512 * f:512 * f + 512],
                            s_mur[:, f:f + 1], xnsh[:, 520 * f:520 * f + 512], ALU.mult, ALU.add))
                        b.op("vector", lambda v, dbuf=dbuf, xnsh=xnsh, f=f, npar=npar: v.scalar_tensor_tensor(
                            scrC2[npar][:, 512 * f:512 * f + 512], dbuf[:, 512 * f:512 * f + 512],
                            s_muk[:, f:f + 1], xnsh[:, 520 * f:520 * f + 512], ALU.mult, ALU.add),
                            inc="ve" if f == 7 else None)
                    ar["xb_g"][npar] = (b.cnt["pe"], b.cnt["ve"])
                    mix_done[npar] = b.cnt["ve"]

                for f in range(NFT):
                    b.op("vector", lambda v, f=f: v.memset(xnsh2[0][:, 520 * f:520 * f + 1], 0.0),
                         inc="ve" if f == 7 else None)
                load_chunk(ar, 0, 0, 0)
                emit_ln(ar, 0, xn_dest=xnd2[0])
                emit_shift(0)
                emit_mix(0, b.cnt["pe"])
                for tc in range(NCHUNK):
                    par = tc % 2
                    t0 = tc * CH
                    scrC = scrC2[par]
                    b.wait("tensor", "po", po_guard)
                    b.wait("tensor", "ve", mix_done[par])

                    def ev_k(eo, pp):
                        b.wait("vector", "pe")
                        b.op("vector", lambda v, pp=pp: v.tensor_scalar(
                            pp[:], pp[:], 0.0, None, ALU.max), inc="ve")
                        b.wait("scalar", "ve")
                        b.op("scalar", lambda s, pp=pp, eo=eo: s.activation(
                            scrD[:, 512 * eo:512 * eo + 512], pp[:],
                            AF.Square), inc="ac")

                    emit_proj_fm(Wkf_, scrC, ev_k, pb, guard)
                    if tc < NCHUNK - 1:
                        b.wait("vector", "po")
                        for f in range(NFT):
                            b.op("vector", lambda v, f=f, par=par: v.tensor_copy(
                                xnsh2[1 - par][:, 520 * f:520 * f + 1],
                                xnsh2[par][:, 520 * f + 512:520 * f + 513]),
                                inc="ve" if f == 7 else None)
                        load_chunk(ar, 1 - par, t0 + CH, tc + 1)
                        emit_ln(ar, 1 - par, xn_dest=xnd2[1 - par])
                        emit_shift(1 - par)

                    def ev_r(eo, pp):
                        b.wait("scalar", "pe")
                        b.op("scalar", lambda s, pp=pp, eo=eo, scrC=scrC: s.activation(
                            scrC[:, 512 * eo:512 * eo + 512], pp[:], AF.Sigmoid), inc="ac")

                    emit_proj_fm(Wr_, scrB, ev_r, pb, guard)
                    wr_pe = b.cnt["pe"]
                    b.wait("tensor", "ac")

                    def ev_o(eo, pp):
                        b.wait("vector", "pe")
                        b.wait("vector", "ac")
                        b.wait("vector", "po", tmp_po_g[eo % 2])
                        b.op("vector", lambda v, pp=pp, eo=eo, scrC=scrC: v.tensor_mul(
                            tmp512[:, 512 * (eo % 2):512 * (eo % 2) + 512], pp[:],
                            scrC[:, 512 * eo:512 * eo + 512]), inc="ve")
                        b.wait("gpsimd", "ve")
                        b.op("gpsimd", lambda g, eo=eo, par=par: g.tensor_add(
                            ar["xc"][par][:, 512 * eo:512 * eo + 512],
                            tmp512[:, 512 * (eo % 2):512 * (eo % 2) + 512],
                            ar["xc"][par][:, 512 * eo:512 * eo + 512]), inc="po")
                        tmp_po_g[eo % 2] = b.cnt["po"]
                        if to_y:
                            addpo[eo] = b.cnt["po"]

                    emit_proj_fm(Wvf_, scrD, ev_o, pb, guard)
                    if tc < NCHUNK - 1:
                        emit_mix(1 - par, wr_pe)
                    if to_y:
                        for it in range(4):
                            b.wait_store("vector", ("fyt", it % 2))
                            for f in range(NFT):
                                bk, co = f % 2, 128 * (f // 2)
                                b.wait("tensor", "po", addpo[f])
                                b.wait("tensor", "ve", ytr_g.get(bk, 0))
                                b.op("tensor", lambda t, par=par, f=f, it=it, bk=bk, co=co: t.transpose(
                                    p_try[bk][:, co:co + 128],
                                    ar["xc"][par][:, 512 * f + 128 * it:512 * f + 128 * it + 128],
                                    s_identf[:]), inc="pe")
                                b.wait("vector", "pe")
                                b.op("vector", lambda v, f=f, it=it, bk=bk, co=co: v.tensor_copy(
                                    ytok[it % 2][:, 128 * f:128 * f + 128],
                                    p_try[bk][:, co:co + 128]), inc="ve")
                                ytr_g[bk] = b.cnt["ve"]
                            b.wait("scalar", "ve")
                            b.store("scalar", d_y[t0 + 128 * it:t0 + 128 * it + 128, :],
                                    ytok[it % 2][:], key=("fyt", it % 2))
                        ar["xc_pe_free"][par] = b.cnt["pe"]
                    else:
                        b.wait("scalar", "po", b.cnt["po"])
                        b.store("scalar", d_xT[:, :, t0:t0 + CH].rearrange("f p j -> p f j"),
                                ar["xc"][par].ap().rearrange("p (f j) -> p f j", f=8),
                                key=(ar["name"], "xc", par))
                        b.sdone[("xT", tc)] = b.cnt["stx"]
                    ar["free_g"][par] = b.cnt["ve"]
                    if chunk_cbs and tc in chunk_cbs:
                        chunk_cbs[tc]()
            b.junction()

        def sl_attn_b():
            s1k, s1v = W3, W4
            with contextlib.ExitStack() as fs:
                kvsb = mk_sb(fs, "kvsb", [64, 16 * 65], F32)
                kvbf = mk_sb(fs, "kvbf", [128, 8 * 65], BF16)
                sl_attn_b_inner(s1k, s1v, kvsb, kvbf)

        def sl_attn_b_inner(s1k, s1v, kvsb, kvbf):
            with contextlib.ExitStack() as ps:
                scrB = mk_sb(ps, "bscrB", [128, 4096], BF16)
                scrC = mk_sb(ps, "bscrC", [128, 4 * 1088], BF16)
                scrE = mk_sb(ps, "bscrE", [128, 1024], F32)
                ar = {
                    "xc": [mk_sb(ps, "bxc", [128, 4096], F32) for _ in range(2)],
                    "xb": [mk_sb(ps, "bxb", [128, 4096], BF16) for _ in range(2)],
                    "xn": [mk_sb(ps, "bxn", [128, 4096], BF16) for _ in range(2)],
                    "sq": mk_sb(ps, "bsq", [128, 4096], BF16),
                    "lnrow": mk_sb(ps, "blnr", [1, 1536], F32),
                }
                pb = [psum(ps, "pk", [128, 512], F32) for _ in range(2)]
                pkv = [psum(ps, "pkv", [64, 65], F32) for _ in range(2)]
                b.op("vector", lambda v: v.memset(kvsb[:], 0.0), inc="ve")
                b.op("vector", lambda v: v.memset(scrC[:], 1.0), inc="ve")
                guard = {}
                kvg = {}
                load_chunk(ar, 0, 0, 0)
                emit_ln(ar, 0, xn_store_key=("bxn_b", 0))
                for tc in range(NCHUNK):
                    par = tc % 2
                    t0 = tc * CH
                    b.wait("scalar", "ve")
                    b.wait_store("scalar", ("bxn_b", par))
                    b.store("scalar", d_xnsp[:, :, t0:t0 + CH].rearrange("f p j -> p f j"),
                            ar["xn"][par].ap().rearrange("p (f j) -> p f j", f=8),
                            key=("bxn_b", par))
                    b.sdone[("bxn", tc)] = b.cnt["stx"]
                    b.wait("tensor", "po", b.cnt["po"])
                    b.wait("tensor", "ve", ar["xn_ready"][par])
                    for which in ("k", "v"):
                        if which == "v" and tc < NCHUNK - 1:
                            load_chunk(ar, 1 - par, t0 + CH, tc + 1)
                            emit_ln(ar, 1 - par, xn_store_key=("bxn_b", 1 - par))
                        slot = s1k if which == "k" else s1v
                        for eo in range(2):
                            for ts in range(4):
                                bi = (eo * 4 + ts) % 2
                                pp = pb[bi]
                                for g in guard.get(bi, []):
                                    b.wait("tensor", g[0], g[1])
                                for c in range(8):
                                    b.op("tensor", lambda t, pp=pp, c=c, ts=ts, eo=eo, slot=slot, par=par: t.matmul(
                                        pp[:], ar["xn"][par][:, 512 * c + 128 * ts:512 * c + 128 * ts + 128],
                                        slot[:, 1024 * c + 512 * eo:1024 * c + 512 * eo + 512],
                                        start=(c == 0), stop=(c == 7)), inc="pe" if c == 7 else None)
                                if which == "k":
                                    b.wait("vector", "pe")
                                    b.op("vector", lambda v, pp=pp: v.tensor_scalar(
                                        scrE[:, 0:512], pp[:], 0.0, None, ALU.min), inc="ve")
                                    b.wait("scalar", "ve")
                                    b.op("scalar", lambda s: s.activation(
                                        scrE[:, 512:1024], scrE[:, 0:512], AF.Exp), inc="ac")
                                    b.wait("vector", "ac")
                                    b.op("vector", lambda v, pp=pp, ts=ts, eo=eo: v.scalar_tensor_tensor(
                                        scrB[:, 1024 * ts + 512 * eo:1024 * ts + 512 * eo + 512],
                                        pp[:], 0.0, scrE[:, 512:1024], ALU.max, ALU.add), inc="ve")
                                else:
                                    b.wait("vector", "pe")
                                    if DEBUG_TAPS and tc == 7 and eo == 0 and ts == 0:
                                        b.op("vector", lambda v, pp=pp: v.tensor_copy(
                                            scrE[:, 0:512], pp[:]), inc="ve")
                                        b.wait("scalar", "ve")
                                        b.dma("scalar", d_dbg["dbg_praw"], scrE[:, 0:512], "stx")
                                        b.dma("scalar", d_dbg["dbg_w1"], W1[:], "stx")
                                        b.dma("scalar", d_dbg["dbg_xnc7"], ar["xn"][par][:], "stx")
                                    for i8 in range(8):
                                        base = 1088 * ts + 544 * eo + 68 * i8
                                        b.op("vector", lambda v, pp=pp, base=base, i8=i8: v.tensor_copy(
                                            scrC[:, base:base + 64], pp[:, 64 * i8:64 * i8 + 64]),
                                            inc="ve" if i8 == 7 else None)
                                guard[bi] = [("ve", b.cnt["ve"]), ("ac", b.cnt["ac"])]
                    b.wait("tensor", "ve")
                    for hh in range(H):
                        pk = pkv[hh % 2]
                        for g in kvg.get(hh % 2, []):
                            b.wait("tensor", g[0], g[1])
                        for ts in range(4):
                            b.op("tensor", lambda t, pk=pk, hh=hh, ts=ts: t.matmul(
                                pk[0:64, 0:65],
                                scrB[:, 1024 * ts + 64 * hh:1024 * ts + 64 * hh + 64],
                                scrC[:, 1088 * ts + 68 * hh:1088 * ts + 68 * hh + 65],
                                start=(ts == 0), stop=(ts == 3)), inc="pe" if ts == 3 else None)
                        b.wait("vector", "pe")
                        b.op("vector", lambda v, pk=pk, hh=hh: v.tensor_add(
                            kvsb[:, 65 * hh:65 * hh + 65], kvsb[:, 65 * hh:65 * hh + 65],
                            pk[0:64, :]), inc="ve")
                        kvg[hh % 2] = [("ve", b.cnt["ve"])]
                        if hh == H - 1:
                            ar["free_g"][par] = b.cnt["ve"]
                            ar["xn_pe_g"][par] = b.cnt["pe"]
                if DEBUG_TAPS:
                    b.junction()
                    b.dma("sync", d_dbg["dbg_xnsp"], d_xnsp, "stx")
                    b.dma("sync", d_dbg["dbg_kv"], kvsb[:], "stx")
                    b.dma("sync", d_dbg["dbg_ktok"], scrB[:], "stx")
                    b.dma("sync", d_dbg["dbg_vaug"], scrC[:], "stx")
                    b.wait("sync", "stx", b.cnt["stx"])
            b.junction()
            load_weight(W3, "bq")
            load_weight(W4, "bo")
            with contextlib.ExitStack() as ps:
                ar2 = {
                    "xc": [mk_sb(ps, "cxc", [128, 4096], F32) for _ in range(2)],
                    "xn": [mk_sb(ps, "cxn", [128, 4096], BF16) for _ in range(2)],
                }
                scrB2 = mk_sb(ps, "cscrB", [128, 4096], BF16)
                scrD2 = mk_sb(ps, "cscrD", [128, 4096], BF16)
                scrE2 = mk_sb(ps, "cscrE", [128, 1024], F32)
                t_ln2 = [mk_sb(ps, "tln2", [1, 512], F32) for _ in range(2)]
                rec64p = [mk_sb(ps, "rec64b", [64, 512], F32) for _ in range(2)]
                s_hilo2 = [mk_sb(ps, "hilo2", [1, 1024], BF16) for _ in range(2)]
                outc2 = [mk_sb(ps, "coc", [128, 4096], F32) for _ in range(2)]
                pb2 = [psum(ps, "pq", [128, 512], F32) for _ in range(2)]
                pnp = [(psum(ps, "pnA2", [128, 512], F32),
                        psum(ps, "pnB2", [128, 512], F32)) for _ in range(2)]
                pw2x = [psum(ps, "pw2", [128, 512], F32) for _ in range(2)]
                pn_g = {0: (0, 0), 1: (0, 0)}
                rec_g = {0: 0, 1: 0}
                pending_wo = [None]
                wo_g = [0]
                div_ve_g = [0]

                def flush_wo():
                    if pending_wo[0] is not None:
                        pending_wo[0]()
                        pending_wo[0] = None

                b.wait("vector", "ve")
                for pr in range(8):
                    b.op("vector", lambda v, pr=pr: v.tensor_copy(
                        kvbf[0:64, 65 * pr:65 * pr + 65],
                        kvsb[0:64, 130 * pr:130 * pr + 65]),
                        inc="ve" if pr == 7 else None)
                b.wait("gpsimd", "ve")
                b.dma("gpsimd",
                      kvbf[64:128, :].rearrange("p (pr c) -> p pr c", c=65),
                      kvsb.ap().rearrange("p (pr two c) -> p two pr c", two=2, c=65)[:, 1, :, :],
                      "ldw")
                b.wait("tensor", "ldw", b.cnt["ldw"])
                b.wait("vector", "ldw", b.cnt["ldw"])
                guard2 = {}
                wg2 = {}
                for tc in range(NCHUNK):
                    par = tc % 2
                    t0 = tc * CH
                    load_chunk(ar2, par, t0, tc, also_xn_sp=True)
                    b.wait("tensor", "ldx")
                    b.wait("tensor", "po", b.cnt["po"])

                    evq_ve = {}

                    def ev_q(eo, pp):
                        b.wait("vector", "pe")
                        b.op("vector", lambda v, pp=pp: v.tensor_scalar(
                            scrE2[:, 0:512], pp[:], 0.0, None, ALU.min), inc="ve")
                        b.wait("scalar", "ve")
                        b.op("scalar", lambda s: s.activation(
                            scrE2[:, 512:1024], scrE2[:, 0:512], AF.Exp), inc="ac")
                        b.wait("vector", "ac")
                        b.op("vector", lambda v, pp=pp, eo=eo: v.scalar_tensor_tensor(
                            scrB2[:, 512 * eo:512 * eo + 512], pp[:], 0.0,
                            scrE2[:, 512:1024], ALU.max, ALU.add), inc="ve")
                        evq_ve[eo] = b.cnt["ve"]

                    emit_proj_fm(W3, ar2["xn"][par], ev_q, pb2, guard2)
                    flush_wo()
                    pending_bc = [None]
                    for pr in range(8):
                        pnA2, pnB2 = pnp[pr % 2]
                        g_ac, g_ve = pn_g[pr % 2]
                        b.wait("tensor", "ve", max(g_ve, evq_ve[pr]))
                        b.wait("tensor", "ac", g_ac)
                        b.op("tensor", lambda t, pr=pr, pnA2=pnA2: t.matmul(
                            pnA2[0:65, :], kvbf[0:64, 65 * pr:65 * pr + 65],
                            scrB2[0:64, 512 * pr:512 * pr + 512],
                            start=True, stop=True), inc="pe")
                        b.op("tensor", lambda t, pr=pr, pnB2=pnB2: t.matmul(
                            pnB2[0:65, :], kvbf[64:128, 65 * pr:65 * pr + 65],
                            scrB2[64:128, 512 * pr:512 * pr + 512],
                            start=True, stop=True, tile_position=(64, 0)), inc="pe")
                        num_pe = b.cnt["pe"]
                        if pending_bc[0] is not None:
                            pending_bc[0]()
                        b.wait("scalar", "pe", num_pe)
                        for two, pn in ((0, pnA2), (1, pnB2)):
                            b.op("scalar", lambda s, pn=pn, tl=t_ln2[two]: s.activation(
                                tl[:], pn[64:65, :], AF.Ln, bias=ATTNB_EPS), inc="ac")
                        ln_ac = b.cnt["ac"]
                        for two in (0, 1):
                            b.wait("vector", "ac", ln_ac - 1 + two)
                            b.op("vector", lambda v, tl=t_ln2[two], sh=s_hilo2[two]:
                                 v.tensor_copy(sh[:, 0:512], tl[:]))
                            b.op("vector", lambda v, tl=t_ln2[two], sh=s_hilo2[two]:
                                 v.scalar_tensor_tensor(
                                     sh[:, 512:1024], sh[:, 0:512], -1.0, tl[:],
                                     ALU.mult, ALU.add), inc="ve")
                        hilo_ve = b.cnt["ve"]

                        def _bc(pr=pr, pnA2=pnA2, pnB2=pnB2, hilo_ve=hilo_ve):
                            for two, pn in ((0, pnA2), (1, pnB2)):
                                b.wait("tensor", "ve", hilo_ve - 1 + two)
                                b.op("tensor", lambda t, pn=pn, sh=s_hilo2[two]: t.matmul(
                                    pn[64:128, :], s_ones64b[:], sh[:, 0:512],
                                    start=True, stop=False))
                                b.op("tensor", lambda t, pn=pn, sh=s_hilo2[two]: t.matmul(
                                    pn[64:128, :], s_ones64b[:], sh[:, 512:1024],
                                    start=False, stop=True), inc="pe")
                            bc_pe = b.cnt["pe"]
                            for two, pn in ((0, pnA2), (1, pnB2)):
                                hh = 2 * pr + two
                                b.wait("scalar", "pe", bc_pe - 1 + two)
                                b.wait("scalar", "ve", rec_g[two])
                                b.op("scalar", lambda s, pn=pn, rc=rec64p[two]:
                                     s.activation(rc[:], pn[64:128, :], AF.Exp,
                                                  scale=-1.0), inc="ac")
                                b.wait("vector", "ac")
                                b.wait("vector", "pe", wo_g[0])
                                fo, ro = hh // 2, 64 * (hh % 2)
                                b.op("vector", lambda v, pn=pn, fo=fo, ro=ro,
                                     rc=rec64p[two]: v.tensor_mul(
                                         scrD2[ro:ro + 64, 512 * fo:512 * fo + 512],
                                         pn[0:64, :], rc[:]), inc="ve")
                                rec_g[two] = b.cnt["ve"]
                            pn_g[pr % 2] = (b.cnt["ac"], b.cnt["ve"])

                        pending_bc[0] = _bc
                    pending_bc[0]()
                    div_end_ve = b.cnt["ve"]

                    def _wo(par=par, t0=t0, tc=tc, div_end_ve=div_end_ve):
                        b.wait("tensor", "ve", div_end_ve)
                        b.wait_store("vector", ("coc", par))
                        for eo in range(8):
                            pp = pw2x[eo % 2]
                            for g in wg2.get(eo % 2, []):
                                b.wait("tensor", g[0], g[1])
                            for c in range(8):
                                b.op("tensor", lambda t, pp=pp, c=c, eo=eo: t.matmul(
                                    pp[:], wslot_ap(W4, c, eo), scrD2[:, 512 * c:512 * c + 512],
                                    start=(c == 0), stop=(c == 7)), inc="pe" if c == 7 else None)
                            b.wait("vector", "pe")
                            b.op("vector", lambda v, pp=pp, eo=eo, par=par: v.tensor_add(
                                outc2[par][:, 512 * eo:512 * eo + 512], pp[:],
                                ar2["xc"][par][:, 512 * eo:512 * eo + 512]), inc="ve")
                            wg2[eo % 2] = [("ve", b.cnt["ve"])]
                        wo_g[0] = b.cnt["pe"]
                        b.wait("scalar", "ve")
                        b.store("scalar", d_xT[:, :, t0:t0 + CH].rearrange("f p j -> p f j"),
                                outc2[par].ap().rearrange("p (f j) -> p f j", f=8),
                                key=("coc", par))
                        b.sdone[("xT", tc)] = b.cnt["stx"]
                        ar2["free_g"][par] = b.cnt["ve"]
                        div_ve_g[0] = div_end_ve

                    pending_wo[0] = _wo
                flush_wo()
            b.junction()

        def final_out():
            with contextlib.ExitStack() as ps:
                xc = [mk_sb(ps, "zxc", [128, 1024], F32) for _ in range(2)]
                ytok = [mk_sb(ps, "ytok", [128, 1024], F32) for _ in range(2)]
                p_tr = [psum(ps, "pft", [128, 128], F32) for _ in range(2)]
                fin_pe = {}
                for tt in range(NTT):
                    t0 = tt * 128
                    buf = xc[tt % 2]
                    b.wait_store("sync", ("xT", tt // 4))
                    b.wait("sync", "pe", fin_pe.get(tt % 2, 0))
                    b.dma("sync", buf.ap().rearrange("p (f j) -> p f j", f=8),
                          d_xT[:, :, t0:t0 + 128].rearrange("f p j -> p f j"), "ldx")
                    b.wait("tensor", "ldx")
                    b.wait_store("vector", ("ytok", tt % 2))
                    for f in range(NFT):
                        pb2 = p_tr[f % 2]
                        b.wait("tensor", "ve", b.cnt["ve"] - 1)
                        b.op("tensor", lambda t, pb2=pb2, buf=buf, f=f: t.transpose(
                            pb2[:], buf[:, 128 * f:128 * f + 128], s_identf[:]), inc="pe")
                        b.wait("vector", "pe")
                        b.op("vector", lambda v, pb2=pb2, f=f, tt=tt: v.tensor_copy(
                            ytok[tt % 2][:, 128 * f:128 * f + 128], pb2[:]), inc="ve")
                    fin_pe[tt % 2] = b.cnt["pe"]
                    b.wait("scalar", "ve")
                    b.store("scalar", d_y[t0:t0 + 128, :], ytok[tt % 2][:],
                            key=("ytok", tt % 2))
            b.wait("sync", "stx")

        po_ffn = [0]

        def _prefetch_ffn():
            load_weight(W0, "fr")
            load_weight(W1, "fk")
            load_weight(W2, "fv")
            po_ffn[0] = b.cnt["po"]

        if N_SL >= 1:
            sl_attn_a(prefetch_after_b=_prefetch_ffn if N_SL >= 2 else None)
        if N_SL >= 2:
            cbs = None
            if N_SL >= 3:
                cbs = {1: lambda: load_weight(W3, "bk"),
                       3: lambda: load_weight(W4, "bv", pe_guard=0)}
            sl_ffn("f1", po_ffn[0], chunk_cbs=cbs)
        if N_SL >= 3:
            sl_attn_b()
        if N_SL >= 4:
            sl_ffn("f2", po_ffn[0], to_y=True)
        else:
            final_out()

        with nc.Block() as blk:
            @blk.sync
            def _(e):
                for fn in b.bodies["sync"]:
                    fn(e)

            @blk.tensor
            def _(e):
                for fn in b.bodies["tensor"]:
                    fn(e)

            @blk.vector
            def _(e):
                for fn in b.bodies["vector"]:
                    fn(e)

            @blk.scalar
            def _(e):
                for fn in b.bodies["scalar"]:
                    fn(e)

            @blk.gpsimd
            def _(e):
                for fn in b.bodies["gpsimd"]:
                    fn(e)

    return nc


def kernel(L0_attn_a_x, L0_attn_a_Wq, L0_attn_a_Wk, L0_attn_a_Wv, L0_attn_a_Wo,
           L0_attn_b_Wq, L0_attn_b_Wk, L0_attn_b_Wv, L0_attn_b_Wo,
           L0_ffn_W_r, L0_ffn_W_k, L0_ffn_W_v, L0_ffn_mu_r, L0_ffn_mu_k):
    x = np.asarray(L0_attn_a_x, dtype=np.float32)
    ws = {
        "aq": L0_attn_a_Wq, "ak": L0_attn_a_Wk, "av": L0_attn_a_Wv, "ao": L0_attn_a_Wo,
        "bq": L0_attn_b_Wq, "bk": L0_attn_b_Wk, "bv": L0_attn_b_Wv, "bo": L0_attn_b_Wo,
        "fr": L0_ffn_W_r, "fk": L0_ffn_W_k, "fv": L0_ffn_W_v,
    }
    wT = {("w_" + k): np.ascontiguousarray(np.asarray(v, dtype=np.float32).T)
          for k, v in ws.items()}
    mu_r = np.ascontiguousarray(np.asarray(L0_ffn_mu_r, np.float32).reshape(8, 128).T)
    mu_k = np.ascontiguousarray(np.asarray(L0_ffn_mu_k, np.float32).reshape(8, 128).T)

    nc = build_program()
    in_maps = []
    for b_ in range(8):
        m = {"x": np.ascontiguousarray(x[b_])}
        m.update(wT)
        m["mu_r"] = mu_r
        m["mu_k"] = mu_k
        in_maps.append(m)
    res = run_bass_kernel_spmd(nc, in_maps, core_ids=list(range(8)))
    kernel._last_results = res.results
    out = np.stack([res.results[b_]["y"] for b_ in range(8)], axis=0)
    return out


# version 21
# speedup vs baseline: 1.0388x; 1.0072x over previous
import contextlib
import math
import numpy as np
import ml_dtypes
import concourse.bass as bass
import concourse.mybir as mybir
from concourse.bass_utils import run_bass_kernel_spmd

F32 = mybir.dt.float32
BF16 = mybir.dt.bfloat16
AF = mybir.ActivationFunctionType
ALU = mybir.AluOpType
BF = ml_dtypes.bfloat16

S = 4096
D = 1024
H = 16
HD = 64
NCHUNK = 8
CH = 512
NFT = 8
NTT = 32
SHIFT = 12.0
ALIBI_M = 20.0
LN_EPS = 1e-5
ATTNB_EPS = 1e-6
DEBUG_TAPS = False
N_SL = 4

SLOPES = [2.0 ** (-(i + 1) / 2.0) for i in range(H)]
CUTOFF = [ALIBI_M / s for s in SLOPES]


def kept_ks_range(h, qi):
    cap = int(math.floor((ALIBI_M + 20.0) / SLOPES[h] / 128.0))
    ks_hi = min(4 * qi + 3, cap, NTT - 1)
    return 0, ks_hi


def exact_split3(v):
    hi = 4.0 * np.floor(v / 4.0)
    rem = v - hi
    mid = 0.125 * np.floor(rem / 0.125)
    lo = rem - mid
    return hi, mid, lo


class Builder:
    ENGS = ("sync", "tensor", "vector", "scalar", "gpsimd")

    def __init__(self, nc):
        self.nc = nc
        self.bodies = {e: [] for e in self.ENGS}
        self.sems = {}
        self.cnt = {}
        self.sdone = {}

    def sem(self, stack, name):
        self.sems[name] = stack.enter_context(self.nc.semaphore(name))
        self.cnt[name] = 0

    def emit(self, eng, fn):
        self.bodies[eng].append(fn)

    def wait(self, eng, semname, val=None):
        v = self.cnt[semname] if val is None else val
        if v <= 0:
            return
        s = self.sems[semname]
        self.emit(eng, lambda e, s=s, v=v: e.wait_ge(s, v))

    def inc(self, semname, n=1):
        self.cnt[semname] += n
        return self.sems[semname]

    def op(self, eng, fn, inc=None, incn=1):
        if inc is not None:
            s = self.inc(inc, incn)
            self.emit(eng, lambda e, fn=fn, s=s, n=incn: fn(e).then_inc(s, n))
        else:
            self.emit(eng, fn)

    def dma(self, eng, out, in_, inc):
        s = self.inc(inc, 16)
        self.emit(eng, lambda e, o=out, i=in_, s=s: e.dma_start(out=o, in_=i).then_inc(s, 16))

    def dma_count(self, n):
        return 16 * n

    def store(self, eng, out, in_, key=None):
        self.dma(eng, out, in_, "stx")
        if key is not None:
            self.sdone[key] = self.cnt["stx"]

    def wait_store(self, eng, key):
        if key in self.sdone:
            self.wait(eng, "stx", self.sdone[key])

    def junction(self):
        for eng in self.ENGS:
            for sname in ("pe", "ve", "ac", "po", "stx"):
                self.wait(eng, sname)


def build_program():
    nc = bass.Bass("TRN2", target_bir_lowering=False)

    d_x = nc.dram_tensor("x", [S, D], F32, kind="ExternalInput").ap()
    d_y = nc.dram_tensor("y", [S, D], F32, kind="ExternalOutput").ap()
    wnames = ["aq", "ak", "av", "ao", "bq", "bk", "bv", "bo", "fr", "fk", "fv"]
    d_w = {n: nc.dram_tensor("w_" + n, [D, D], F32, kind="ExternalInput").ap()
           for n in wnames}
    d_mur = nc.dram_tensor("mu_r", [128, 8], F32, kind="ExternalInput").ap()
    d_muk = nc.dram_tensor("mu_k", [128, 8], F32, kind="ExternalInput").ap()

    d_xT = nc.dram_tensor("xT_scr", [NFT, 128, S], F32, kind="Internal").ap()
    d_Qsp = nc.dram_tensor("Q_sp", [H, HD, S], BF16, kind="Internal").ap()
    d_Ksp = nc.dram_tensor("K_sp", [H, HD, S], BF16, kind="Internal").ap()
    d_Vsp = nc.dram_tensor("V_sp", [H, S, HD], BF16, kind="Internal").ap()
    d_xnsp = nc.dram_tensor("xn_sp", [NFT, 128, S], BF16, kind="Internal").ap()
    d_dbg = {}
    if DEBUG_TAPS:
        d_dbg["dbg_Q"] = nc.dram_tensor("dbg_Q", [H, HD, S], BF16, kind="ExternalOutput").ap()
        d_dbg["dbg_K"] = nc.dram_tensor("dbg_K", [H, HD, S], BF16, kind="ExternalOutput").ap()
        d_dbg["dbg_V"] = nc.dram_tensor("dbg_V", [H, S, HD], BF16, kind="ExternalOutput").ap()
        d_dbg["dbg_ao"] = nc.dram_tensor("dbg_ao", [128, 8 * 4096], BF16, kind="ExternalOutput").ap()
        d_dbg["dbg_xnsp"] = nc.dram_tensor("dbg_xnsp", [NFT, 128, S], BF16, kind="ExternalOutput").ap()
        d_dbg["dbg_kv"] = nc.dram_tensor("dbg_kv", [128, 8 * 65], F32, kind="ExternalOutput").ap()
        d_dbg["dbg_ktok"] = nc.dram_tensor("dbg_ktok", [128, 4096], BF16, kind="ExternalOutput").ap()
        d_dbg["dbg_vaug"] = nc.dram_tensor("dbg_vaug", [128, 4352], BF16, kind="ExternalOutput").ap()
        d_dbg["dbg_praw"] = nc.dram_tensor("dbg_praw", [128, 512], F32, kind="ExternalOutput").ap()
        d_dbg["dbg_w1"] = nc.dram_tensor("dbg_w1", [128, 8192], BF16, kind="ExternalOutput").ap()
        d_dbg["dbg_xnc7"] = nc.dram_tensor("dbg_xnc7", [128, 4096], BF16, kind="ExternalOutput").ap()

    ident_f = nc.inline_tensor(np.eye(128, dtype=np.float32), "ident_f").ap()
    ones_col_b = nc.inline_tensor(np.ones((128, 1), dtype=np.float32).astype(BF), "ones_col_b").ap()
    ones_1x128_f = nc.inline_tensor(np.ones((1, 128), dtype=np.float32), "ones_1x128_f").ap()
    ones_1x64_f = nc.inline_tensor(np.ones((1, 64), dtype=np.float32), "ones_1x64_f").ap()
    ones_1x64_b = nc.inline_tensor(np.ones((1, 64), dtype=np.float32).astype(BF), "ones_1x64_b").ap()
    ones3_b = nc.inline_tensor(np.ones((3, S), dtype=np.float32).astype(BF), "ones3_b").ap()
    tri_np = np.where(np.arange(128)[:, None] > np.arange(128)[None, :], -1e9, 0.0).astype(np.float32)
    tri_m = nc.inline_tensor(tri_np, "tri_m").ap()

    kal = np.zeros((H, 3, S), dtype=np.float64)
    for h in range(H):
        vk = SLOPES[h] * (np.arange(S) % 128).astype(np.float64)
        hi, mid, lo = exact_split3(vk)
        kal[h, 0], kal[h, 1], kal[h, 2] = -hi, -mid, -lo
    d_kal = nc.inline_tensor(kal.astype(np.float32).astype(BF), "kal").ap()

    fvals_raw = [0.0, 1.0, -1.0, 1.0 / D, LN_EPS, 0.125, -SHIFT, ATTNB_EPS]
    for h in range(H):
        for qi in range(NCHUNK):
            ks_lo, ks_hi = kept_ks_range(h, qi)
            for ks in range(ks_lo, ks_hi + 1):
                fvals_raw.append(-SLOPES[h] * 128 * ks - SHIFT)
    fidx = {}
    fcols = []
    for v in fvals_raw:
        v32 = np.float32(v).item()
        if v32 not in fidx:
            fidx[v32] = len(fcols)
            fcols.append(v32)
        fidx[v] = fidx[v32]
    d_fpool = nc.inline_tensor(
        np.tile(np.array([fcols], dtype=np.float32), (128, 1)), "fpool").ap()

    uid = [0]

    with contextlib.ExitStack() as st:
        b = Builder(nc)
        for s in ("ldx", "ldw", "ldh", "ldm", "stx", "pe", "ve", "ac", "po"):
            b.sem(st, s)

        def mk_sb(stack, name, shape, dt):
            uid[0] += 1
            return stack.enter_context(nc.sbuf_tensor(f"{name}_{uid[0]}", shape, dt))

        s_fpool = mk_sb(st, "fpool", [128, len(fcols)], F32)
        s_identf = mk_sb(st, "identf", [128, 128], F32)
        s_onescol = mk_sb(st, "onescol", [128, 1], BF16)
        s_ones128 = mk_sb(st, "ones128", [1, 128], F32)
        s_ones64 = mk_sb(st, "ones64", [1, 64], F32)
        s_ones64b = mk_sb(st, "ones64b", [1, 64], BF16)
        s_tri = mk_sb(st, "tri", [128, 128], F32)
        s_mur = mk_sb(st, "mur", [128, 8], F32)
        s_muk = mk_sb(st, "muk", [128, 8], F32)
        W0 = mk_sb(st, "W0", [128, 8192], BF16)
        W1 = mk_sb(st, "W1", [128, 8192], BF16)
        W2 = mk_sb(st, "W2", [128, 8192], BF16)
        W3 = mk_sb(st, "W3", [128, 8192], BF16)
        W4 = mk_sb(st, "W4", [128, 8192], BF16)

        for v, i in fidx.items():
            nc.const_aps.register(F32, v, s_fpool[:, i:i + 1])

        def fc(v):
            return s_fpool[:, fidx[v]:fidx[v] + 1]

        b.dma("sync", s_fpool[:], d_fpool, "ldm")
        b.dma("sync", s_identf[:], ident_f, "ldm")
        b.dma("sync", s_onescol[:], ones_col_b, "ldm")
        b.dma("sync", s_ones128[:], ones_1x128_f, "ldm")
        b.dma("sync", s_ones64[:], ones_1x64_f, "ldm")
        b.dma("sync", s_ones64b[:], ones_1x64_b, "ldm")
        b.dma("sync", s_tri[:], tri_m, "ldm")
        b.dma("sync", s_mur[:], d_mur, "ldm")
        b.dma("sync", s_muk[:], d_muk, "ldm")
        for eng in ("tensor", "vector", "scalar", "gpsimd"):
            b.wait(eng, "ldm")

        def psum(stack2, name, shape, dt=F32):
            uid[0] += 1
            return stack2.enter_context(nc.psum_tensor(f"{name}_{uid[0]}", shape, dt))

        def load_weight(slot, wname, pe_guard=None):
            b.wait("gpsimd", "pe", b.cnt["pe"] if pe_guard is None else pe_guard)
            b.dma("gpsimd", slot.ap().rearrange("p (c d) -> p c d", c=8),
                  d_w[wname].rearrange("(c p) d -> p c d", p=128), "po", )

        def wslot_ap(slot, c, eo):
            return slot[:, 1024 * c + 128 * eo: 1024 * c + 128 * eo + 128]

        def emit_ln(ar, par, xn_dest=None, xn_store_key=None):
            xcb, xbb = ar["xc"][par], ar["xb"][par]
            ln_row = ar["lnrow"]
            if xn_dest is None:
                xnb = ar["xn"][par]
                xn_dest = [xnb[:, 512 * f:512 * f + 512] for f in range(NFT)]
            with contextlib.ExitStack() as ps:
                p_m = psum(ps, "lnm", [128, 512], F32)
                p_r = psum(ps, "lnr", [128, 512], F32)
                p_s = p_m[0:1, :]
                p_q = p_r[0:1, :]
                xg = ar.setdefault("xb_g", {})
                b.wait("gpsimd", "pe", xg.get(par, (0, 0))[0])
                b.wait("gpsimd", "ve", xg.get(par, (0, 0))[1])
                b.wait("gpsimd", "pe", ar.setdefault("sq_g", [0])[0])
                xcr = ar.get("xc_ready")
                if xcr is not None:
                    b.wait("gpsimd", "ve", xcr[par])
                else:
                    b.wait("gpsimd", "ldx")
                b.op("gpsimd", lambda g: g.tensor_copy(xbb[:], xcb[:]), inc="po")
                ar.setdefault("cast_po", {})[par] = b.cnt["po"]
                b.op("gpsimd", lambda g: g.tensor_mul(ar["sq"][:], xbb[:], xbb[:]), inc="po")
                b.wait("tensor", "po")
                for f in range(NFT):
                    b.op("tensor", lambda t, f=f: t.matmul(
                        p_s, s_onescol[:], xbb[:, 512 * f:512 * f + 512],
                        start=(f == 0), stop=(f == 7)), inc="pe" if f == 7 else None)
                for f in range(NFT):
                    b.op("tensor", lambda t, f=f: t.matmul(
                        p_q, s_onescol[:], ar["sq"][:, 512 * f:512 * f + 512],
                        start=(f == 0), stop=(f == 7)), inc="pe" if f == 7 else None)
                ar["sq_g"][0] = b.cnt["pe"]
                b.wait("vector", "pe")
                b.op("vector", lambda v: v.tensor_scalar(
                    ln_row[:, 512:1024], p_s, 1.0 / D, None, ALU.mult))
                b.op("vector", lambda v: v.tensor_mul(
                    ln_row[:, 1024:1536], ln_row[:, 512:1024], ln_row[:, 512:1024]))
                b.op("vector", lambda v: v.scalar_tensor_tensor(
                    ln_row[:, 0:512], ln_row[:, 1024:1536], -float(D), p_q,
                    ALU.mult, ALU.add), inc="ve")
                b.wait("scalar", "ve")
                b.op("scalar", lambda s: s.activation(
                    ln_row[:, 0:512], ln_row[:, 0:512], AF.Ln,
                    bias=LN_EPS, scale=1.0 / D))
                b.op("scalar", lambda s: s.activation(
                    ln_row[:, 0:512], ln_row[:, 0:512], AF.Exp, scale=-0.5), inc="ac")
                b.wait("tensor", "ac")
                b.op("tensor", lambda t: t.matmul(p_m[:], s_ones128[:], ln_row[:, 512:1024],
                                                  start=True, stop=True))
                b.op("tensor", lambda t: t.matmul(p_r[:], s_ones128[:], ln_row[:, 0:512],
                                                  start=True, stop=True), inc="pe")
                b.wait("vector", "pe", ar.setdefault("xn_pe_g", {}).get(par, 0))
                b.wait("vector", "pe")
                b.wait("vector", "po")
                if xn_store_key is not None:
                    b.wait_store("vector", xn_store_key)
                for f in range(NFT):
                    dst = xn_dest[f]
                    b.op("vector", lambda v, dst=dst, f=f: v.tensor_sub(
                        dst, xbb[:, 512 * f:512 * f + 512], p_m[:]))
                    b.op("vector", lambda v, dst=dst: v.tensor_mul(
                        dst, dst, p_r[:]), inc="ve" if f == 7 else None)
                ar["xb_g"][par] = (b.cnt["pe"], b.cnt["ve"])
            ar.setdefault("xn_ready", {})[par] = b.cnt["ve"]

        def emit_proj_fm(slot, rhs_buf, evac_fn, pbanks, guard):
            nb = len(pbanks)
            for eo in range(8):
                pp = pbanks[eo % nb]
                for g in guard.get(eo % nb, []):
                    b.wait("tensor", g[0], g[1])
                for c in range(8):
                    b.op("tensor", lambda t, pp=pp, c=c, eo=eo: t.matmul(
                        pp[:], wslot_ap(slot, c, eo),
                        rhs_buf[:, 512 * c:512 * c + 512],
                        start=(c == 0), stop=(c == 7)), inc="pe" if c == 7 else None)
                evac_fn(eo, pp)
                guard[eo % nb] = [("ve", b.cnt["ve"]), ("ac", b.cnt["ac"])]

        def load_chunk(ar, par, t0, tc, also_xn_sp=False):
            b.wait_store("sync", ("xT", tc))
            fg = ar.setdefault("free_g", {})
            b.wait("sync", "ve", fg.get(par, 0))
            xpf = ar.get("xc_pe_free")
            if xpf is not None:
                b.wait("sync", "pe", xpf.get(par, 0))
            if "name" in ar:
                b.wait_store("sync", (ar["name"], "xc", par))
            if also_xn_sp:
                b.wait_store("sync", ("bxn", tc))
                b.dma("sync", ar["xn"][par].ap().rearrange("p (f j) -> p f j", f=8),
                      d_xnsp[:, :, t0:t0 + CH].rearrange("f p j -> p f j"), "ldx")
            b.dma("sync", ar["xc"][par].ap().rearrange("p (f j) -> p f j", f=8),
                  d_xT[:, :, t0:t0 + CH].rearrange("f p j -> p f j"), "ldx")

        def sl_attn_a(prefetch_after_b=None):
            load_weight(W0, "aq", pe_guard=0)
            load_weight(W1, "ak", pe_guard=0)
            load_weight(W2, "av", pe_guard=0)
            load_weight(W3, "ao", pe_guard=0)
            with contextlib.ExitStack() as ps:
                scrB = mk_sb(ps, "ascrB", [128, 4096], BF16)
                scrC = mk_sb(ps, "ascrC", [128, 4096], BF16)
                scrD = mk_sb(ps, "ascrD", [128, 4096], BF16)
                ar = {
                    "xc": [mk_sb(ps, "axc", [128, 4096], F32) for _ in range(2)],
                    "xb": [mk_sb(ps, "axb", [128, 4096], BF16) for _ in range(2)],
                    "xn": [mk_sb(ps, "axn", [128, 4096], BF16) for _ in range(2)],
                    "sq": mk_sb(ps, "asq", [128, 4096], BF16),
                    "lnrow": mk_sb(ps, "alnr", [1, 1536], F32),
                    "xc_ready": {},
                    "free_g": {},
                }
                tkb = [mk_sb(ps, "atkb", [128, 1024], F32) for _ in range(4)]
                pb = [psum(ps, "pa", [128, 512], F32) for _ in range(2)]
                pv = [psum(ps, "pv", [128, 512], F32) for _ in range(2)]
                p_tr = [psum(ps, "ptrA", [128, 512], F32) for _ in range(2)]
                guard = {}
                vguard = {}
                tkb_pe = {}
                ptr_g = {}

                def trans_chunk(par, t0, tc):
                    xcb = ar["xc"][par]
                    b.wait("vector", "po", ar.get("cast_po", {}).get(par, 0))
                    b.wait_store("vector", ("atr", par))
                    for it in range(4):
                        tt = 4 * tc + it
                        tb = tkb[tt % 4]
                        b.wait("sync", "pe", tkb_pe.get(tt % 4, 0))
                        b.dma("sync", tb[:], d_x[t0 + 128 * it:t0 + 128 * it + 128, :], "ldx")
                        b.wait("tensor", "ldx")
                        for f in range(NFT):
                            bk, co = f % 2, 128 * (f // 2)
                            b.wait("tensor", "ve", ptr_g.get(bk, 0))
                            b.op("tensor", lambda t, tb=tb, f=f, bk=bk, co=co: t.transpose(
                                p_tr[bk][:, co:co + 128],
                                tb[:, 128 * f:128 * f + 128], s_identf[:]), inc="pe")
                            b.wait("vector", "pe")
                            b.op("vector", lambda v, xcb=xcb, f=f, it=it, bk=bk, co=co: v.tensor_copy(
                                xcb[:, 512 * f + 128 * it:512 * f + 128 * it + 128],
                                p_tr[bk][:, co:co + 128]), inc="ve")
                            ptr_g[bk] = b.cnt["ve"]
                        tkb_pe[tt % 4] = b.cnt["pe"]
                    ar["xc_ready"][par] = b.cnt["ve"]
                    b.wait("scalar", "ve")
                    b.store("scalar", d_xT[:, :, t0:t0 + CH].rearrange("f p j -> p f j"),
                            xcb.ap().rearrange("p (f j) -> p f j", f=8),
                            key=("atr", par))
                    b.sdone[("xT", tc)] = b.cnt["stx"]

                trans_chunk(0, 0, 0)
                emit_ln(ar, 0)
                for tc in range(NCHUNK):
                    par = tc % 2
                    t0 = tc * CH
                    b.wait("tensor", "po", b.cnt["po"])
                    b.wait("tensor", "ve", ar["xn_ready"][par])
                    b.wait_store("scalar", "aQK")

                    def ev_q(eo, pp):
                        b.wait("scalar", "pe")
                        b.op("scalar", lambda s, pp=pp, eo=eo: s.activation(
                            scrB[:, 512 * eo:512 * eo + 512], pp[:], AF.Copy,
                            scale=0.125), inc="ac")

                    def ev_k(eo, pp):
                        b.wait("scalar", "pe")
                        b.op("scalar", lambda s, pp=pp, eo=eo: s.activation(
                            scrC[:, 512 * eo:512 * eo + 512], pp[:], AF.Copy), inc="ac")

                    emit_proj_fm(W0, ar["xn"][par], ev_q, pb, guard)
                    if tc < NCHUNK - 1:
                        trans_chunk(1 - par, t0 + CH, tc + 1)
                        emit_ln(ar, 1 - par)
                    emit_proj_fm(W1, ar["xn"][par], ev_k, pb, guard)
                    b.wait("scalar", "ac")
                    b.store("scalar",
                            d_Qsp.rearrange("(e two) hd t -> e (two hd) t", two=2)[:, :, t0:t0 + CH].rearrange("e p t -> p e t"),
                            scrB.ap().rearrange("p (e t) -> p e t", e=8))
                    b.store("scalar",
                            d_Ksp.rearrange("(e two) hd t -> e (two hd) t", two=2)[:, :, t0:t0 + CH].rearrange("e p t -> p e t"),
                            scrC.ap().rearrange("p (e t) -> p e t", e=8), key="aQK")
                    b.wait_store("vector", "aV")
                    for eo in range(2):
                        for ts in range(4):
                            bi = (eo * 4 + ts) % 2
                            pp = pv[bi]
                            for g in vguard.get(bi, []):
                                b.wait("tensor", g[0], g[1])
                            for c in range(8):
                                b.op("tensor", lambda t, pp=pp, c=c, ts=ts, eo=eo, par=par: t.matmul(
                                    pp[:], ar["xn"][par][:, 512 * c + 128 * ts:512 * c + 128 * ts + 128],
                                    W2[:, 1024 * c + 512 * eo:1024 * c + 512 * eo + 512],
                                    start=(c == 0), stop=(c == 7)), inc="pe" if c == 7 else None)
                            b.wait("vector", "pe")
                            b.op("vector", lambda v, pp=pp, ts=ts, eo=eo: v.tensor_copy(
                                scrD[:, 1024 * ts + 512 * eo:1024 * ts + 512 * eo + 512], pp[:]),
                                inc="ve")
                            vguard[bi] = [("ve", b.cnt["ve"])]
                    b.wait("scalar", "ve")
                    for ts in range(4):
                        b.store("scalar",
                                d_Vsp[:, t0 + 128 * ts:t0 + 128 * ts + 128, :].rearrange("h p hd -> p h hd"),
                                scrD[:, 1024 * ts:1024 * ts + 1024].rearrange("p (h hd) -> p h hd", h=16),
                                key="aV" if ts == 3 else None)
                    ar["free_g"][par] = b.cnt["ve"]
                    ar["xn_pe_g"][par] = b.cnt["pe"]
            po_own = b.cnt["po"]
            if prefetch_after_b:
                prefetch_after_b()
            if DEBUG_TAPS:
                b.wait("sync", "stx", b.cnt["stx"])
                b.dma("sync", d_dbg["dbg_Q"], d_Qsp, "stx")
                b.dma("sync", d_dbg["dbg_K"], d_Ksp, "stx")
                b.dma("sync", d_dbg["dbg_V"], d_Vsp, "stx")
            b.junction()
            with contextlib.ExitStack() as outer:
                aoT = mk_sb(outer, "aoT", [128, 8 * 4096], BF16)
                with contextlib.ExitStack() as ps:
                    Kbuf = [mk_sb(ps, "Kb", [128, S], BF16) for _ in range(2)]
                    Qbuf = [mk_sb(ps, "Qb", [128, S], BF16) for _ in range(2)]
                    Vbuf = [mk_sb(ps, "Vb", [128, 32 * 66], BF16) for _ in range(2)]
                    Pbuf = [mk_sb(ps, "Pb", [128, 512], BF16) for _ in range(3)]
                    rec_sb = mk_sb(ps, "recsb", [1, 512], F32)
                    rec64 = mk_sb(ps, "rec64", [64, 512], F32)
                    s_hilo = mk_sb(ps, "hilo", [1, 1024], BF16)
                    p_s2 = [psum(ps, "ps", [128, 512], F32) for _ in range(3)]
                    p_av = [psum(ps, "pav", [65, 512], F32) for _ in range(2)]
                    p_rb = psum(ps, "prb", [64, 512], F32)
                    for bb in range(2):
                        b.op("vector", lambda v, bb=bb: v.memset(Vbuf[bb][:], 1.0))
                        b.op("vector", lambda v, bb=bb: v.memset(Kbuf[bb][64:128, :], 0.0))
                        b.op("vector", lambda v, bb=bb: v.memset(Qbuf[bb][64:128, :], 0.0), inc="ve")
                    b.wait("sync", "ve", b.cnt["ve"])
                    for bb in range(2):
                        b.dma("sync", Qbuf[bb][64:67, :], ones3_b, "ldm")
                    b.wait("tensor", "ldm", b.cnt["ldm"])
                    last_head_pe = {}
                    sbank_g = {0: 0, 1: 0, 2: 0}
                    pbuf_g = {0: 0, 1: 0, 2: 0}
                    pav_g = {0: 0, 1: 0}
                    prb_g = [0]
                    pending_div = [None]

                    def flush_div():
                        if pending_div[0] is not None:
                            pending_div[0]()
                            pending_div[0] = None
                    for h in range(H):
                        hb = h % 2
                        if h >= 2:
                            b.wait("sync", "pe", last_head_pe[h - 2])
                        b.dma("sync", Kbuf[hb][0:64, :], d_Ksp[h], "ldh")
                        b.dma("sync", Kbuf[hb][64:67, :], d_kal[h], "ldh")
                        b.dma("sync", Qbuf[hb][0:64, :], d_Qsp[h], "ldh")
                        b.dma("sync", Vbuf[hb].ap().rearrange("p (ts c) -> p ts c", c=66)[:, :, 0:64],
                              d_Vsp[h].rearrange("(ts p) hd -> p ts hd", p=128), "ldh")
                        b.wait("tensor", "ldh")
                        for qi in range(NCHUNK):
                            q0 = qi * CH
                            ks_lo, ks_hi = kept_ks_range(h, qi)
                            ks_list = list(range(ks_lo, ks_hi + 1))
                            nks = len(ks_list)
                            pav = p_av[qi % 2]
                            b.wait("tensor", "ve", pav_g[qi % 2])
                            exp_at = {}
                            av_emitted = [0]

                            def emit_av(i, ks, first, last, pav=pav, hb=hb):
                                co = max(0, 128 * ks - q0)
                                b.wait("tensor", "ac", exp_at[i])
                                b.op("tensor", lambda t, pav=pav, i=i, ks=ks, co=co, hb=hb,
                                     first=first, last=last: t.matmul(
                                    pav[0:65, co:512], Vbuf[hb][:, 66 * ks:66 * ks + 65],
                                    Pbuf[i % 3][:, co:512],
                                    start=first, stop=last), inc="pe")
                                pbuf_g[i % 3] = b.cnt["pe"]

                            for i, ks in enumerate(ks_list):
                                k0 = ks * 128
                                co = max(0, k0 - q0)
                                sp = p_s2[i % 3]
                                diag = (k0 >= q0)
                                b.wait("tensor", "ac", sbank_g[i % 3])
                                b.op("tensor", lambda t, sp=sp, ks=ks, co=co, q0=q0, hb=hb: t.matmul(
                                    sp[:, co:512], Kbuf[hb][0:128, 128 * ks:128 * ks + 128],
                                    Qbuf[hb][0:128, q0 + co:q0 + 512],
                                    start=True, stop=True), inc="pe")
                                s_pe = b.cnt["pe"]
                                if diag:
                                    b.wait("vector", "pe", s_pe)
                                    b.op("vector", lambda v, sp=sp, co=co: v.tensor_add(
                                        sp[:, co:co + 128], sp[:, co:co + 128], s_tri[:]), inc="ve")
                                    b.wait("scalar", "ve")
                                else:
                                    b.wait("scalar", "pe", s_pe)
                                b.wait("scalar", "pe", pbuf_g[i % 3])
                                bias = np.float32(-SLOPES[h] * 128 * ks - SHIFT).item()
                                b.op("scalar", lambda s, sp=sp, co=co, i=i, bias=bias: s.activation(
                                    Pbuf[i % 3][:, co:512], sp[:, co:512], AF.Exp, bias=fc(bias)),
                                    inc="ac")
                                exp_at[i] = b.cnt["ac"]
                                sbank_g[i % 3] = b.cnt["ac"]
                                if i == min(2, nks - 1):
                                    flush_div()
                                if i >= 1:
                                    emit_av(i - 1, ks_list[i - 1], i - 1 == 0, False)
                            emit_av(nks - 1, ks_list[nks - 1], nks == 1, True)
                            av_pe = b.cnt["pe"]

                            def _div(pav=pav, av_pe=av_pe, prb_tgt=prb_g[0],
                                     h=h, q0=q0, qparity=qi % 2):
                                b.wait("scalar", "pe", av_pe)
                                b.op("scalar", lambda s, pav=pav: s.activation(
                                    rec_sb[:], pav[64:65, :], AF.Ln))
                                b.op("scalar", lambda s: s.activation(
                                    rec_sb[:], rec_sb[:], AF.Exp, scale=-1.0), inc="ac")
                                div_ac = b.cnt["ac"]
                                b.wait("vector", "ac", div_ac)
                                b.op("vector", lambda v: v.tensor_copy(
                                    s_hilo[:, 0:512], rec_sb[:]))
                                b.op("vector", lambda v: v.scalar_tensor_tensor(
                                    s_hilo[:, 512:1024], s_hilo[:, 0:512], -1.0, rec_sb[:],
                                    ALU.mult, ALU.add), inc="ve")
                                hilo_ve = b.cnt["ve"]
                                b.wait("tensor", "ve", max(hilo_ve, prb_tgt))
                                b.op("tensor", lambda t: t.matmul(
                                    p_rb[:], s_ones64b[:], s_hilo[:, 0:512],
                                    start=True, stop=False))
                                b.op("tensor", lambda t: t.matmul(
                                    p_rb[:], s_ones64b[:], s_hilo[:, 512:1024],
                                    start=False, stop=True), inc="pe")
                                b.wait("vector", "pe")
                                b.op("vector", lambda v: v.tensor_copy(rec64[:], p_rb[:]))
                                fo, ro = h // 2, 64 * (h % 2)
                                b.op("vector", lambda v, pav=pav, fo=fo, ro=ro, q0=q0: v.tensor_mul(
                                    aoT[ro:ro + 64, 4096 * fo + q0:4096 * fo + q0 + 512],
                                    pav[0:64, :], rec64[:]), inc="ve")
                                prb_g[0] = b.cnt["ve"]
                                pav_g[qparity] = b.cnt["ve"]

                            pending_div[0] = _div
                        last_head_pe[h] = b.cnt["pe"]
                    flush_div()
                b.junction()
                if DEBUG_TAPS:
                    b.dma("sync", d_dbg["dbg_ao"], aoT[:], "stx")
                with contextlib.ExitStack() as ps:
                    xc2 = [mk_sb(ps, "wxc", [128, 4096], F32) for _ in range(2)]
                    pw = [psum(ps, "pw", [128, 512], F32) for _ in range(2)]
                    wg = {}
                    for tc in range(NCHUNK):
                        par = tc % 2
                        t0 = tc * CH
                        b.wait_store("sync", ("wxc", par))
                        b.wait_store("sync", ("xT", tc))
                        b.dma("sync", xc2[par].ap().rearrange("p (f j) -> p f j", f=8),
                              d_xT[:, :, t0:t0 + CH].rearrange("f p j -> p f j"), "ldx")
                        b.wait("tensor", "ldx")
                        b.wait("tensor", "po", po_own)
                        for eo in range(8):
                            pp = pw[eo % 2]
                            for g in wg.get(eo % 2, []):
                                b.wait("tensor", g[0], g[1])
                            for c in range(8):
                                b.op("tensor", lambda t, pp=pp, c=c, eo=eo, t0=t0: t.matmul(
                                    pp[:], wslot_ap(W3, c, eo),
                                    aoT[:, 4096 * c + t0:4096 * c + t0 + 512],
                                    start=(c == 0), stop=(c == 7)), inc="pe" if c == 7 else None)
                            b.wait("vector", "pe")
                            b.op("vector", lambda v, pp=pp, eo=eo, par=par: v.tensor_add(
                                xc2[par][:, 512 * eo:512 * eo + 512], pp[:],
                                xc2[par][:, 512 * eo:512 * eo + 512]), inc="ve")
                            wg[eo % 2] = [("ve", b.cnt["ve"])]
                        b.wait("scalar", "ve")
                        b.store("scalar", d_xT[:, :, t0:t0 + CH].rearrange("f p j -> p f j"),
                                xc2[par].ap().rearrange("p (f j) -> p f j", f=8),
                                key=("wxc", par))
                        b.sdone[("xT", tc)] = b.cnt["stx"]
            b.junction()

        def sl_ffn(tag, po_guard, chunk_cbs=None, to_y=False):
            Wr_, Wkf_, Wvf_ = W0, W1, W2
            with contextlib.ExitStack() as ps:
                xnsh2 = [mk_sb(ps, "fxnsh", [128, 8 * 520], BF16) for _ in range(2)]
                scrB = mk_sb(ps, "fscrB", [128, 4096], BF16)
                scrC2 = [mk_sb(ps, "fscrC", [128, 4096], BF16) for _ in range(2)]
                scrD = mk_sb(ps, "fscrD", [128, 4096], BF16)
                ar = {
                    "xc": [mk_sb(ps, "fxc", [128, 4096], F32) for _ in range(2)],
                    "xb": [mk_sb(ps, "fxb", [128, 4096], BF16) for _ in range(2)],
                    "sq": mk_sb(ps, "fsq", [128, 4096], BF16),
                    "lnrow": mk_sb(ps, "flnr", [1, 1536], F32),
                    "name": "f" + tag,
                }
                tmp512 = mk_sb(ps, "ftmp", [128, 1024], F32)
                pb = [psum(ps, "pf", [128, 512], F32) for _ in range(3)]
                if to_y:
                    ar["xc_pe_free"] = {}
                    ytok = [mk_sb(ps, "fyt", [128, 1024], F32) for _ in range(2)]
                    p_try = [psum(ps, "ptry", [128, 512], F32) for _ in range(2)]
                    ytr_g = {}
                    addpo = {}
                xnd2 = [[xnsh2[p][:, 520 * f + 1:520 * f + 513] for f in range(NFT)]
                        for p in range(2)]
                mix_done = {}
                shift_po = {}
                tmp_po_g = {0: 0, 1: 0}
                guard = {}

                def emit_shift(npar):
                    dbuf, xnsh = ar["xb"][npar], xnsh2[npar]
                    b.wait("gpsimd", "ve")
                    b.wait("gpsimd", "pe", ar["sq_g"][0])
                    for f in range(NFT):
                        b.op("gpsimd", lambda g, dbuf=dbuf, xnsh=xnsh, f=f: g.tensor_sub(
                            dbuf[:, 512 * f:512 * f + 512],
                            xnsh[:, 520 * f + 1:520 * f + 513], xnsh[:, 520 * f:520 * f + 512]),
                            inc="po" if f == 7 else None)
                    shift_po[npar] = b.cnt["po"]

                def emit_mix(npar, scrB_pe_guard):
                    dbuf, xnsh = ar["xb"][npar], xnsh2[npar]
                    b.wait("vector", "po", shift_po[npar])
                    b.wait("vector", "pe", scrB_pe_guard)
                    for f in range(NFT):
                        b.op("vector", lambda v, dbuf=dbuf, xnsh=xnsh, f=f: v.scalar_tensor_tensor(
                            scrB[:, 512 * f:512 * f + 512], dbuf[:, 512 * f:512 * f + 512],
                            s_mur[:, f:f + 1], xnsh[:, 520 * f:520 * f + 512], ALU.mult, ALU.add))
                        b.op("vector", lambda v, dbuf=dbuf, xnsh=xnsh, f=f, npar=npar: v.scalar_tensor_tensor(
                            scrC2[npar][:, 512 * f:512 * f + 512], dbuf[:, 512 * f:512 * f + 512],
                            s_muk[:, f:f + 1], xnsh[:, 520 * f:520 * f + 512], ALU.mult, ALU.add),
                            inc="ve" if f == 7 else None)
                    ar["xb_g"][npar] = (b.cnt["pe"], b.cnt["ve"])
                    mix_done[npar] = b.cnt["ve"]

                for f in range(NFT):
                    b.op("vector", lambda v, f=f: v.memset(xnsh2[0][:, 520 * f:520 * f + 1], 0.0),
                         inc="ve" if f == 7 else None)
                load_chunk(ar, 0, 0, 0)
                emit_ln(ar, 0, xn_dest=xnd2[0])
                emit_shift(0)
                emit_mix(0, b.cnt["pe"])
                for tc in range(NCHUNK):
                    par = tc % 2
                    t0 = tc * CH
                    scrC = scrC2[par]
                    b.wait("tensor", "po", po_guard)
                    b.wait("tensor", "ve", mix_done[par])

                    def ev_k(eo, pp):
                        b.wait("vector", "pe")
                        b.op("vector", lambda v, pp=pp: v.tensor_scalar(
                            pp[:], pp[:], 0.0, None, ALU.max), inc="ve")
                        b.wait("scalar", "ve")
                        b.op("scalar", lambda s, pp=pp, eo=eo: s.activation(
                            scrD[:, 512 * eo:512 * eo + 512], pp[:],
                            AF.Square), inc="ac")

                    emit_proj_fm(Wkf_, scrC, ev_k, pb, guard)
                    if tc < NCHUNK - 1:
                        b.wait("vector", "po")
                        for f in range(NFT):
                            b.op("vector", lambda v, f=f, par=par: v.tensor_copy(
                                xnsh2[1 - par][:, 520 * f:520 * f + 1],
                                xnsh2[par][:, 520 * f + 512:520 * f + 513]),
                                inc="ve" if f == 7 else None)
                        load_chunk(ar, 1 - par, t0 + CH, tc + 1)
                        emit_ln(ar, 1 - par, xn_dest=xnd2[1 - par])
                        emit_shift(1 - par)

                    def ev_r(eo, pp):
                        b.wait("scalar", "pe")
                        b.op("scalar", lambda s, pp=pp, eo=eo, scrC=scrC: s.activation(
                            scrC[:, 512 * eo:512 * eo + 512], pp[:], AF.Sigmoid), inc="ac")

                    emit_proj_fm(Wr_, scrB, ev_r, pb, guard)
                    wr_pe = b.cnt["pe"]
                    b.wait("tensor", "ac")

                    def ev_o(eo, pp):
                        b.wait("vector", "pe")
                        b.wait("vector", "ac")
                        b.wait("vector", "po", tmp_po_g[eo % 2])
                        b.op("vector", lambda v, pp=pp, eo=eo, scrC=scrC: v.tensor_mul(
                            tmp512[:, 512 * (eo % 2):512 * (eo % 2) + 512], pp[:],
                            scrC[:, 512 * eo:512 * eo + 512]), inc="ve")
                        b.wait("gpsimd", "ve")
                        b.op("gpsimd", lambda g, eo=eo, par=par: g.tensor_add(
                            ar["xc"][par][:, 512 * eo:512 * eo + 512],
                            tmp512[:, 512 * (eo % 2):512 * (eo % 2) + 512],
                            ar["xc"][par][:, 512 * eo:512 * eo + 512]), inc="po")
                        tmp_po_g[eo % 2] = b.cnt["po"]
                        if to_y:
                            addpo[eo] = b.cnt["po"]

                    emit_proj_fm(Wvf_, scrD, ev_o, pb, guard)
                    if tc < NCHUNK - 1:
                        emit_mix(1 - par, wr_pe)
                    if to_y:
                        for it in range(4):
                            b.wait_store("vector", ("fyt", it % 2))
                            for f in range(NFT):
                                bk, co = f % 2, 128 * (f // 2)
                                b.wait("tensor", "po", addpo[f])
                                b.wait("tensor", "ve", ytr_g.get(bk, 0))
                                b.op("tensor", lambda t, par=par, f=f, it=it, bk=bk, co=co: t.transpose(
                                    p_try[bk][:, co:co + 128],
                                    ar["xc"][par][:, 512 * f + 128 * it:512 * f + 128 * it + 128],
                                    s_identf[:]), inc="pe")
                                b.wait("vector", "pe")
                                b.op("vector", lambda v, f=f, it=it, bk=bk, co=co: v.tensor_copy(
                                    ytok[it % 2][:, 128 * f:128 * f + 128],
                                    p_try[bk][:, co:co + 128]), inc="ve")
                                ytr_g[bk] = b.cnt["ve"]
                            b.wait("scalar", "ve")
                            b.store("scalar", d_y[t0 + 128 * it:t0 + 128 * it + 128, :],
                                    ytok[it % 2][:], key=("fyt", it % 2))
                        ar["xc_pe_free"][par] = b.cnt["pe"]
                    else:
                        b.wait("scalar", "po", b.cnt["po"])
                        b.store("scalar", d_xT[:, :, t0:t0 + CH].rearrange("f p j -> p f j"),
                                ar["xc"][par].ap().rearrange("p (f j) -> p f j", f=8),
                                key=(ar["name"], "xc", par))
                        b.sdone[("xT", tc)] = b.cnt["stx"]
                    ar["free_g"][par] = b.cnt["ve"]
                    if chunk_cbs and tc in chunk_cbs:
                        chunk_cbs[tc]()
            b.junction()

        def sl_attn_b():
            s1k, s1v = W3, W4
            with contextlib.ExitStack() as fs:
                kvsb = mk_sb(fs, "kvsb", [64, 16 * 65], F32)
                kvbf = mk_sb(fs, "kvbf", [128, 8 * 65], BF16)
                sl_attn_b_inner(s1k, s1v, kvsb, kvbf)

        def sl_attn_b_inner(s1k, s1v, kvsb, kvbf):
            with contextlib.ExitStack() as ps:
                scrB = mk_sb(ps, "bscrB", [128, 4096], BF16)
                scrC = mk_sb(ps, "bscrC", [128, 4 * 1088], BF16)
                scrE = mk_sb(ps, "bscrE", [128, 1024], F32)
                ar = {
                    "xc": [mk_sb(ps, "bxc", [128, 4096], F32) for _ in range(2)],
                    "xb": [mk_sb(ps, "bxb", [128, 4096], BF16) for _ in range(2)],
                    "xn": [mk_sb(ps, "bxn", [128, 4096], BF16) for _ in range(2)],
                    "sq": mk_sb(ps, "bsq", [128, 4096], BF16),
                    "lnrow": mk_sb(ps, "blnr", [1, 1536], F32),
                }
                pb = [psum(ps, "pk", [128, 512], F32) for _ in range(2)]
                pkv = [psum(ps, "pkv", [64, 65], F32) for _ in range(2)]
                b.op("vector", lambda v: v.memset(kvsb[:], 0.0), inc="ve")
                b.op("vector", lambda v: v.memset(scrC[:], 1.0), inc="ve")
                guard = {}
                kvg = {}
                load_chunk(ar, 0, 0, 0)
                emit_ln(ar, 0, xn_store_key=("bxn_b", 0))
                for tc in range(NCHUNK):
                    par = tc % 2
                    t0 = tc * CH
                    b.wait("scalar", "ve")
                    b.wait_store("scalar", ("bxn_b", par))
                    b.store("scalar", d_xnsp[:, :, t0:t0 + CH].rearrange("f p j -> p f j"),
                            ar["xn"][par].ap().rearrange("p (f j) -> p f j", f=8),
                            key=("bxn_b", par))
                    b.sdone[("bxn", tc)] = b.cnt["stx"]
                    b.wait("tensor", "po", b.cnt["po"])
                    b.wait("tensor", "ve", ar["xn_ready"][par])
                    for which in ("k", "v"):
                        if which == "v" and tc < NCHUNK - 1:
                            load_chunk(ar, 1 - par, t0 + CH, tc + 1)
                            emit_ln(ar, 1 - par, xn_store_key=("bxn_b", 1 - par))
                        slot = s1k if which == "k" else s1v
                        for eo in range(2):
                            for ts in range(4):
                                bi = (eo * 4 + ts) % 2
                                pp = pb[bi]
                                for g in guard.get(bi, []):
                                    b.wait("tensor", g[0], g[1])
                                for c in range(8):
                                    b.op("tensor", lambda t, pp=pp, c=c, ts=ts, eo=eo, slot=slot, par=par: t.matmul(
                                        pp[:], ar["xn"][par][:, 512 * c + 128 * ts:512 * c + 128 * ts + 128],
                                        slot[:, 1024 * c + 512 * eo:1024 * c + 512 * eo + 512],
                                        start=(c == 0), stop=(c == 7)), inc="pe" if c == 7 else None)
                                if which == "k":
                                    b.wait("vector", "pe")
                                    b.op("vector", lambda v, pp=pp: v.tensor_scalar(
                                        scrE[:, 0:512], pp[:], 0.0, None, ALU.min), inc="ve")
                                    b.wait("scalar", "ve")
                                    b.op("scalar", lambda s: s.activation(
                                        scrE[:, 512:1024], scrE[:, 0:512], AF.Exp), inc="ac")
                                    b.wait("vector", "ac")
                                    b.op("vector", lambda v, pp=pp, ts=ts, eo=eo: v.scalar_tensor_tensor(
                                        scrB[:, 1024 * ts + 512 * eo:1024 * ts + 512 * eo + 512],
                                        pp[:], 0.0, scrE[:, 512:1024], ALU.max, ALU.add), inc="ve")
                                else:
                                    b.wait("vector", "pe")
                                    if DEBUG_TAPS and tc == 7 and eo == 0 and ts == 0:
                                        b.op("vector", lambda v, pp=pp: v.tensor_copy(
                                            scrE[:, 0:512], pp[:]), inc="ve")
                                        b.wait("scalar", "ve")
                                        b.dma("scalar", d_dbg["dbg_praw"], scrE[:, 0:512], "stx")
                                        b.dma("scalar", d_dbg["dbg_w1"], W1[:], "stx")
                                        b.dma("scalar", d_dbg["dbg_xnc7"], ar["xn"][par][:], "stx")
                                    for i8 in range(8):
                                        base = 1088 * ts + 544 * eo + 68 * i8
                                        b.op("vector", lambda v, pp=pp, base=base, i8=i8: v.tensor_copy(
                                            scrC[:, base:base + 64], pp[:, 64 * i8:64 * i8 + 64]),
                                            inc="ve" if i8 == 7 else None)
                                guard[bi] = [("ve", b.cnt["ve"]), ("ac", b.cnt["ac"])]
                    b.wait("tensor", "ve")
                    for hh in range(H):
                        pk = pkv[hh % 2]
                        for g in kvg.get(hh % 2, []):
                            b.wait("tensor", g[0], g[1])
                        for ts in range(4):
                            b.op("tensor", lambda t, pk=pk, hh=hh, ts=ts: t.matmul(
                                pk[0:64, 0:65],
                                scrB[:, 1024 * ts + 64 * hh:1024 * ts + 64 * hh + 64],
                                scrC[:, 1088 * ts + 68 * hh:1088 * ts + 68 * hh + 65],
                                start=(ts == 0), stop=(ts == 3)), inc="pe" if ts == 3 else None)
                        b.wait("vector", "pe")
                        b.op("vector", lambda v, pk=pk, hh=hh: v.tensor_add(
                            kvsb[:, 65 * hh:65 * hh + 65], kvsb[:, 65 * hh:65 * hh + 65],
                            pk[0:64, :]), inc="ve")
                        kvg[hh % 2] = [("ve", b.cnt["ve"])]
                        if hh == H - 1:
                            ar["free_g"][par] = b.cnt["ve"]
                            ar["xn_pe_g"][par] = b.cnt["pe"]
                if DEBUG_TAPS:
                    b.junction()
                    b.dma("sync", d_dbg["dbg_xnsp"], d_xnsp, "stx")
                    b.dma("sync", d_dbg["dbg_kv"], kvsb[:], "stx")
                    b.dma("sync", d_dbg["dbg_ktok"], scrB[:], "stx")
                    b.dma("sync", d_dbg["dbg_vaug"], scrC[:], "stx")
                    b.wait("sync", "stx", b.cnt["stx"])
            b.junction()
            load_weight(W3, "bq")
            load_weight(W4, "bo")
            with contextlib.ExitStack() as ps:
                ar2 = {
                    "xc": [mk_sb(ps, "cxc", [128, 4096], F32) for _ in range(2)],
                    "xn": [mk_sb(ps, "cxn", [128, 4096], BF16) for _ in range(2)],
                }
                scrB2 = mk_sb(ps, "cscrB", [128, 4096], BF16)
                scrD2 = mk_sb(ps, "cscrD", [128, 4096], BF16)
                scrE2 = mk_sb(ps, "cscrE", [128, 1024], F32)
                t_ln2 = [mk_sb(ps, "tln2", [1, 512], F32) for _ in range(2)]
                rec64p = [mk_sb(ps, "rec64b", [64, 512], F32) for _ in range(2)]
                s_hilo2 = [mk_sb(ps, "hilo2", [1, 1024], BF16) for _ in range(2)]
                outc2 = [mk_sb(ps, "coc", [128, 4096], F32) for _ in range(2)]
                pb2 = [psum(ps, "pq", [128, 512], F32) for _ in range(2)]
                pnp = [(psum(ps, "pnA2", [128, 512], F32),
                        psum(ps, "pnB2", [128, 512], F32)) for _ in range(2)]
                pw2x = [psum(ps, "pw2", [128, 512], F32) for _ in range(2)]
                pn_g = {0: (0, 0), 1: (0, 0)}
                rec_g = {0: 0, 1: 0}
                pending_wo = [None]
                wo_g = [0]
                div_ve_g = [0]

                def flush_wo():
                    if pending_wo[0] is not None:
                        pending_wo[0]()
                        pending_wo[0] = None

                b.wait("vector", "ve")
                for pr in range(8):
                    b.op("vector", lambda v, pr=pr: v.tensor_copy(
                        kvbf[0:64, 65 * pr:65 * pr + 65],
                        kvsb[0:64, 130 * pr:130 * pr + 65]),
                        inc="ve" if pr == 7 else None)
                b.wait("gpsimd", "ve")
                b.dma("gpsimd",
                      kvbf[64:128, :].rearrange("p (pr c) -> p pr c", c=65),
                      kvsb.ap().rearrange("p (pr two c) -> p two pr c", two=2, c=65)[:, 1, :, :],
                      "ldw")
                b.wait("tensor", "ldw", b.cnt["ldw"])
                b.wait("vector", "ldw", b.cnt["ldw"])
                guard2 = {}
                wg2 = {}
                for tc in range(NCHUNK):
                    par = tc % 2
                    t0 = tc * CH
                    load_chunk(ar2, par, t0, tc, also_xn_sp=True)
                    b.wait("tensor", "ldx")
                    b.wait("tensor", "po", b.cnt["po"])

                    evq_ve = {}

                    def ev_q(eo, pp):
                        b.wait("vector", "pe")
                        b.op("vector", lambda v, pp=pp: v.tensor_scalar(
                            scrE2[:, 0:512], pp[:], 0.0, None, ALU.min), inc="ve")
                        b.wait("scalar", "ve")
                        b.op("scalar", lambda s: s.activation(
                            scrE2[:, 512:1024], scrE2[:, 0:512], AF.Exp), inc="ac")
                        b.wait("vector", "ac")
                        b.op("vector", lambda v, pp=pp, eo=eo: v.scalar_tensor_tensor(
                            scrB2[:, 512 * eo:512 * eo + 512], pp[:], 0.0,
                            scrE2[:, 512:1024], ALU.max, ALU.add), inc="ve")
                        evq_ve[eo] = b.cnt["ve"]

                    emit_proj_fm(W3, ar2["xn"][par], ev_q, pb2, guard2)
                    flush_wo()
                    pending_bc = [None]
                    for pr in range(8):
                        pnA2, pnB2 = pnp[pr % 2]
                        g_ac, g_ve = pn_g[pr % 2]
                        b.wait("tensor", "ve", max(g_ve, evq_ve[pr]))
                        b.wait("tensor", "ac", g_ac)
                        b.op("tensor", lambda t, pr=pr, pnA2=pnA2: t.matmul(
                            pnA2[0:65, :], kvbf[0:64, 65 * pr:65 * pr + 65],
                            scrB2[0:64, 512 * pr:512 * pr + 512],
                            start=True, stop=True), inc="pe")
                        b.op("tensor", lambda t, pr=pr, pnB2=pnB2: t.matmul(
                            pnB2[0:65, :], kvbf[64:128, 65 * pr:65 * pr + 65],
                            scrB2[64:128, 512 * pr:512 * pr + 512],
                            start=True, stop=True, tile_position=(64, 0)), inc="pe")
                        num_pe = b.cnt["pe"]
                        if pending_bc[0] is not None:
                            pending_bc[0]()
                        b.wait("scalar", "pe", num_pe)
                        for two, pn in ((0, pnA2), (1, pnB2)):
                            b.op("scalar", lambda s, pn=pn, tl=t_ln2[two]: s.activation(
                                tl[:], pn[64:65, :], AF.Ln, bias=ATTNB_EPS), inc="ac")
                        ln_ac = b.cnt["ac"]
                        for two in (0, 1):
                            b.wait("vector", "ac", ln_ac - 1 + two)
                            b.op("vector", lambda v, tl=t_ln2[two], sh=s_hilo2[two]:
                                 v.tensor_copy(sh[:, 0:512], tl[:]))
                            b.op("vector", lambda v, tl=t_ln2[two], sh=s_hilo2[two]:
                                 v.scalar_tensor_tensor(
                                     sh[:, 512:1024], sh[:, 0:512], -1.0, tl[:],
                                     ALU.mult, ALU.add), inc="ve")
                        hilo_ve = b.cnt["ve"]

                        def _bc(pr=pr, pnA2=pnA2, pnB2=pnB2, hilo_ve=hilo_ve):
                            for two, pn in ((0, pnA2), (1, pnB2)):
                                b.wait("tensor", "ve", hilo_ve - 1 + two)
                                b.op("tensor", lambda t, pn=pn, sh=s_hilo2[two]: t.matmul(
                                    pn[64:128, :], s_ones64b[:], sh[:, 0:512],
                                    start=True, stop=False))
                                b.op("tensor", lambda t, pn=pn, sh=s_hilo2[two]: t.matmul(
                                    pn[64:128, :], s_ones64b[:], sh[:, 512:1024],
                                    start=False, stop=True), inc="pe")
                            bc_pe = b.cnt["pe"]
                            for two, pn in ((0, pnA2), (1, pnB2)):
                                hh = 2 * pr + two
                                b.wait("scalar", "pe", bc_pe - 1 + two)
                                b.wait("scalar", "ve", rec_g[two])
                                b.op("scalar", lambda s, pn=pn, rc=rec64p[two]:
                                     s.activation(rc[:], pn[64:128, :], AF.Exp,
                                                  scale=-1.0), inc="ac")
                                b.wait("vector", "ac")
                                b.wait("vector", "pe", wo_g[0])
                                fo, ro = hh // 2, 64 * (hh % 2)
                                b.op("vector", lambda v, pn=pn, fo=fo, ro=ro,
                                     rc=rec64p[two]: v.tensor_mul(
                                         scrD2[ro:ro + 64, 512 * fo:512 * fo + 512],
                                         pn[0:64, :], rc[:]), inc="ve")
                                rec_g[two] = b.cnt["ve"]
                            pn_g[pr % 2] = (b.cnt["ac"], b.cnt["ve"])

                        pending_bc[0] = _bc
                    pending_bc[0]()
                    div_end_ve = b.cnt["ve"]

                    def _wo(par=par, t0=t0, tc=tc, div_end_ve=div_end_ve):
                        b.wait("tensor", "ve", div_end_ve)
                        b.wait_store("vector", ("coc", par))
                        for eo in range(8):
                            pp = pw2x[eo % 2]
                            for g in wg2.get(eo % 2, []):
                                b.wait("tensor", g[0], g[1])
                            for c in range(8):
                                b.op("tensor", lambda t, pp=pp, c=c, eo=eo: t.matmul(
                                    pp[:], wslot_ap(W4, c, eo), scrD2[:, 512 * c:512 * c + 512],
                                    start=(c == 0), stop=(c == 7)), inc="pe" if c == 7 else None)
                            b.wait("vector", "pe")
                            b.op("vector", lambda v, pp=pp, eo=eo, par=par: v.tensor_add(
                                outc2[par][:, 512 * eo:512 * eo + 512], pp[:],
                                ar2["xc"][par][:, 512 * eo:512 * eo + 512]), inc="ve")
                            wg2[eo % 2] = [("ve", b.cnt["ve"])]
                        wo_g[0] = b.cnt["pe"]
                        b.wait("scalar", "ve")
                        b.store("scalar", d_xT[:, :, t0:t0 + CH].rearrange("f p j -> p f j"),
                                outc2[par].ap().rearrange("p (f j) -> p f j", f=8),
                                key=("coc", par))
                        b.sdone[("xT", tc)] = b.cnt["stx"]
                        ar2["free_g"][par] = b.cnt["ve"]
                        div_ve_g[0] = div_end_ve

                    pending_wo[0] = _wo
                flush_wo()
            b.junction()

        def final_out():
            with contextlib.ExitStack() as ps:
                xc = [mk_sb(ps, "zxc", [128, 1024], F32) for _ in range(2)]
                ytok = [mk_sb(ps, "ytok", [128, 1024], F32) for _ in range(2)]
                p_tr = [psum(ps, "pft", [128, 128], F32) for _ in range(2)]
                fin_pe = {}
                for tt in range(NTT):
                    t0 = tt * 128
                    buf = xc[tt % 2]
                    b.wait_store("sync", ("xT", tt // 4))
                    b.wait("sync", "pe", fin_pe.get(tt % 2, 0))
                    b.dma("sync", buf.ap().rearrange("p (f j) -> p f j", f=8),
                          d_xT[:, :, t0:t0 + 128].rearrange("f p j -> p f j"), "ldx")
                    b.wait("tensor", "ldx")
                    b.wait_store("vector", ("ytok", tt % 2))
                    for f in range(NFT):
                        pb2 = p_tr[f % 2]
                        b.wait("tensor", "ve", b.cnt["ve"] - 1)
                        b.op("tensor", lambda t, pb2=pb2, buf=buf, f=f: t.transpose(
                            pb2[:], buf[:, 128 * f:128 * f + 128], s_identf[:]), inc="pe")
                        b.wait("vector", "pe")
                        b.op("vector", lambda v, pb2=pb2, f=f, tt=tt: v.tensor_copy(
                            ytok[tt % 2][:, 128 * f:128 * f + 128], pb2[:]), inc="ve")
                    fin_pe[tt % 2] = b.cnt["pe"]
                    b.wait("scalar", "ve")
                    b.store("scalar", d_y[t0:t0 + 128, :], ytok[tt % 2][:],
                            key=("ytok", tt % 2))
            b.wait("sync", "stx")

        po_ffn = [0]

        def _prefetch_ffn():
            load_weight(W0, "fr")
            load_weight(W1, "fk")
            load_weight(W2, "fv")
            po_ffn[0] = b.cnt["po"]

        if N_SL >= 1:
            sl_attn_a(prefetch_after_b=_prefetch_ffn if N_SL >= 2 else None)
        if N_SL >= 2:
            cbs = None
            if N_SL >= 3:
                cbs = {1: lambda: load_weight(W3, "bk"),
                       3: lambda: load_weight(W4, "bv", pe_guard=0)}
            sl_ffn("f1", po_ffn[0], chunk_cbs=cbs)
        if N_SL >= 3:
            sl_attn_b()
        if N_SL >= 4:
            sl_ffn("f2", po_ffn[0], to_y=True)
        else:
            final_out()

        with nc.Block() as blk:
            @blk.sync
            def _(e):
                for fn in b.bodies["sync"]:
                    fn(e)

            @blk.tensor
            def _(e):
                for fn in b.bodies["tensor"]:
                    fn(e)

            @blk.vector
            def _(e):
                for fn in b.bodies["vector"]:
                    fn(e)

            @blk.scalar
            def _(e):
                for fn in b.bodies["scalar"]:
                    fn(e)

            @blk.gpsimd
            def _(e):
                for fn in b.bodies["gpsimd"]:
                    fn(e)

    return nc


def kernel(L0_attn_a_x, L0_attn_a_Wq, L0_attn_a_Wk, L0_attn_a_Wv, L0_attn_a_Wo,
           L0_attn_b_Wq, L0_attn_b_Wk, L0_attn_b_Wv, L0_attn_b_Wo,
           L0_ffn_W_r, L0_ffn_W_k, L0_ffn_W_v, L0_ffn_mu_r, L0_ffn_mu_k):
    x = np.asarray(L0_attn_a_x, dtype=np.float32)
    ws = {
        "aq": L0_attn_a_Wq, "ak": L0_attn_a_Wk, "av": L0_attn_a_Wv, "ao": L0_attn_a_Wo,
        "bq": L0_attn_b_Wq, "bk": L0_attn_b_Wk, "bv": L0_attn_b_Wv, "bo": L0_attn_b_Wo,
        "fr": L0_ffn_W_r, "fk": L0_ffn_W_k, "fv": L0_ffn_W_v,
    }
    wT = {("w_" + k): np.ascontiguousarray(np.asarray(v, dtype=np.float32).T)
          for k, v in ws.items()}
    mu_r = np.ascontiguousarray(np.asarray(L0_ffn_mu_r, np.float32).reshape(8, 128).T)
    mu_k = np.ascontiguousarray(np.asarray(L0_ffn_mu_k, np.float32).reshape(8, 128).T)

    nc = build_program()
    in_maps = []
    for b_ in range(8):
        m = {"x": np.ascontiguousarray(x[b_])}
        m.update(wT)
        m["mu_r"] = mu_r
        m["mu_k"] = mu_k
        in_maps.append(m)
    res = run_bass_kernel_spmd(nc, in_maps, core_ids=list(range(8)))
    kernel._last_results = res.results
    out = np.stack([res.results[b_]["y"] for b_ in range(8)], axis=0)
    return out
